# version 01
# baseline (speedup 1.0000x reference)
import contextlib

import numpy as np
import ml_dtypes
import concourse.bass as bass
import concourse.mybir as mybir
from concourse.bass_types import DynSlice
from concourse.bass_utils import run_bass_kernel_spmd

F32 = mybir.dt.float32
BF16 = mybir.dt.bfloat16
AF = mybir.ActivationFunctionType
ALU = mybir.AluOpType

D = 1024
NC8 = 8
DH = 4096
NH = 16
DHD = 64
LN_EPS = 1e-5
RMS_EPS = 1.1920929e-07


class Sem:
    def __init__(self, es, nc, name):
        self.h = es.enter_context(nc.semaphore(name))
        self.n = 0

    def inc(self, instr, k=1):
        instr.then_inc(self.h, k)
        self.n += k
        return instr

    def wait(self, eng, val=None):
        eng.wait_ge(self.h, self.n if val is None else val)


class PsumPool:

    def __init__(self, b, tiles, name):
        self.tiles = tiles
        self.pending = [None] * len(tiles)
        self.count = 0

    def acquire(self, tensor):
        slot = self.count % len(self.tiles)
        self.count += 1
        if self.pending[slot] is not None:
            sem, val = self.pending[slot]
            tensor.wait_ge(sem.h, val)
            self.pending[slot] = None
        return self.tiles[slot], slot

    def expect(self, slot, sem, val):
        self.pending[slot] = (sem, val)


class B:
    pass


def wait_reg(b, eng, sem, iv, const, lo, hi):
    from concourse.expressions import RuntimeValue
    from concourse.bass_primitives_rust import RegisterHandles
    et = eng.engine
    reg = b.wregs.get(et)
    if reg is None:
        reg = b.nc.alloc_register(et, f"wreg{len(b.wregs)}")
        b.wregs[et] = reg
    eng.reg_add(reg, iv, const)
    rv = RuntimeValue(val=RegisterHandles(reg), min_val=lo, max_val=hi)
    eng.wait_ge(sem.h, rv)


@contextlib.contextmanager
def fori(nc, eng, start, end, step=1):
    from concourse.ordered_engine_set import OrderedEngineSet
    engines = OrderedEngineSet([eng.engine])
    loop_id = nc.next_id()
    loop_start = f"ffori_{loop_id}_loop"
    loop_end = f"ffori_{loop_id}_end"
    regs = nc.alloc_registers(f"ffori_{loop_id}_i", engines=engines)
    nc.regs_mov(regs, start)
    nc.br(loop_start, engines=engines)
    iv = None
    with nc.body(loop_start, valid_engines=engines):
        iv = nc.snap(regs, min_val=start, max_val=end - 1)
        yield iv
        nc.regs_alu(regs, regs, step, op=ALU.add)
        nc.br_lt(regs, end, on_true=loop_start, on_false=loop_end,
                 engines=engines)
    nc.switch_bb(loop_end)
    try:
        for h in iv.val.handles:
            nc.free_register(h)
    except Exception:
        pass
    for h in regs:
        nc.free_register(h)


def barrier(b):
    blk, s = b.blk, b.bar
    dma_waits = [(sem, sem.n) for sem in
                 (b.s_st, b.s_hs, b.s_ld, b.s_ldg, b.s_ac, b.s_w,
                  b.s_ac2[0], b.s_ac2[1], b.s_hs2[0], b.s_hs2[1],
                  b.s_wb[0], b.s_wb[1], b.s_w2)]

    def _inc_sync(eng, _w=dma_waits, _s=s):
        for sem, n in _w:
            eng.wait_ge(sem.h, n)
        eng.drain()
        eng.sem_inc(_s.h, 1)
    blk.sync(_inc_sync)

    def _inc(eng, _s=s):
        eng.drain()
        eng.sem_inc(_s.h, 1)
    for name in ("tensor", "vector", "scalar", "gpsimd"):
        getattr(blk, name)(_inc)
    s.n += 5
    t = s.n
    for name in ("sync", "tensor", "vector", "scalar", "gpsimd"):
        getattr(blk, name)(lambda eng, _s=s, _t=t: eng.wait_ge(_s.h, _t))


def load_weight_1024(b, dst, src, n_out_cols, sem=None):
    s = sem if sem is not None else b.s_w
    def go(gpsimd):
        ins = gpsimd.dma_start(
            out=dst[:, 0:NC8 * n_out_cols].rearrange("p (kc o) -> p kc o",
                                                     o=n_out_cols),
            in_=src.rearrange("(kc p) o -> p kc o", p=128))
        s.inc(ins, 16)
    b.blk.gpsimd(go)


def mm_acc(b, tensor, lhsT_sb, rhs, tt, oc, n_k, ps, stride):
    for kc in range(n_k):
        col = kc * stride + oc * 128
        ins = tensor.matmul(ps[:, 0:tt], lhsT_sb[:, col:col + 128], rhs(kc),
                            start=(kc == 0), stop=(kc == n_k - 1))
    return ins


def emit_stats(b, src, n_chunks, tt, rows, wait_sems=()):
    blk = b.blk
    waits = list(wait_sems) + [b.s_v]
    for row, kind in rows:
        if kind == "sumsq":
            wl = [(w, w.n) for w in waits]

            def v_sq(vector, _wl=wl, _tt=tt, _n=n_chunks, _src=src):
                for w, n in _wl:
                    w.wait(vector, n)
                for kc in range(_n):
                    ins = vector.tensor_mul(b.SQ[:, kc * _tt:(kc + 1) * _tt],
                                            _src(kc), _src(kc))
                b.s_sq.inc(ins)
            blk.vector(v_sq)

            def t_sq(tensor, _row=row, _tt=tt, _n=n_chunks, _sq=b.s_sq.n):
                tensor.wait_ge(b.s_sq.h, _sq)
                for kc in range(_n):
                    ins = tensor.matmul(b.PSST[_row:_row + 1, 0:_tt],
                                        b.ones_col[:, 0:1],
                                        b.SQ[:, kc * _tt:(kc + 1) * _tt],
                                        start=(kc == 0), stop=(kc == _n - 1))
                b.s_mm.inc(ins)
            blk.tensor(t_sq)
        else:
            wl = [(w, w.n) for w in waits]

            def t_sum(tensor, _wl=wl, _row=row, _tt=tt, _n=n_chunks, _src=src):
                for w, n in _wl:
                    w.wait(tensor, n)
                for kc in range(_n):
                    ins = tensor.matmul(b.PSST[_row:_row + 1, 0:_tt],
                                        b.ones_col[:, 0:1], _src(kc),
                                        start=(kc == 0), stop=(kc == _n - 1))
                b.s_mm.inc(ins)
            blk.tensor(t_sum)


def bcast(b, tensor, row_ap, ps, tt):
    return tensor.matmul(ps[:, 0:tt], b.ones_row[0:1, :], row_ap,
                         start=True, stop=True)


def emit_ln(b, src, out, tt, n_chunks=NC8, out_bf=None, wait_sems=(),
            add_src=None, eps=LN_EPS):
    blk = b.blk
    emit_stats(b, src, n_chunks, tt, [(0, "sum"), (32, "sumsq")], wait_sems)
    nfeat = float(128 * n_chunks)

    def v_rows(vector, _tt=tt, _mm=b.s_mm.n):
        vector.wait_ge(b.s_mm.h, _mm)
        vector.tensor_scalar_mul(b.row_m[0:1, 0:_tt], b.PSST[0:1, 0:_tt], 1.0 / nfeat)
        vector.tensor_scalar_mul(b.row_a[0:1, 0:_tt], b.PSST[32:33, 0:_tt], 1.0 / nfeat)
        vector.drain()
        vector.tensor_mul(b.row_b[0:1, 0:_tt], b.row_m[0:1, 0:_tt],
                          b.row_m[0:1, 0:_tt])
        vector.drain()
        ins = vector.tensor_sub(b.row_a[0:1, 0:_tt], b.row_a[0:1, 0:_tt],
                                b.row_b[0:1, 0:_tt])
        b.s_v.inc(ins)
    blk.vector(v_rows)

    def a_rstd(scalar, _tt=tt, _v=b.s_v.n):
        scalar.wait_ge(b.s_v.h, _v)
        ins = scalar.activation(b.row_b[0:1, 0:_tt], b.row_a[0:1, 0:_tt],
                                AF.Sqrt, bias=float(eps))
        b.s_a.inc(ins)
    blk.scalar(a_rstd)

    def v_recip(vector, _tt=tt, _a=b.s_a.n):
        vector.wait_ge(b.s_a.h, _a)
        vector.drain()
        ins = vector.reciprocal(b.row_b[0:1, 0:_tt], b.row_b[0:1, 0:_tt])
        b.s_v.inc(ins)
    blk.vector(v_recip)

    def t_bc(tensor, _tt=tt, _a=b.s_v.n):
        tensor.wait_ge(b.s_v.h, _a)
        bcast(b, tensor, b.row_m[0:1, 0:_tt], b.PSB, _tt)
        ins = bcast(b, tensor, b.row_b[0:1, 0:_tt], b.PSB2, _tt)
        b.s_mm.inc(ins)
    blk.tensor(t_bc)

    def v_norm(vector, _tt=tt, _n=n_chunks, _mm=b.s_mm.n):
        vector.wait_ge(b.s_mm.h, _mm)
        for kc in range(_n):
            vector.tensor_sub(out(kc), src(kc), b.PSB[:, 0:_tt])
        vector.drain()
        for kc in range(_n):
            ins = vector.tensor_mul(out(kc), out(kc), b.PSB2[:, 0:_tt])
        if add_src is not None:
            vector.drain()
            for kc in range(_n):
                ins = vector.tensor_add(out(kc), out(kc), add_src(kc))
        if out_bf is not None:
            vector.drain()
            for kc in range(_n):
                ins = vector.tensor_copy(out_bf(kc), out(kc))
        b.s_v.inc(ins)
    blk.vector(v_norm)


def mm_set(b, WT, rhs, TT, n_out, n_k, stride, pre_waits, data_sem,
           incs_per_oc=1):
    plan = []
    base = data_sem.n

    def t_go(tensor, _pw=[(w, w.n) for w in pre_waits]):
        for w, n in _pw:
            w.wait(tensor, n)
        for oc in range(n_out):
            ps, slot = b.pool.acquire(tensor)
            b.pool.expect(slot, data_sem, base + incs_per_oc * oc + 1)
            ins = mm_acc(b, tensor, WT, rhs, TT, oc, n_k, ps, stride)
            b.s_mm.inc(ins)
            plan.append((oc, ps, b.s_mm.n))
    b.blk.tensor(t_go)
    return plan



def build_program(S=4096, CH=1024, TT=512, TTF=512, QTILE=512, phases=99,
                  debug=False, scan_stub=False, scan_no_hist=False,
                  scan_no_mm=False):
    nc = bass.Bass("TRN2", target_bir_lowering=False)
    es = contextlib.ExitStack()
    nc._keepalive_es = es
    b = B()
    b.nc, b.es, b.S = nc, es, S
    b.wregs = {}
    b.scan_no_hist, b.scan_no_mm = scan_no_hist, scan_no_mm
    NT = S // TT
    NAL = QTILE // 128
    dk = "ExternalOutput" if debug else "Internal"

    def din(name, shape, dt=F32):
        return nc.dram_tensor(name, shape, dt, kind="ExternalInput").ap()

    def dint(name, shape, dt=F32):
        return nc.dram_tensor(name, shape, dt, kind=dk).ap()

    x_in = din("x", [D, S])
    gamma = din("gamma", [D])
    lWiT = [din(f"lWiT{i}", [D, D]) for i in range(2)]
    lWifT = [din(f"lWifT{i}", [D, D]) for i in range(2)]
    lWhfT = [din(f"lWhfT{i}", [D, D]) for i in range(2)]
    lbhf = [din(f"lbhf{i}", [D]) for i in range(2)]
    linit = [din(f"linit{i}", [D]) for i in range(2)]
    gWiT = din("gWiT", [D, D]); gWifT = din("gWifT", [D, D])
    gWhfT = din("gWhfT", [D, D]); gbhf = din("gbhf", [D])
    w1T = din("w1T", [D, DH]); w2T = din("w2T", [D, DH])
    h1T = din("h1T", [DH, DH]); h2T = din("h2T", [DH, DH])
    w3T = din("w3T", [DH, D])
    w1s = din("w1s", [DH]); w2s = din("w2s", [DH])
    h1s = din("h1s", [DH]); h2s = din("h2s", [DH])
    w3s = din("w3s", [D])
    WqkvT = din("WqkvT", [D, 3 * D])
    bqkv = din("bqkv", [3 * D])
    WoT = din("WoT", [D, D])
    bo = din("bo", [D])
    l1w = din("l1w", [D, D]); l1a1 = din("l1a1", [D, D]); l1a2 = din("l1a2", [D, D])
    l1bias = din("l1bias", [D])
    amask = din("amask", [NAL * 128, QTILE])

    y_out = nc.dram_tensor("y", [D, S], F32, kind="ExternalOutput").ap()

    xres = dint("xres", [D, S])
    xn_h = dint("xn_h", [D, S])
    xln_h = dint("xln_h", [D, S])
    A_h = [dint(f"A_h{i}", [D, S]) for i in range(2)]
    C_h = [dint(f"C_h{i}", [D, S]) for i in range(2)]
    hs_h = [dint(f"hs_h{i}", [D, S], BF16) for i in range(2)]
    qT_h = dint("qT_h", [D, S], BF16)
    kT_h = dint("kT_h", [D, S], BF16)
    v_h = dint("v_h", [NH, S, DHD])
    zT_h = dint("zT_h", [D, S], BF16)

    sb = nc.alloc_sbuf_tensor
    b.ones_col = sb("ones_col", [128, 1], F32)
    b.ones_row = sb("ones_row", [1, 128], F32)
    b.row_m = sb("row_m", [1, 512], F32)
    b.row_a = sb("row_a", [1, 512], F32)
    b.row_b = sb("row_b", [1, 512], F32)
    b.g1 = sb("g1", [128, 8], F32)
    b.v_lbhf = sb("v_lbhf", [128, 16], F32)
    b.v_linit = sb("v_linit", [128, 16], F32)
    b.v_gbhf = sb("v_gbhf", [128, 8], F32)
    b.v_w1s = sb("v_w1s", [128, 32], F32)
    b.v_w2s = sb("v_w2s", [128, 32], F32)
    b.v_h1s = sb("v_h1s", [128, 32], F32)
    b.v_h2s = sb("v_h2s", [128, 32], F32)
    b.v_w3s = sb("v_w3s", [128, 8], F32)
    b.v_bqkv = sb("v_bqkv", [128, 24], F32)
    b.v_bo = sb("v_bo", [128, 8], F32)
    b.v_l1b = sb("v_l1b", [128, 8], F32)
    b.SQ = sb("SQ", [128, NC8 * 512], F32)

    b.cst = sb("cst", [1, 8], F32)
    b.cvals = {float(LN_EPS): 0, float(RMS_EPS): 1, float(1.0 / D): 2}
    for val, idx in b.cvals.items():
        nc.const_aps.register(F32, val, b.cst[0:1, idx:idx + 1])

    b.PS = [nc.alloc_psum_tensor(f"PS{i}", [128, 512], F32) for i in range(4)]
    b.PSST = nc.alloc_psum_tensor("PSST", [128, 512], F32)
    b.PSB = nc.alloc_psum_tensor("PSB", [128, 512], F32)
    b.PSB2 = nc.alloc_psum_tensor("PSB2", [128, 512], F32)
    b.PSS = nc.alloc_psum_tensor("PSS", [128, 512], F32)

    for nm in ("bar", "s_ld", "s_ldg", "s_st", "s_w", "s_mm", "s_sq", "s_v",
               "s_a", "s_g", "s_ac", "s_hs", "s_smm", "s_su", "s_sg", "s_sh"):
        setattr(b, nm, Sem(es, nc, nm))
    b.s_ac2 = [Sem(es, nc, f"s_ac2_{i}") for i in range(2)]
    b.s_hs2 = [Sem(es, nc, f"s_hs2_{i}") for i in range(2)]
    b.s_wb = [Sem(es, nc, f"s_wb_{i}") for i in range(2)]
    b.s_w2 = Sem(es, nc, "s_w2")

    with nc.Block() as blk:
        b.blk = blk
        b.pool = PsumPool(b, b.PS, "psf")

        def p0_sync(sync):
            pairs = [(b.g1[:, 0:8], gamma)]
            for i in range(2):
                pairs.append((b.v_lbhf[:, 8 * i:8 * i + 8], lbhf[i]))
                pairs.append((b.v_linit[:, 8 * i:8 * i + 8], linit[i]))
            pairs += [(b.v_gbhf[:, 0:8], gbhf), (b.v_w3s[:, 0:8], w3s),
                      (b.v_bo[:, 0:8], bo), (b.v_l1b[:, 0:8], l1bias),
                      (b.v_w1s[:, 0:32], w1s), (b.v_w2s[:, 0:32], w2s),
                      (b.v_h1s[:, 0:32], h1s), (b.v_h2s[:, 0:32], h2s),
                      (b.v_bqkv[:, 0:24], bqkv)]
            with nc.allow_non_contiguous_dma(reason="tiny per-feature vectors"):
                for dstap, src in pairs:
                    ins = sync.dma_start(out=dstap,
                                         in_=src.rearrange("(kc p) -> p kc", p=128))
                    b.s_ld.inc(ins, 16)
            ins = sync.dma_start(out=xres[:, :], in_=x_in[:, :])
            b.s_ld.inc(ins, 16)
        blk.sync(p0_sync)

        def p0_v(vector):
            b.s_ld.wait(vector)
            vector.memset(b.ones_col[:, :], 1.0)
            vector.memset(b.ones_row[:, :], 1.0)
            for val, idx in b.cvals.items():
                vector.memset(b.cst[0:1, idx:idx + 1], val)
            vector.tensor_scalar_add(b.g1[:, 0:8], b.g1[:, 0:8], 1.0)
        blk.vector(p0_v)
        barrier(b)

        if phases >= 1:
            with contextlib.ExitStack() as ph:
                XC = ph.enter_context(nc.sbuf_tensor("XC1", [128, NC8 * TT], F32))
                XLN = ph.enter_context(nc.sbuf_tensor("XLN1", [128, NC8 * TT], F32))
                XN = ph.enter_context(nc.sbuf_tensor("XN1", [128, NC8 * TT], F32))
                XNB = ph.enter_context(nc.sbuf_tensor("XNB1", [128, NC8 * TT], BF16))
                ST1 = ph.enter_context(nc.sbuf_tensor("STa1", [128, NC8 * TT], F32))
                ST2 = ph.enter_context(nc.sbuf_tensor("STb1", [128, NC8 * TT], F32))
                WT1 = ph.enter_context(nc.sbuf_tensor("WT1a", [128, NC8 * D], BF16))
                WT2 = ph.enter_context(nc.sbuf_tensor("WT1b", [128, NC8 * D], BF16))
                load_weight_1024(b, WT1, lWiT[0], D)
                load_weight_1024(b, WT2, lWifT[0], D)
                for it in range(NT):
                    p1_tile(b, x_in, xln_h, xn_h, A_h[0], C_h[0],
                            XC, XLN, XN, XNB, ST1, ST2, WT1, WT2, it * TT, TT)
            barrier(b)

        if phases >= 2:
            if scan_stub:
                def stub2(sync, _st=b.s_st.n):
                    sync.wait_ge(b.s_st.h, _st)
                    ins = sync.dma_start(out=hs_h[0][:, :], in_=A_h[0][:, :])
                    b.s_st.inc(ins, 16)
                b.blk.sync(stub2)
            else:
                emit_scan(b, lWhfT[0], A_h[0], C_h[0], hs_h[0], 0, S, CH)
            barrier(b)

        if phases >= 3:
            with contextlib.ExitStack() as ph:
                XNB = ph.enter_context(nc.sbuf_tensor("XNB3", [128, NC8 * TT], BF16))
                ST1 = ph.enter_context(nc.sbuf_tensor("STa3", [128, NC8 * TT], F32))
                ST2 = ph.enter_context(nc.sbuf_tensor("STb3", [128, NC8 * TT], F32))
                WT1 = ph.enter_context(nc.sbuf_tensor("WT3a", [128, NC8 * D], BF16))
                WT2 = ph.enter_context(nc.sbuf_tensor("WT3b", [128, NC8 * D], BF16))
                load_weight_1024(b, WT1, lWiT[1], D)
                load_weight_1024(b, WT2, lWifT[1], D)
                for it in range(NT):
                    t0 = it * TT

                    def p3_ld(gpsimd, _t0=t0, _vn=b.s_v.n, _an=b.s_a.n):
                        gpsimd.wait_ge(b.s_v.h, _vn)
                        gpsimd.wait_ge(b.s_a.h, _an)
                        for kc in range(NC8):
                            ins = gpsimd.dma_start(
                                out=XNB[:, kc * TT:(kc + 1) * TT],
                                in_=hs_h[0][kc * 128:(kc + 1) * 128, _t0:_t0 + TT])
                            b.s_ldg.inc(ins, 16)
                    blk.gpsimd(p3_ld)
                    mixer_pre(b, XNB, ST1, ST2, WT1, WT2, A_h[1], C_h[1], 1,
                              t0, TT, in_sems=(b.s_ldg, b.s_w))
            barrier(b)

        if phases >= 4:
            emit_scan(b, lWhfT[1], A_h[1], C_h[1], hs_h[1], 1, S, CH)
            barrier(b)

        if phases >= 5:
            with contextlib.ExitStack() as ph:
                XC = ph.enter_context(nc.sbuf_tensor("XC5", [128, NC8 * TT], F32))
                XLN = ph.enter_context(nc.sbuf_tensor("XLN5", [128, NC8 * TT], F32))
                XN = ph.enter_context(nc.sbuf_tensor("XN5", [128, NC8 * TT], F32))
                XNB = ph.enter_context(nc.sbuf_tensor("XNB5", [128, NC8 * TT], BF16))
                H1B = ph.enter_context(nc.sbuf_tensor("H1B5", [128, NC8 * TT], BF16))
                ST1 = ph.enter_context(nc.sbuf_tensor("STa5", [128, NC8 * TT], F32))
                ST2 = ph.enter_context(nc.sbuf_tensor("STb5", [128, NC8 * TT], F32))
                GS = ph.enter_context(nc.sbuf_tensor("GS5", [128, NC8 * TT], F32))
                WT1 = ph.enter_context(nc.sbuf_tensor("WT5a", [128, NC8 * D], BF16))
                WT2 = ph.enter_context(nc.sbuf_tensor("WT5b", [128, NC8 * D], BF16))
                WT3 = ph.enter_context(nc.sbuf_tensor("WT5c", [128, NC8 * D], BF16))
                load_weight_1024(b, WT1, gWiT, D)
                load_weight_1024(b, WT2, gWifT, D)
                load_weight_1024(b, WT3, gWhfT, D)
                for it in range(NT):
                    p5_tile(b, hs_h[1], xn_h, xln_h, x_in, xres, XC, XLN, XN,
                            XNB, H1B, ST1, ST2, GS, WT1, WT2, WT3, it * TT, TT)
            barrier(b)

        if phases >= 6:
            emit_p6(b, xres, w1T, w2T, h1T, h2T, w3T, S, TTF)
            barrier(b)

        if phases >= 7:
            emit_p7(b, xres, WqkvT, WoT, bqkv, qT_h, kT_h, v_h, zT_h, amask,
                    S, TT, QTILE)
            barrier(b)

        if phases >= 8:
            with contextlib.ExitStack() as ph:
                XC = ph.enter_context(nc.sbuf_tensor("XC8", [128, NC8 * TT], F32))
                XLN = ph.enter_context(nc.sbuf_tensor("XLN8", [128, NC8 * TT], F32))
                XNB = ph.enter_context(nc.sbuf_tensor("XNB8", [128, NC8 * TT], BF16))
                STB = ph.enter_context(nc.sbuf_tensor("STB8", [128, NC8 * TT], BF16))
                ST1 = ph.enter_context(nc.sbuf_tensor("STa8", [128, NC8 * TT], F32))
                ST2 = ph.enter_context(nc.sbuf_tensor("STb8", [128, NC8 * TT], F32))
                WT1 = ph.enter_context(nc.sbuf_tensor("WT8a", [128, NC8 * D], BF16))
                WT2 = ph.enter_context(nc.sbuf_tensor("WT8b", [128, NC8 * D], BF16))
                WT3 = ph.enter_context(nc.sbuf_tensor("WT8c", [128, NC8 * D], BF16))
                load_weight_1024(b, WT1, l1w, D)
                load_weight_1024(b, WT2, l1a1, D)
                load_weight_1024(b, WT3, l1a2, D)
                for it in range(NT):
                    p8_tile(b, xres, y_out, XC, XLN, XNB, STB, ST1, ST2,
                            WT1, WT2, WT3, it * TT, TT)
        elif phases < 8:
            def cp_out(sync, _vn=b.s_v.n):
                sync.wait_ge(b.s_v.h, _vn)
                ins = sync.dma_start(out=y_out[:, :], in_=xres[:, :])
                b.s_st.inc(ins, 16)
            blk.sync(cp_out)

        def fin(gpsimd, _n=b.s_st.n):
            gpsimd.wait_ge(b.s_st.h, _n)
        blk.gpsimd(fin)

    return nc



def p1_tile(b, x_in, xln_h, xn_h, A_hb, C_hb, XC, XLN, XN, XNB, ST1, ST2,
            WT1, WT2, t0, TT):
    blk = b.blk

    def ld(sync, _vn=b.s_v.n, _an=b.s_a.n):
        sync.wait_ge(b.s_v.h, _vn)
        sync.wait_ge(b.s_a.h, _an)
        for kc in range(NC8):
            ins = sync.dma_start(out=XC[:, kc * TT:(kc + 1) * TT],
                                 in_=x_in[kc * 128:(kc + 1) * 128, t0:t0 + TT])
            b.s_ld.inc(ins, 16)
    blk.sync(ld)

    src = lambda kc: XC[:, kc * TT:(kc + 1) * TT]
    oln = lambda kc: XLN[:, kc * TT:(kc + 1) * TT]
    emit_ln(b, src, oln, TT, wait_sems=(b.s_ld,))
    emit_stats(b, oln, NC8, TT, [(0, "sumsq")])

    def v_mx(vector, _mm=b.s_mm.n):
        vector.wait_ge(b.s_mm.h, _mm)
        ins = vector.tensor_scalar_max(b.row_a[0:1, 0:TT], b.PSST[0:1, 0:TT],
                                       1e-24)
        b.s_v.inc(ins)
    blk.vector(v_mx)

    def a_rs(scalar, _v=b.s_v.n):
        scalar.wait_ge(b.s_v.h, _v)
        ins = scalar.activation(b.row_b[0:1, 0:TT], b.row_a[0:1, 0:TT],
                                AF.Sqrt, scale=1.0 / D)
        b.s_a.inc(ins)
    blk.scalar(a_rs)

    def v_rec2(vector, _a=b.s_a.n):
        vector.wait_ge(b.s_a.h, _a)
        vector.drain()
        ins = vector.reciprocal(b.row_b[0:1, 0:TT], b.row_b[0:1, 0:TT])
        b.s_v.inc(ins)
    blk.vector(v_rec2)

    def t_bc(tensor, _a=b.s_v.n):
        tensor.wait_ge(b.s_v.h, _a)
        ins = bcast(b, tensor, b.row_b[0:1, 0:TT], b.PSB, TT)
        b.s_mm.inc(ins)
    blk.tensor(t_bc)

    def v_xn(vector, _mm=b.s_mm.n):
        vector.wait_ge(b.s_mm.h, _mm)
        for kc in range(NC8):
            vector.tensor_mul(XN[:, kc * TT:(kc + 1) * TT],
                              XLN[:, kc * TT:(kc + 1) * TT], b.PSB[:, 0:TT])
        vector.drain()
        for kc in range(NC8):
            vector.tensor_scalar_mul(XN[:, kc * TT:(kc + 1) * TT],
                                     XN[:, kc * TT:(kc + 1) * TT],
                                     b.g1[:, kc:kc + 1])
        vector.drain()
        for kc in range(NC8):
            ins = vector.tensor_copy(XNB[:, kc * TT:(kc + 1) * TT],
                                     XN[:, kc * TT:(kc + 1) * TT])
        b.s_v.inc(ins)
    blk.vector(v_xn)

    def st(sync, _v=b.s_v.n):
        sync.wait_ge(b.s_v.h, _v)
        for kc in range(NC8):
            i1 = sync.dma_start(out=xln_h[kc * 128:(kc + 1) * 128, t0:t0 + TT],
                                in_=XLN[:, kc * TT:(kc + 1) * TT])
            b.s_st.inc(i1, 16)
            i2 = sync.dma_start(out=xn_h[kc * 128:(kc + 1) * 128, t0:t0 + TT],
                                in_=XN[:, kc * TT:(kc + 1) * TT])
            b.s_st.inc(i2, 16)
    blk.sync(st)

    mixer_pre(b, XNB, ST1, ST2, WT1, WT2, A_hb, C_hb, 0, t0, TT,
              in_sems=(b.s_v, b.s_w))


def mixer_pre(b, SRCB, ST1, ST2, WT1, WT2, A_hb, C_hb, layer, t0, TT,
              in_sems=()):
    blk = b.blk
    rhs = lambda kc: SRCB[:, kc * TT:(kc + 1) * TT]

    plan = mm_set(b, WT1, rhs, TT, NC8, NC8, NC8 * 128, in_sems, b.s_a,
                  incs_per_oc=2)

    def a_t(scalar):
        for oc, ps, n in plan:
            scalar.wait_ge(b.s_mm.h, n)
            i1 = scalar.activation(ST1[:, oc * TT:(oc + 1) * TT],
                                   ps[:, 0:TT], AF.Tanh)
            b.s_a.inc(i1)
            scalar.drain()
            i2 = scalar.activation(ST1[:, oc * TT:(oc + 1) * TT],
                                   ST1[:, oc * TT:(oc + 1) * TT], AF.Tanh)
            b.s_a.inc(i2)
    blk.scalar(a_t)

    def st_a(sync, _a=b.s_a.n):
        sync.wait_ge(b.s_a.h, _a)
        for oc in range(NC8):
            ins = sync.dma_start(out=A_hb[oc * 128:(oc + 1) * 128, t0:t0 + TT],
                                 in_=ST1[:, oc * TT:(oc + 1) * TT])
            b.s_st.inc(ins, 16)
    blk.sync(st_a)

    plan2 = mm_set(b, WT2, rhs, TT, NC8, NC8, NC8 * 128, (), b.s_v)

    def v_c(vector):
        for oc, ps, n in plan2:
            vector.wait_ge(b.s_mm.h, n)
            ins = vector.tensor_scalar_add(
                ST2[:, oc * TT:(oc + 1) * TT], ps[:, 0:TT],
                b.v_lbhf[:, 8 * layer + oc:8 * layer + oc + 1])
            b.s_v.inc(ins)
    blk.vector(v_c)

    def st_c(sync, _v=b.s_v.n):
        sync.wait_ge(b.s_v.h, _v)
        for oc in range(NC8):
            ins = sync.dma_start(out=C_hb[oc * 128:(oc + 1) * 128, t0:t0 + TT],
                                 in_=ST2[:, oc * TT:(oc + 1) * TT])
            b.s_st.inc(ins, 16)
    blk.sync(st_c)



def emit_scan(b, WhfT, A_hb, C_hb, hs_hb, layer, S, CH):
    blk, nc = b.blk, b.nc
    n_chunks = S // CH

    with contextlib.ExitStack() as ph:
        WSC = ph.enter_context(nc.sbuf_tensor(f"WSC{layer}", [128, 64 * 128], BF16))
        A_sb = [ph.enter_context(nc.sbuf_tensor(f"scA{layer}_{i}", [128, NC8 * CH], F32)) for i in range(2)]
        C_sb = [ph.enter_context(nc.sbuf_tensor(f"scC{layer}_{i}", [128, NC8 * CH], F32)) for i in range(2)]
        H_sb = [ph.enter_context(nc.sbuf_tensor(f"scH{layer}_{i}", [128, NC8 * CH], BF16)) for i in range(2)]
        hcur = ph.enter_context(nc.sbuf_tensor(f"hcur{layer}", [128, 8], F32))
        hbf = ph.enter_context(nc.sbuf_tensor(f"hbf{layer}", [128, 8], BF16))
        ut = ph.enter_context(nc.sbuf_tensor(f"ut{layer}", [128, 8], F32))
        gt = ph.enter_context(nc.sbuf_tensor(f"gt{layer}", [128, 8], F32))
        et = ph.enter_context(nc.sbuf_tensor(f"et{layer}", [128, 8], F32))

        load_weight_1024(b, WSC, WhfT, 128 * NC8)

        base_mm, base_h = b.s_smm.n, b.s_sh.n
        base_u, base_g = b.s_su.n, b.s_sg.n
        base_ac, base_hs = b.s_ac.n, b.s_hs.n

        def vinit(vector, _w=b.s_w.n):
            vector.wait_ge(b.s_w.h, _w)
            vector.tensor_copy(hcur[:, :], b.v_linit[:, 8 * layer:8 * layer + 8])
            vector.drain()
            ins = vector.tensor_copy(hbf[:, :], hcur[:, :])
            ins.then_inc(b.s_sh.h, 1)
        blk.vector(vinit)
        b.s_sh.n += 1

        base_ac2 = [b.s_ac2[i].n for i in range(2)]
        base_hs2 = [b.s_hs2[i].n for i in range(2)]
        st_base = b.s_st.n

        def dma_ac(sync):
            sync.wait_ge(b.s_st.h, st_base)
            for c in range(n_chunks):
                if c >= 2:
                    sync.wait_ge(b.s_sh.h, base_h + 1 + (c - 1) * CH)
                for kc in range(NC8):
                    i1 = sync.dma_start(
                        out=A_sb[c % 2][:, kc * CH:(kc + 1) * CH],
                        in_=A_hb[kc * 128:(kc + 1) * 128, c * CH:(c + 1) * CH])
                    b.s_ac2[c % 2].inc(i1, 16)
                    i2 = sync.dma_start(
                        out=C_sb[c % 2][:, kc * CH:(kc + 1) * CH],
                        in_=C_hb[kc * 128:(kc + 1) * 128, c * CH:(c + 1) * CH])
                    b.s_ac2[c % 2].inc(i2, 16)
        blk.sync(dma_ac)

        def dma_hist(eng):
            if getattr(b, 'scan_no_hist', False):
                return
            for c in range(n_chunks):
                eng.wait_ge(b.s_sh.h, base_h + 1 + (c + 1) * CH)
                for kc in range(NC8):
                    ins = eng.dma_start(
                        out=hs_hb[kc * 128:(kc + 1) * 128, c * CH:(c + 1) * CH],
                        in_=H_sb[c % 2][:, kc * CH:(kc + 1) * CH])
                    b.s_hs2[c % 2].inc(ins, 16)
        blk.scalar(dma_hist) if getattr(b, 'scan_hist_on_scalar', False) else blk.sync(dma_hist)

        for c in range(n_chunks):
            def t_chunk(tensor, _c=c, _w=b.s_w.n):
                if getattr(b, 'scan_no_mm', False):
                    return
                if _c == 0:
                    tensor.wait_ge(b.s_w.h, _w)
                with fori(nc, tensor, 0, CH) as i:
                    wait_reg(b, tensor, b.s_sh, i, base_h + 1 + _c * CH,
                             base_h + 1 + _c * CH, base_h + _c * CH + CH)
                    for dc in range(NC8):
                        for kc in range(NC8):
                            col = (kc * NC8 + dc) * 128
                            ins = tensor.matmul(b.PSS[:, dc:dc + 1],
                                                WSC[:, col:col + 128],
                                                hbf[:, kc:kc + 1],
                                                start=(kc == 0),
                                                stop=(kc == NC8 - 1))
                    ins.then_inc(b.s_smm.h, 1)
            blk.tensor(t_chunk)
            b.s_smm.n += CH

            def a_chunk(scalar, _c=c):
                with fori(nc, scalar, 0, CH) as i:
                    wait_reg(b, scalar, b.s_su, i, base_u + _c * CH + 1,
                             base_u + _c * CH + 1, base_u + _c * CH + CH)
                    ins = scalar.activation(gt[:, :], ut[:, :], AF.Sigmoid)
                    ins.then_inc(b.s_sg.h, 1)
            blk.scalar(a_chunk)
            b.s_sg.n += CH

            def v_chunk(vector, _c=c):
                vector.wait_ge(b.s_ac2[_c % 2].h,
                               base_ac2[_c % 2] + 16 * 2 * NC8 * (_c // 2 + 1))
                if _c >= 2 and not getattr(b, 'scan_no_hist', False):
                    vector.wait_ge(b.s_hs2[_c % 2].h,
                                   base_hs2[_c % 2] + 16 * NC8 * ((_c - 2) // 2 + 1))
                with fori(nc, vector, 0, CH) as i:
                    if not getattr(b, 'scan_no_mm', False):
                        wait_reg(b, vector, b.s_smm, i, base_mm + _c * CH + 1,
                                 base_mm + _c * CH + 1, base_mm + _c * CH + CH)
                    sl = DynSlice(i, 8, CH)
                    ins = vector.tensor_tensor(ut[:, :], b.PSS[:, 0:8],
                                               C_sb[_c % 2][:, sl], ALU.add)
                    ins.then_inc(b.s_su.h, 1)
                    vector.tensor_tensor(et[:, :], A_sb[_c % 2][:, sl],
                                         hcur[:, :], ALU.subtract)
                    wait_reg(b, vector, b.s_sg, i, base_g + _c * CH + 1,
                             base_g + _c * CH + 1, base_g + _c * CH + CH)
                    vector.drain()
                    vector.tensor_mul(et[:, :], gt[:, :], et[:, :])
                    vector.drain()
                    vector.tensor_add(hcur[:, :], hcur[:, :], et[:, :])
                    vector.drain()
                    vector.tensor_copy(hbf[:, :], hcur[:, :])
                    ins = vector.tensor_copy(H_sb[_c % 2][:, sl], hcur[:, :])
                    ins.then_inc(b.s_sh.h, 1)
            blk.vector(v_chunk)
            b.s_su.n += CH
            b.s_sh.n += CH



def p5_tile(b, hs1_h, xn_h, xln_h, x_in, xres, XC, XLN, XN, XNB, H1B,
            ST1, ST2, GS, WT1, WT2, WT3, t0, TT):
    blk = b.blk

    def ld(sync, _vn=b.s_v.n, _an=b.s_a.n):
        sync.wait_ge(b.s_v.h, _vn)
        sync.wait_ge(b.s_a.h, _an)
        for kc in range(NC8):
            for dst, src in ((XC, x_in), (XLN, xln_h), (XN, xn_h)):
                ins = sync.dma_start(out=dst[:, kc * TT:(kc + 1) * TT],
                                     in_=src[kc * 128:(kc + 1) * 128, t0:t0 + TT])
                b.s_ld.inc(ins, 16)
    blk.sync(ld)

    def ldg(gpsimd, _vn=b.s_v.n, _an=b.s_a.n):
        gpsimd.wait_ge(b.s_v.h, _vn)
        gpsimd.wait_ge(b.s_a.h, _an)
        for kc in range(NC8):
            i1 = gpsimd.dma_start(out=H1B[:, kc * TT:(kc + 1) * TT],
                                  in_=hs1_h[kc * 128:(kc + 1) * 128, t0:t0 + TT])
            b.s_ldg.inc(i1, 16)
            i2 = gpsimd.dma_start(out=XNB[:, kc * TT:(kc + 1) * TT],
                                  in_=xn_h[kc * 128:(kc + 1) * 128, t0:t0 + TT])
            b.s_ldg.inc(i2, 16)
    blk.gpsimd(ldg)

    rh = lambda kc: H1B[:, kc * TT:(kc + 1) * TT]
    rx = lambda kc: XNB[:, kc * TT:(kc + 1) * TT]

    plan1 = mm_set(b, WT1, rh, TT, NC8, NC8, NC8 * 128, (b.s_ldg, b.s_w),
                   b.s_a, incs_per_oc=2)

    def a_t(scalar):
        for oc, ps, n in plan1:
            scalar.wait_ge(b.s_mm.h, n)
            i1 = scalar.activation(ST1[:, oc * TT:(oc + 1) * TT],
                                   ps[:, 0:TT], AF.Tanh)
            b.s_a.inc(i1)
            scalar.drain()
            i2 = scalar.activation(ST1[:, oc * TT:(oc + 1) * TT],
                                   ST1[:, oc * TT:(oc + 1) * TT], AF.Tanh)
            b.s_a.inc(i2)
    blk.scalar(a_t)

    plan2 = mm_set(b, WT2, rh, TT, NC8, NC8, NC8 * 128, (), b.s_v)

    def v_c(vector):
        for oc, ps, n in plan2:
            vector.wait_ge(b.s_mm.h, n)
            ins = vector.tensor_copy(ST2[:, oc * TT:(oc + 1) * TT],
                                     ps[:, 0:TT])
            b.s_v.inc(ins)
    blk.vector(v_c)

    plan3 = mm_set(b, WT3, rx, TT, NC8, NC8, NC8 * 128, (), b.s_v)

    def v_u(vector):
        for oc, ps, n in plan3:
            vector.wait_ge(b.s_mm.h, n)
            ins = vector.scalar_tensor_tensor(
                GS[:, oc * TT:(oc + 1) * TT], ps[:, 0:TT],
                b.v_gbhf[:, oc:oc + 1], ST2[:, oc * TT:(oc + 1) * TT],
                ALU.add, ALU.add)
            b.s_v.inc(ins)
    blk.vector(v_u)

    def a_sig(scalar, _v=b.s_v.n):
        scalar.wait_ge(b.s_v.h, _v)
        for oc in range(NC8):
            ins = scalar.activation(GS[:, oc * TT:(oc + 1) * TT],
                                    GS[:, oc * TT:(oc + 1) * TT], AF.Sigmoid)
        b.s_a.inc(ins)
    blk.scalar(a_sig)

    def v_cell(vector, _a=b.s_a.n, _ld=b.s_ld.n):
        vector.wait_ge(b.s_a.h, _a)
        vector.wait_ge(b.s_ld.h, _ld)
        vector.drain()
        for oc in range(NC8):
            sl = slice(oc * TT, (oc + 1) * TT)
            vector.tensor_sub(ST1[:, sl], ST1[:, sl], XN[:, sl])
        vector.drain()
        for oc in range(NC8):
            sl = slice(oc * TT, (oc + 1) * TT)
            vector.tensor_mul(ST1[:, sl], GS[:, sl], ST1[:, sl])
        vector.drain()
        for oc in range(NC8):
            sl = slice(oc * TT, (oc + 1) * TT)
            vector.tensor_add(ST1[:, sl], ST1[:, sl], XN[:, sl])
        vector.drain()
        for oc in range(NC8):
            sl = slice(oc * TT, (oc + 1) * TT)
            ins = vector.tensor_add(ST1[:, sl], ST1[:, sl], XLN[:, sl])
        b.s_v.inc(ins)
    blk.vector(v_cell)

    emit_ln(b, lambda kc: ST1[:, kc * TT:(kc + 1) * TT],
            lambda kc: ST2[:, kc * TT:(kc + 1) * TT], TT,
            add_src=lambda kc: XC[:, kc * TT:(kc + 1) * TT])

    def st(sync, _v=b.s_v.n):
        sync.wait_ge(b.s_v.h, _v)
        for kc in range(NC8):
            ins = sync.dma_start(out=xres[kc * 128:(kc + 1) * 128, t0:t0 + TT],
                                 in_=ST2[:, kc * TT:(kc + 1) * TT])
            b.s_st.inc(ins, 16)
    blk.sync(st)



def emit_p6(b, xres, w1T, w2T, h1T, h2T, w3T, S, TTF):
    blk, nc = b.blk, b.nc
    NB = S // TTF
    NSUB = TTF // 512

    with contextlib.ExitStack() as ph:
        XC = ph.enter_context(nc.sbuf_tensor("XC6", [128, NC8 * 512], F32))
        XLN = ph.enter_context(nc.sbuf_tensor("XLN6", [128, NC8 * 512], F32))
        XNB = ph.enter_context(nc.sbuf_tensor("XNB6", [128, NC8 * TTF], BF16))
        U = ph.enter_context(nc.sbuf_tensor("U6", [128, 32 * TTF], BF16))
        HM = ph.enter_context(nc.sbuf_tensor("HM6", [128, 32 * TTF], BF16))
        ZS = ph.enter_context(nc.sbuf_tensor("ZS6", [128, 512], F32))
        OUT4 = XLN
        OLN = XLN
        WB = [ph.enter_context(nc.sbuf_tensor(f"WB6{i}", [128, 32 * 128], BF16))
              for i in range(2)]
        WB2 = [ph.enter_context(nc.sbuf_tensor(f"WBb6{i}", [128, NC8 * 128], BF16))
               for i in range(2)]

        for ib in range(NB):
            t0 = ib * TTF
            for sub in range(NSUB):
                ts = t0 + sub * 512

                def ld(sync, _ts=ts, _vn=b.s_v.n):
                    sync.wait_ge(b.s_v.h, _vn)
                    for kc in range(NC8):
                        ins = sync.dma_start(
                            out=XC[:, kc * 512:(kc + 1) * 512],
                            in_=xres[kc * 128:(kc + 1) * 128, _ts:_ts + 512])
                        b.s_ld.inc(ins, 16)
                blk.sync(ld)
                emit_ln(b, lambda kc: XC[:, kc * 512:(kc + 1) * 512],
                        lambda kc: XLN[:, kc * 512:(kc + 1) * 512], 512,
                        wait_sems=(b.s_ld,),
                        out_bf=lambda kc, _s=sub: XNB[:, kc * TTF + _s * 512:
                                                      kc * TTF + (_s + 1) * 512])

            guards = {}
            base_wb = [b.s_wb[i].n for i in range(2)]
            for oc in range(32):
                pw = oc % 2

                def wld(gpsimd, _oc=oc, _pw=pw, _mmn=b.s_mm.n):
                    if _oc >= 2:
                        gv, ga = guards[_oc - 2]
                        gpsimd.wait_ge(b.s_v.h, gv)
                        gpsimd.wait_ge(b.s_a.h, ga)
                    else:
                        gpsimd.wait_ge(b.s_mm.h, _mmn)
                    i1 = gpsimd.dma_start(
                        out=WB[_pw][:, 0:NC8 * 128].rearrange(
                            "p (kc o) -> p kc o", o=128),
                        in_=w1T[:, _oc * 128:(_oc + 1) * 128].rearrange(
                            "(kc p) o -> p kc o", p=128))
                    b.s_wb[_pw].inc(i1, 16)
                    i2 = gpsimd.dma_start(
                        out=WB2[_pw][:, 0:NC8 * 128].rearrange(
                            "p (kc o) -> p kc o", o=128),
                        in_=w2T[:, _oc * 128:(_oc + 1) * 128].rearrange(
                            "(kc p) o -> p kc o", p=128))
                    b.s_wb[_pw].inc(i2, 16)
                blk.gpsimd(wld)

                for sub in range(NSUB):
                    rhs = lambda kc, _s=sub: XNB[:, kc * TTF + _s * 512:
                                                 kc * TTF + (_s + 1) * 512]
                    plan = []

                    def t_z(tensor, _oc=oc, _pw=pw, _s=sub, _rhs=rhs,
                            _wn=base_wb[pw] + 32 * (oc // 2 + 1),
                            _vn=b.s_v.n, _plan=plan,
                            _a0=b.s_a.n, _v0=b.s_v.n):
                        tensor.wait_ge(b.s_wb[_pw].h, _wn)
                        if _oc == 0 and _s == 0:
                            tensor.wait_ge(b.s_v.h, _vn)
                        ps1, sl1 = b.pool.acquire(tensor)
                        b.pool.expect(sl1, b.s_a, _a0 + 1)
                        ins = mm_acc(b, tensor, WB[_pw], _rhs, 512, 0, NC8,
                                     ps1, 128)
                        b.s_mm.inc(ins)
                        _plan.append((ps1, b.s_mm.n))
                        ps2, sl2 = b.pool.acquire(tensor)
                        b.pool.expect(sl2, b.s_v, _v0 + 1)
                        ins = mm_acc(b, tensor, WB2[_pw], _rhs, 512, 0, NC8,
                                     ps2, 128)
                        b.s_mm.inc(ins)
                        _plan.append((ps2, b.s_mm.n))
                    blk.tensor(t_z)

                    def a_sil(scalar, _oc=oc, _s=sub, _plan=plan,
                              _v0=b.s_v.n):
                        scalar.wait_ge(b.s_v.h, _v0)
                        ps1, n1 = _plan[0]
                        scalar.wait_ge(b.s_mm.h, n1)
                        ins = scalar.activation(
                            ZS[:, 0:512], ps1[:, 0:512],
                            AF.Silu, scale=b.v_w1s[:, _oc:_oc + 1])
                        b.s_a.inc(ins)
                    blk.scalar(a_sil)

                    def v_u(vector, _oc=oc, _s=sub, _plan=plan, _an=b.s_a.n):
                        ps2, n2 = _plan[1]
                        vector.wait_ge(b.s_mm.h, n2)
                        vector.wait_ge(b.s_a.h, _an)
                        ins = vector.scalar_tensor_tensor(
                            U[:, _oc * TTF + _s * 512:_oc * TTF + (_s + 1) * 512],
                            ps2[:, 0:512], b.v_w2s[:, _oc:_oc + 1],
                            ZS[:, 0:512],
                            ALU.mult, ALU.mult)
                        b.s_v.inc(ins)
                    blk.vector(v_u)
                guards[oc] = (b.s_v.n, b.s_a.n)

            _p6_mid(b, U, HM, h1T, b.v_h1s, WB, TTF, NSUB)
            _p6_mid(b, HM, U, h2T, b.v_h2s, WB, TTF, NSUB)

            for sub in range(NSUB):
                ts = t0 + sub * 512
                for grp in range(2):
                    g_hist = {}
                    base_wb = [b.s_wb[i].n for i in range(2)]
                    for oc3 in range(32):
                        pw = oc3 % 2

                        def w3ld(gpsimd, _oc3=oc3, _grp=grp, _pw=pw,
                                 _mmn=b.s_mm.n):
                            if _oc3 >= 2:
                                gpsimd.wait_ge(b.s_g.h, g_hist[_oc3 - 2])
                            else:
                                gpsimd.wait_ge(b.s_mm.h, _mmn)
                            ins = gpsimd.dma_start(
                                out=WB[_pw][:, 0:512],
                                in_=w3T[_oc3 * 128:(_oc3 + 1) * 128,
                                        _grp * 512:(_grp + 1) * 512])
                            b.s_wb[_pw].inc(ins, 16)
                        blk.gpsimd(w3ld)

                        def t_w3(tensor, _oc3=oc3, _pw=pw, _s=sub,
                                 _wn=base_wb[oc3 % 2] + 16 * (oc3 // 2 + 1),
                                 _vn=b.s_v.n, _an=b.s_a.n):
                            tensor.wait_ge(b.s_wb[_pw].h, _wn)
                            if _oc3 == 0:
                                tensor.wait_ge(b.s_v.h, _vn)
                                tensor.wait_ge(b.s_a.h, _an)
                            for oc4 in range(4):
                                ins = tensor.matmul(
                                    b.PS[oc4][:, 0:512],
                                    WB[_pw][:, oc4 * 128:(oc4 + 1) * 128],
                                    U[:, _oc3 * TTF + _s * 512:
                                      _oc3 * TTF + (_s + 1) * 512],
                                    start=(_oc3 == 0), stop=(_oc3 == 31))
                            if _oc3 == 31:
                                b.s_mm.inc(ins)
                            else:
                                b.s_g.inc(ins)
                        blk.tensor(t_w3)
                        g_hist[oc3] = b.s_g.n

                    def v_ev(vector, _grp=grp, _n=b.s_mm.n):
                        vector.wait_ge(b.s_mm.h, _n)
                        for oc4 in range(4):
                            oc = _grp * 4 + oc4
                            ins = vector.tensor_scalar_mul(
                                OUT4[:, oc * 512:(oc + 1) * 512],
                                b.PS[oc4][:, 0:512], b.v_w3s[:, oc:oc + 1])
                        b.s_v.inc(ins)
                    blk.vector(v_ev)

                def ld2(sync, _ts=ts, _vn=b.s_v.n):
                    sync.wait_ge(b.s_v.h, _vn)
                    for kc in range(NC8):
                        ins = sync.dma_start(
                            out=XC[:, kc * 512:(kc + 1) * 512],
                            in_=xres[kc * 128:(kc + 1) * 128, _ts:_ts + 512])
                        b.s_ld.inc(ins, 16)
                blk.sync(ld2)
                emit_ln(b, lambda kc: OUT4[:, kc * 512:(kc + 1) * 512],
                        lambda kc: OLN[:, kc * 512:(kc + 1) * 512], 512,
                        wait_sems=(b.s_ld,),
                        add_src=lambda kc: XC[:, kc * 512:(kc + 1) * 512])

                def st2(sync, _ts=ts, _v=b.s_v.n):
                    sync.wait_ge(b.s_v.h, _v)
                    for kc in range(NC8):
                        ins = sync.dma_start(
                            out=xres[kc * 128:(kc + 1) * 128, _ts:_ts + 512],
                            in_=OLN[:, kc * 512:(kc + 1) * 512])
                        b.s_st.inc(ins, 16)
                blk.sync(st2)


def _p6_mid(b, SRC, DST, WT_h, scale_vec, WB, TTF, NSUB):
    blk = b.blk
    guards = {}
    base_wb = [b.s_wb[i].n for i in range(2)]
    for oc in range(32):
        pw = oc % 2

        def wld(gpsimd, _oc=oc, _pw=pw, _mmn=b.s_mm.n):
            if _oc >= 2:
                gpsimd.wait_ge(b.s_a.h, guards[_oc - 2])
            else:
                gpsimd.wait_ge(b.s_mm.h, _mmn)
            ins = gpsimd.dma_start(
                out=WB[_pw][:, :].rearrange("p (kc o) -> p kc o", o=128),
                in_=WT_h[:, _oc * 128:(_oc + 1) * 128].rearrange(
                    "(kc p) o -> p kc o", p=128))
            b.s_wb[_pw].inc(ins, 16)
        blk.gpsimd(wld)

        for sub in range(NSUB):
            plan = []

            def t_mm(tensor, _oc=oc, _pw=pw, _s=sub,
                     _wn=base_wb[pw] + 16 * (oc // 2 + 1),
                     _vn=b.s_v.n, _an=b.s_a.n, _plan=plan):
                tensor.wait_ge(b.s_wb[_pw].h, _wn)
                if _oc == 0 and _s == 0:
                    tensor.wait_ge(b.s_v.h, _vn)
                    tensor.wait_ge(b.s_a.h, _an)
                ps, sl = b.pool.acquire(tensor)
                b.pool.expect(sl, b.s_a, _an + 1)
                for kc in range(32):
                    ins = tensor.matmul(
                        ps[:, 0:512], WB[_pw][:, kc * 128:(kc + 1) * 128],
                        SRC[:, kc * TTF + _s * 512:kc * TTF + (_s + 1) * 512],
                        start=(kc == 0), stop=(kc == 31))
                b.s_mm.inc(ins)
                _plan.append((ps, b.s_mm.n))
            blk.tensor(t_mm)

            def a_g(scalar, _oc=oc, _s=sub, _plan=plan):
                ps, n = _plan[0]
                scalar.wait_ge(b.s_mm.h, n)
                ins = scalar.activation(
                    DST[:, _oc * TTF + _s * 512:_oc * TTF + (_s + 1) * 512],
                    ps[:, 0:512], AF.Gelu, scale=scale_vec[:, _oc:_oc + 1])
                b.s_a.inc(ins)
            blk.scalar(a_g)
        guards[oc] = b.s_a.n



def emit_p7(b, xres, WqkvT, WoT, bqkv, qT_h, kT_h, v_h, zT_h, amask,
            S, TT, QTILE):
    blk, nc = b.blk, b.nc
    NT = S // TT
    NKT = S // 128
    NQT = S // QTILE
    KPQ = QTILE // 128

    with contextlib.ExitStack() as ph:
        XC = ph.enter_context(nc.sbuf_tensor("XC7", [128, NC8 * 512], F32))
        b.row_q = ph.enter_context(nc.sbuf_tensor("row_q", [1, D], F32))
        XLN = ph.enter_context(nc.sbuf_tensor("XLN7", [128, NC8 * 512], F32))
        pha = ph.enter_context(contextlib.ExitStack())
        XNB = pha.enter_context(nc.sbuf_tensor("XNB7", [128, NC8 * TT], BF16))
        STB = pha.enter_context(nc.sbuf_tensor("STB7", [128, NC8 * TT], BF16))
        VS = pha.enter_context(nc.sbuf_tensor("VS7", [128, D], F32))
        WQ = pha.enter_context(nc.sbuf_tensor("WQ7", [128, NC8 * 3 * D], BF16))
        BIASV = pha.enter_context(nc.sbuf_tensor("BIASV7", [128, D], F32))
        MASK = pha.enter_context(nc.sbuf_tensor("MASK7", [128, KPQ * QTILE], F32))

        load_weight_1024(b, WQ, WqkvT, 3 * D)

        def mload(sync):
            ins0 = sync.dma_start(out=b.row_q[0:1, :], in_=bqkv[2 * D:3 * D])
            b.s_ld.inc(ins0, 16)
            ins = sync.dma_start(
                out=MASK[:, :].rearrange("p (a q) -> p a q", q=QTILE),
                in_=amask.rearrange("(a p) q -> p a q", p=128))
            b.s_ld.inc(ins, 16)
        blk.sync(mload)

        def t_bv(tensor, _ld=b.s_ld.n):
            tensor.wait_ge(b.s_ld.h, _ld)
            bcast(b, tensor, b.row_q[0:1, 0:512], b.PSB, 512)
            ins = bcast(b, tensor, b.row_q[0:1, 512:1024], b.PSB2, 512)
            b.s_mm.inc(ins)
        blk.tensor(t_bv)

        def v_bv(vector, _mm=b.s_mm.n):
            vector.wait_ge(b.s_mm.h, _mm)
            vector.tensor_copy(BIASV[:, 0:512], b.PSB[:, 0:512])
            ins = vector.tensor_copy(BIASV[:, 512:1024], b.PSB2[:, 0:512])
            b.s_v.inc(ins)
        blk.vector(v_bv)

        for it in range(NT):
            t0 = it * TT
            for sub in range(TT // 512):
                ts = t0 + sub * 512

                def ld(sync, _ts=ts, _vn=b.s_v.n):
                    sync.wait_ge(b.s_v.h, _vn)
                    for kc in range(NC8):
                        ins = sync.dma_start(
                            out=XC[:, kc * 512:(kc + 1) * 512],
                            in_=xres[kc * 128:(kc + 1) * 128, _ts:_ts + 512])
                        b.s_ld.inc(ins, 16)
                blk.sync(ld)

                emit_ln(b, lambda kc: XC[:, kc * 512:(kc + 1) * 512],
                        lambda kc: XLN[:, kc * 512:(kc + 1) * 512], 512,
                        wait_sems=(b.s_ld,))
                emit_stats(b, lambda kc: XLN[:, kc * 512:(kc + 1) * 512],
                           NC8, 512, [(0, "sumsq")])

                def a_rs(scalar, _mm=b.s_mm.n):
                    scalar.wait_ge(b.s_mm.h, _mm)
                    ins = scalar.activation(b.row_b[0:1, 0:512],
                                            b.PSST[0:1, 0:512], AF.Sqrt,
                                            bias=RMS_EPS, scale=1.0 / D)
                    b.s_a.inc(ins)
                blk.scalar(a_rs)

                def v_rec3(vector, _a=b.s_a.n):
                    vector.wait_ge(b.s_a.h, _a)
                    vector.drain()
                    ins = vector.reciprocal(b.row_b[0:1, 0:512],
                                            b.row_b[0:1, 0:512])
                    b.s_v.inc(ins)
                blk.vector(v_rec3)

                def t_bc(tensor, _a=b.s_v.n):
                    tensor.wait_ge(b.s_v.h, _a)
                    ins = bcast(b, tensor, b.row_b[0:1, 0:512], b.PSB, 512)
                    b.s_mm.inc(ins)
                blk.tensor(t_bc)

                def v_n(vector, _s=sub, _mm=b.s_mm.n):
                    vector.wait_ge(b.s_mm.h, _mm)
                    for kc in range(NC8):
                        ins = vector.tensor_mul(
                            XNB[:, kc * TT + _s * 512:kc * TT + (_s + 1) * 512],
                            XLN[:, kc * 512:(kc + 1) * 512], b.PSB[:, 0:512])
                    b.s_v.inc(ins)
                blk.vector(v_n)

            for part, dst_h in ((0, qT_h), (1, kT_h)):
                for sub in range(TT // 512):
                    rhs = lambda kc, _s=sub: XNB[:, kc * TT + _s * 512:
                                                 kc * TT + (_s + 1) * 512]
                    plan = []

                    def t_qk(tensor, _part=part, _s=sub, _rhs=rhs,
                             _wn=b.s_w.n, _vn=b.s_v.n, _plan=plan,
                             _v0=b.s_v.n):
                        tensor.wait_ge(b.s_w.h, _wn)
                        tensor.wait_ge(b.s_v.h, _vn)
                        for oc in range(NC8):
                            ps, sl = b.pool.acquire(tensor)
                            b.pool.expect(sl, b.s_v, _v0 + oc + 1)
                            ins = mm_acc(b, tensor, WQ, _rhs, 512,
                                         _part * 8 + oc, NC8, ps, 3 * D)
                            b.s_mm.inc(ins)
                            _plan.append((oc, ps, b.s_mm.n))
                    blk.tensor(t_qk)

                    def v_qk(vector, _part=part, _s=sub, _plan=plan,
                             _stn=b.s_st.n):
                        vector.wait_ge(b.s_st.h, _stn)
                        for oc, ps, n in _plan:
                            vector.wait_ge(b.s_mm.h, n)
                            ins = vector.tensor_scalar_add(
                                STB[:, oc * TT + _s * 512:oc * TT + (_s + 1) * 512],
                                ps[:, 0:512],
                                b.v_bqkv[:, _part * 8 + oc:_part * 8 + oc + 1])
                            b.s_v.inc(ins)
                    blk.vector(v_qk)

                def stqk(sync, _dst=dst_h, _t0=t0, _v=b.s_v.n):
                    sync.wait_ge(b.s_v.h, _v)
                    for oc in range(NC8):
                        ins = sync.dma_start(
                            out=_dst[oc * 128:(oc + 1) * 128, _t0:_t0 + TT],
                            in_=STB[:, oc * TT:(oc + 1) * TT])
                        b.s_st.inc(ins, 16)
                blk.sync(stqk)

            for tsub in range(TT // 128):
                tks = t0 + tsub * 128
                lsub = tsub * 128 // 512
                loff = (tsub * 128) % 512
                plan = []

                def t_v(tensor, _lsub=lsub, _loff=loff, _plan=plan,
                        _wn=b.s_w.n, _vn=b.s_v.n, _v0=b.s_v.n):
                    tensor.wait_ge(b.s_w.h, _wn)
                    tensor.wait_ge(b.s_v.h, _vn)
                    for half in range(2):
                        ps, sl = b.pool.acquire(tensor)
                        b.pool.expect(sl, b.s_v, _v0 + half + 1)
                        for kc in range(NC8):
                            lhs = XNB[:, kc * TT + _lsub * 512 + _loff:
                                      kc * TT + _lsub * 512 + _loff + 128]
                            rhscol = kc * 3 * D + 2 * D + half * 512
                            ins = tensor.matmul(
                                ps[:, 0:512], lhs,
                                WQ[:, rhscol:rhscol + 512],
                                start=(kc == 0), stop=(kc == NC8 - 1))
                        b.s_mm.inc(ins)
                        _plan.append((half, ps, b.s_mm.n))
                blk.tensor(t_v)

                def v_v(vector, _plan=plan, _stn=b.s_st.n):
                    vector.wait_ge(b.s_st.h, _stn)
                    for half, ps, n in _plan:
                        vector.wait_ge(b.s_mm.h, n)
                        ins = vector.tensor_add(
                            VS[:, half * 512:(half + 1) * 512], ps[:, 0:512],
                            BIASV[:, half * 512:(half + 1) * 512])
                        b.s_v.inc(ins)
                blk.vector(v_v)

                def st_v(sync, _tks=tks, _v=b.s_v.n):
                    sync.wait_ge(b.s_v.h, _v)
                    ins = sync.dma_start(
                        out=v_h[:, _tks:_tks + 128, :].rearrange("h t dh -> t h dh"),
                        in_=VS[:, :].rearrange("p (h dh) -> p h dh", dh=DHD))
                    b.s_st.inc(ins, 16)
                blk.sync(st_v)

        QT = pha.enter_context(nc.sbuf_tensor("QT7", [64, S], BF16))
        KT = pha.enter_context(nc.sbuf_tensor("KT7", [64, S], BF16))
        VT = pha.enter_context(nc.sbuf_tensor("VT7", [128, NKT * DHD], F32))
        VNB = pha.enter_context(nc.sbuf_tensor("VNB7", [128, NKT * DHD], BF16))
        MC = pha.enter_context(nc.sbuf_tensor("MC7", [128, NKT], F32))
        ES = pha.enter_context(nc.sbuf_tensor("ES7", [128, QTILE], BF16))
        SD = pha.enter_context(nc.sbuf_tensor("SD7", [128, QTILE], F32))
        CL = pha.enter_context(nc.sbuf_tensor("CL7", [128, 8], F32))
        ZB = pha.enter_context(nc.sbuf_tensor("ZB7", [64, S], BF16))

        for h in range(NH):
            def ldh(sync, _h=h, _vn=b.s_v.n, _an=b.s_a.n, _mmn=b.s_mm.n,
                    _stn=b.s_st.n):
                sync.wait_ge(b.s_v.h, _vn)
                sync.wait_ge(b.s_a.h, _an)
                sync.wait_ge(b.s_mm.h, _mmn)
                sync.wait_ge(b.s_st.h, _stn)
                i1 = sync.dma_start(out=QT[:, :],
                                    in_=qT_h[_h * 64:(_h + 1) * 64, :])
                b.s_ld.inc(i1, 16)
                i2 = sync.dma_start(out=KT[:, :],
                                    in_=kT_h[_h * 64:(_h + 1) * 64, :])
                b.s_ld.inc(i2, 16)
                i3 = sync.dma_start(
                    out=VT[:, :].rearrange("p (kt dh) -> p kt dh", dh=DHD),
                    in_=v_h[_h, :, :].rearrange("(kt p) dh -> p kt dh", p=128))
                b.s_ld.inc(i3, 16)
            blk.sync(ldh)

            for kt in range(NKT):
                qt0 = kt // KPQ
                al = kt % KPQ

                def v_init(vector, _ld=b.s_ld.n):
                    vector.wait_ge(b.s_ld.h, _ld)
                    vector.drain()
                    vector.memset(CL[:, 0:1], -1e30)
                    vector.memset(CL[:, 1:2], 0.0)
                    vector.drain()
                blk.vector(v_init)

                for qt in range(qt0, NQT):
                    diag = (qt == qt0)
                    plan = []

                    def t_s(tensor, _kt=kt, _qt=qt, _plan=plan, _diag=diag,
                            _first=(kt == 0 and qt == qt0), _ldn=b.s_ld.n,
                            _v0=b.s_v.n, _a0=b.s_a.n):
                        if _first:
                            tensor.wait_ge(b.s_ld.h, _ldn)
                        ps, sl = b.pool.acquire(tensor)
                        if _diag:
                            b.pool.expect(sl, b.s_v, _v0 + 1)
                        else:
                            b.pool.expect(sl, b.s_a, _a0 + 2)
                        ins = tensor.matmul(
                            ps[:, 0:QTILE],
                            KT[:, _kt * 128:(_kt + 1) * 128],
                            QT[:, _qt * QTILE:(_qt + 1) * QTILE],
                            start=True, stop=True)
                        b.s_mm.inc(ins)
                        _plan.append((ps, b.s_mm.n))
                    blk.tensor(t_s)

                    def v_flash(vector, _diag=diag, _al=al, _plan=plan):
                        ps, n = _plan[0]
                        vector.wait_ge(b.s_mm.h, n)
                        vector.drain()
                        if _diag:
                            ins = vector.tensor_add(
                                SD[:, :], ps[:, 0:QTILE],
                                MASK[:, _al * QTILE:(_al + 1) * QTILE])
                            b.s_v.inc(ins)
                            src = SD[:, :]
                            vector.drain()
                        else:
                            src = ps[:, 0:QTILE]
                        vector.reduce_max(CL[:, 2:3], src, axis=mybir.AxisListType.X)
                        vector.drain()
                        vector.tensor_max(CL[:, 3:4], CL[:, 0:1], CL[:, 2:3])
                        vector.drain()
                        vector.tensor_sub(CL[:, 4:5], CL[:, 0:1], CL[:, 3:4])
                        ins = vector.tensor_scalar_mul(CL[:, 7:8], CL[:, 3:4], -1.0)
                        b.s_v.inc(ins)
                    blk.vector(v_flash)

                    def a_flash(scalar, _diag=diag, _plan=plan, _v=b.s_v.n):
                        scalar.wait_ge(b.s_v.h, _v)
                        ins = scalar.activation(CL[:, 5:6], CL[:, 4:5], AF.Exp)
                        b.s_a.inc(ins)
                        ps, n = _plan[0]
                        src = SD[:, :] if _diag else ps[:, 0:QTILE]
                        ins = scalar.activation(ES[:, :], src, AF.Exp,
                                                bias=CL[:, 7:8])
                        b.s_a.inc(ins)
                    blk.scalar(a_flash)

                    def v_merge(vector, _a=b.s_a.n):
                        vector.wait_ge(b.s_a.h, _a)
                        vector.drain()
                        vector.reduce_sum(CL[:, 6:7], ES[:, :], axis=mybir.AxisListType.X)
                        vector.tensor_mul(CL[:, 1:2], CL[:, 1:2], CL[:, 5:6])
                        vector.drain()
                        vector.tensor_add(CL[:, 1:2], CL[:, 1:2], CL[:, 6:7])
                        ins = vector.tensor_copy(CL[:, 0:1], CL[:, 3:4])
                        b.s_v.inc(ins)
                    blk.vector(v_merge)

                def v_fin(vector, _kt=kt, _v=b.s_v.n):
                    vector.wait_ge(b.s_v.h, _v)
                    vector.drain()
                    vector.reciprocal(CL[:, 6:7], CL[:, 1:2])
                    vector.tensor_scalar_mul(MC[:, _kt:_kt + 1], CL[:, 0:1], -1.0)
                    vector.drain()
                    ins = vector.tensor_scalar_mul(
                        VNB[:, _kt * DHD:(_kt + 1) * DHD],
                        VT[:, _kt * DHD:(_kt + 1) * DHD], CL[:, 6:7])
                    b.s_v.inc(ins)
                blk.vector(v_fin)

            for qt in range(NQT):
                nk = (qt + 1) * KPQ
                for kt in range(nk):
                    diag = (kt // KPQ == qt)
                    al = kt % KPQ
                    plan = []

                    def t_s2(tensor, _kt=kt, _qt=qt, _plan=plan, _diag=diag,
                             _v0=b.s_v.n, _a0=b.s_a.n):
                        ps, sl = b.pool.acquire(tensor)
                        if _diag:
                            b.pool.expect(sl, b.s_v, _v0 + 1)
                        else:
                            b.pool.expect(sl, b.s_a, _a0 + 1)
                        ins = tensor.matmul(
                            ps[:, 0:QTILE],
                            KT[:, _kt * 128:(_kt + 1) * 128],
                            QT[:, _qt * QTILE:(_qt + 1) * QTILE],
                            start=True, stop=True)
                        b.s_mm.inc(ins)
                        _plan.append((ps, b.s_mm.n))
                    blk.tensor(t_s2)

                    if diag:
                        def v_m(vector, _al=al, _plan=plan):
                            ps, n = _plan[0]
                            vector.wait_ge(b.s_mm.h, n)
                            ins = vector.tensor_add(
                                SD[:, :], ps[:, 0:QTILE],
                                MASK[:, _al * QTILE:(_al + 1) * QTILE])
                            b.s_v.inc(ins)
                        blk.vector(v_m)

                    def a_e2(scalar, _kt=kt, _diag=diag, _plan=plan,
                             _v=b.s_v.n, _g=b.s_g.n):
                        scalar.wait_ge(b.s_g.h, _g)
                        ps, n = _plan[0]
                        if _diag:
                            scalar.wait_ge(b.s_v.h, _v)
                            src = SD[:, :]
                        else:
                            scalar.wait_ge(b.s_mm.h, n)
                            src = ps[:, 0:QTILE]
                        ins = scalar.activation(ES[:, :], src, AF.Exp,
                                                bias=MC[:, _kt:_kt + 1])
                        b.s_a.inc(ins)
                    blk.scalar(a_e2)

                    def t_av(tensor, _kt=kt, _first=(kt == 0),
                             _last=(kt == nk - 1), _a=b.s_a.n, _vn=b.s_v.n):
                        tensor.wait_ge(b.s_a.h, _a)
                        if _first:
                            tensor.wait_ge(b.s_v.h, _vn)
                        ins = tensor.matmul(
                            b.PSS[0:64, 0:QTILE],
                            VNB[:, _kt * DHD:(_kt + 1) * DHD],
                            ES[:, :], start=_first, stop=_last)
                        b.s_g.inc(ins)
                    blk.tensor(t_av)

                def v_z(vector, _qt=qt, _mm=b.s_g.n, _stn=b.s_st.n):
                    vector.wait_ge(b.s_g.h, _mm)
                    vector.wait_ge(b.s_st.h, _stn)
                    ins = vector.tensor_copy(
                        ZB[:, _qt * QTILE:(_qt + 1) * QTILE],
                        b.PSS[0:64, 0:QTILE])
                    b.s_v.inc(ins)
                blk.vector(v_z)

            def st_z(sync, _h=h, _v=b.s_v.n):
                sync.wait_ge(b.s_v.h, _v)
                ins = sync.dma_start(out=zT_h[_h * 64:(_h + 1) * 64, :],
                                     in_=ZB[:, :])
                b.s_st.inc(ins, 16)
            blk.sync(st_z)

        pha.close()

        def gate_wob(gpsimd, _vn=b.s_v.n, _an=b.s_a.n, _gn=b.s_g.n):
            gpsimd.wait_ge(b.s_v.h, _vn)
            gpsimd.wait_ge(b.s_a.h, _an)
            gpsimd.wait_ge(b.s_g.h, _gn)
        blk.gpsimd(gate_wob)
        WOB = ph.enter_context(nc.sbuf_tensor("WOB7", [128, NC8 * D], BF16))
        ZNB = ph.enter_context(nc.sbuf_tensor("ZNB7", [128, NC8 * 512], BF16))
        OP = ph.enter_context(nc.sbuf_tensor("OP7", [128, NC8 * 512], F32))
        OL = ph.enter_context(nc.sbuf_tensor("OL7", [128, NC8 * 512], F32))
        load_weight_1024(b, WOB, WoT, D, sem=b.s_w2)

        for it in range(S // 512):
            t0 = it * 512

            def ldz(gpsimd, _t0=t0, _vn=b.s_v.n, _stn=b.s_st.n):
                gpsimd.wait_ge(b.s_v.h, _vn)
                gpsimd.wait_ge(b.s_st.h, _stn)
                for kc in range(NC8):
                    ins = gpsimd.dma_start(
                        out=ZNB[:, kc * 512:(kc + 1) * 512],
                        in_=zT_h[kc * 128:(kc + 1) * 128, _t0:_t0 + 512])
                    b.s_ldg.inc(ins, 16)
            blk.gpsimd(ldz)

            def ldx(sync, _t0=t0, _vn=b.s_v.n):
                sync.wait_ge(b.s_v.h, _vn)
                for kc in range(NC8):
                    ins = sync.dma_start(
                        out=XC[:, kc * 512:(kc + 1) * 512],
                        in_=xres[kc * 128:(kc + 1) * 128, _t0:_t0 + 512])
                    b.s_ld.inc(ins, 16)
            blk.sync(ldx)

            rhs = lambda kc: ZNB[:, kc * 512:(kc + 1) * 512]
            plan = mm_set(b, WOB, rhs, 512, NC8, NC8, NC8 * 128,
                          (b.s_ldg, b.s_w2), b.s_v)

            def v_o(vector, _plan=plan):
                for oc, ps, n in _plan:
                    vector.wait_ge(b.s_mm.h, n)
                    ins = vector.tensor_scalar_add(
                        OP[:, oc * 512:(oc + 1) * 512], ps[:, 0:512],
                        b.v_bo[:, oc:oc + 1])
                    b.s_v.inc(ins)
            blk.vector(v_o)

            emit_ln(b, lambda kc: OP[:, kc * 512:(kc + 1) * 512],
                    lambda kc: OL[:, kc * 512:(kc + 1) * 512], 512,
                    wait_sems=(b.s_ld,),
                    add_src=lambda kc: XC[:, kc * 512:(kc + 1) * 512])

            def sto(sync, _t0=t0, _v=b.s_v.n):
                sync.wait_ge(b.s_v.h, _v)
                for kc in range(NC8):
                    ins = sync.dma_start(
                        out=xres[kc * 128:(kc + 1) * 128, _t0:_t0 + 512],
                        in_=OL[:, kc * 512:(kc + 1) * 512])
                    b.s_st.inc(ins, 16)
            blk.sync(sto)



def p8_tile(b, xres, y_out, XC, XLN, XNB, STB, ST1, ST2, WT1, WT2, WT3,
            t0, TT):
    blk = b.blk

    def ld(sync, _vn=b.s_v.n, _an=b.s_a.n):
        sync.wait_ge(b.s_v.h, _vn)
        sync.wait_ge(b.s_a.h, _an)
        for kc in range(NC8):
            ins = sync.dma_start(out=XC[:, kc * TT:(kc + 1) * TT],
                                 in_=xres[kc * 128:(kc + 1) * 128, t0:t0 + TT])
            b.s_ld.inc(ins, 16)
    blk.sync(ld)

    emit_ln(b, lambda kc: XC[:, kc * TT:(kc + 1) * TT],
            lambda kc: XLN[:, kc * TT:(kc + 1) * TT], TT,
            wait_sems=(b.s_ld,),
            out_bf=lambda kc: XNB[:, kc * TT:(kc + 1) * TT])

    for wi, (WT, srcbuf, dstbuf) in enumerate(((WT1, XNB, STB),
                                               (WT2, STB, XNB),
                                               (WT3, XNB, STB))):
        rhs = lambda kc, _s=srcbuf: _s[:, kc * TT:(kc + 1) * TT]
        last = (wi == 2)
        dsem = b.s_a if last else b.s_v
        plan = mm_set(b, WT, rhs, TT, NC8, NC8, NC8 * 128, (b.s_v, b.s_w),
                      dsem)
        if not last:
            def v_c(vector, _dst=dstbuf, _plan=plan):
                for oc, ps, n in _plan:
                    vector.wait_ge(b.s_mm.h, n)
                    ins = vector.tensor_copy(
                        _dst[:, oc * TT:(oc + 1) * TT], ps[:, 0:TT])
                    b.s_v.inc(ins)
            blk.vector(v_c)
        else:
            def a_c(scalar, _plan=plan):
                for oc, ps, n in _plan:
                    scalar.wait_ge(b.s_mm.h, n)
                    ins = scalar.activation(
                        ST1[:, oc * TT:(oc + 1) * TT], ps[:, 0:TT],
                        AF.Tanh, bias=b.v_l1b[:, oc:oc + 1])
                    b.s_a.inc(ins)
            blk.scalar(a_c)

    def w_t(vector, _a=b.s_a.n):
        vector.wait_ge(b.s_a.h, _a)
        ins = vector.engine_nop()
        b.s_v.inc(ins)
    blk.vector(w_t)

    emit_ln(b, lambda kc: ST1[:, kc * TT:(kc + 1) * TT],
            lambda kc: ST2[:, kc * TT:(kc + 1) * TT], TT,
            add_src=lambda kc: XC[:, kc * TT:(kc + 1) * TT])

    def st(sync, _v=b.s_v.n):
        sync.wait_ge(b.s_v.h, _v)
        for kc in range(NC8):
            ins = sync.dma_start(out=y_out[kc * 128:(kc + 1) * 128, t0:t0 + TT],
                                 in_=ST2[:, kc * TT:(kc + 1) * TT])
            b.s_st.inc(ins, 16)
    blk.sync(st)



def make_masks(QTILE):
    NAL = QTILE // 128
    m = np.full((NAL * 128, QTILE), -1e30, np.float32)
    for al in range(NAL):
        for p in range(128):
            k = al * 128 + p
            m[al * 128 + p, k:] = 0.0
    return m


def make_in_map(inputs, bidx, S=4096, QTILE=512):
    T = np.ascontiguousarray
    x = np.asarray(inputs["L0_attn_x"])[bidx]
    Wq = np.asarray(inputs["L1_attn_W_qkv"])
    Wq3 = Wq.reshape(NH, DHD, 3, D)
    Wq_de = np.concatenate([Wq3[:, :, c, :].reshape(NH * DHD, D)
                            for c in range(3)], axis=0)
    bq = np.asarray(inputs["L1_attn_b_qkv"]).reshape(NH, DHD, 3)
    bq_de = np.concatenate([bq[:, :, c].reshape(-1) for c in range(3)])

    m = {
        "x": T(x.T),
        "gamma": np.asarray(inputs["L0_attn_gamma"]),
        "gWiT": T(np.asarray(inputs["L0_attn_gWi"]).T),
        "gWifT": T(np.asarray(inputs["L0_attn_gWif"]).T),
        "gWhfT": T(np.asarray(inputs["L0_attn_gWhf"]).T),
        "gbhf": np.asarray(inputs["L0_attn_gbhf"]),
        "w1T": T(np.asarray(inputs["L0_ffn_w1_w"]).T),
        "w2T": T(np.asarray(inputs["L0_ffn_w2_w"]).T),
        "h1T": T(np.asarray(inputs["L0_ffn_h1_w"]).T),
        "h2T": T(np.asarray(inputs["L0_ffn_h2_w"]).T),
        "w3T": T(np.asarray(inputs["L0_ffn_w3_w"]).T),
        "w1s": np.asarray(inputs["L0_ffn_w1_s"]),
        "w2s": np.asarray(inputs["L0_ffn_w2_s"]),
        "h1s": np.asarray(inputs["L0_ffn_h1_s"]),
        "h2s": np.asarray(inputs["L0_ffn_h2_s"]),
        "w3s": np.asarray(inputs["L0_ffn_w3_s"]),
        "WqkvT": T(Wq_de.T),
        "bqkv": T(bq_de),
        "WoT": T(np.asarray(inputs["L1_attn_W_o"]).T),
        "bo": np.asarray(inputs["L1_attn_b_o"]),
        "l1w": np.asarray(inputs["L1_ffn_weight"]),
        "l1a1": np.asarray(inputs["L1_ffn_aux_weight1"]),
        "l1a2": np.asarray(inputs["L1_ffn_aux_weight2"]),
        "l1bias": np.asarray(inputs["L1_ffn_bias"]),
        "amask": make_masks(QTILE),
    }
    for i in range(2):
        m[f"lWiT{i}"] = T(np.asarray(inputs["L0_attn_lWi"])[i].T)
        m[f"lWifT{i}"] = T(np.asarray(inputs["L0_attn_lWif"])[i].T)
        m[f"lWhfT{i}"] = T(np.asarray(inputs["L0_attn_lWhf"])[i].T)
        m[f"lbhf{i}"] = np.asarray(inputs["L0_attn_lbhf"])[i]
        m[f"linit{i}"] = np.asarray(inputs["L0_attn_l_init_hidden"])[i]
    return {k: np.ascontiguousarray(np.asarray(v, dtype=np.float32))
            for k, v in m.items()}


_CACHED = {}


def kernel(**inputs):
    S = 4096
    if "nc" not in _CACHED:
        _CACHED["nc"] = build_program(S=S)
    nc = _CACHED["nc"]
    in_maps = [make_in_map(inputs, c % 4, S=S) for c in range(8)]
    res = run_bass_kernel_spmd(nc, in_maps, core_ids=list(range(8)))
    out = np.stack([res.results[bidx]["y"].T for bidx in range(4)])
    return out.astype(np.float32)


# version 02
# speedup vs baseline: 1.1434x; 1.1434x over previous
import contextlib

import numpy as np
import ml_dtypes
import concourse.bass as bass
import concourse.mybir as mybir
from concourse.bass_types import DynSlice
from concourse.bass_utils import run_bass_kernel_spmd

F32 = mybir.dt.float32
BF16 = mybir.dt.bfloat16
AF = mybir.ActivationFunctionType
ALU = mybir.AluOpType

D = 1024
NC8 = 8
DH = 4096
NH = 16
DHD = 64
LN_EPS = 1e-5
RMS_EPS = 1.1920929e-07


class Sem:
    def __init__(self, es, nc, name):
        self.h = es.enter_context(nc.semaphore(name))
        self.n = 0

    def inc(self, instr, k=1):
        instr.then_inc(self.h, k)
        self.n += k
        return instr

    def wait(self, eng, val=None):
        eng.wait_ge(self.h, self.n if val is None else val)


class PsumPool:

    def __init__(self, b, tiles, name):
        self.tiles = tiles
        self.pending = [None] * len(tiles)
        self.count = 0

    def acquire(self, tensor):
        slot = self.count % len(self.tiles)
        self.count += 1
        if self.pending[slot] is not None:
            sem, val = self.pending[slot]
            tensor.wait_ge(sem.h, val)
            self.pending[slot] = None
        return self.tiles[slot], slot

    def expect(self, slot, sem, val):
        self.pending[slot] = (sem, val)


class B:
    pass


def wait_reg(b, eng, sem, iv, const, lo, hi):
    from concourse.expressions import RuntimeValue
    from concourse.bass_primitives_rust import RegisterHandles
    et = eng.engine
    reg = b.wregs.get(et)
    if reg is None:
        reg = b.nc.alloc_register(et, f"wreg{len(b.wregs)}")
        b.wregs[et] = reg
    eng.reg_add(reg, iv, const)
    rv = RuntimeValue(val=RegisterHandles(reg), min_val=lo, max_val=hi)
    eng.wait_ge(sem.h, rv)


@contextlib.contextmanager
def fori(nc, eng, start, end, step=1):
    from concourse.ordered_engine_set import OrderedEngineSet
    engines = OrderedEngineSet([eng.engine])
    loop_id = nc.next_id()
    loop_start = f"ffori_{loop_id}_loop"
    loop_end = f"ffori_{loop_id}_end"
    regs = nc.alloc_registers(f"ffori_{loop_id}_i", engines=engines)
    nc.regs_mov(regs, start)
    nc.br(loop_start, engines=engines)
    iv = None
    with nc.body(loop_start, valid_engines=engines):
        iv = nc.snap(regs, min_val=start, max_val=end - 1)
        yield iv
        nc.regs_alu(regs, regs, step, op=ALU.add)
        nc.br_lt(regs, end, on_true=loop_start, on_false=loop_end,
                 engines=engines)
    nc.switch_bb(loop_end)
    try:
        for h in iv.val.handles:
            nc.free_register(h)
    except Exception:
        pass
    for h in regs:
        nc.free_register(h)


def barrier(b):
    blk, s = b.blk, b.bar
    dma_waits = [(sem, sem.n) for sem in
                 (b.s_st, b.s_hs, b.s_ld, b.s_ldg, b.s_ac, b.s_w,
                  b.s_ac2[0], b.s_ac2[1], b.s_hs2[0], b.s_hs2[1],
                  b.s_wb[0], b.s_wb[1], b.s_w2)]

    def _inc_sync(eng, _w=dma_waits, _s=s):
        for sem, n in _w:
            eng.wait_ge(sem.h, n)
        eng.drain()
        eng.sem_inc(_s.h, 1)
    blk.sync(_inc_sync)

    def _inc(eng, _s=s):
        eng.drain()
        eng.sem_inc(_s.h, 1)
    for name in ("tensor", "vector", "scalar", "gpsimd"):
        getattr(blk, name)(_inc)
    s.n += 5
    t = s.n
    for name in ("sync", "tensor", "vector", "scalar", "gpsimd"):
        getattr(blk, name)(lambda eng, _s=s, _t=t: eng.wait_ge(_s.h, _t))


def load_weight_1024(b, dst, src, n_out_cols, sem=None):
    s = sem if sem is not None else b.s_w
    def go(gpsimd):
        ins = gpsimd.dma_start(
            out=dst[:, 0:NC8 * n_out_cols].rearrange("p (kc o) -> p kc o",
                                                     o=n_out_cols),
            in_=src.rearrange("(kc p) o -> p kc o", p=128))
        s.inc(ins, 16)
    b.blk.gpsimd(go)


def mm_acc(b, tensor, lhsT_sb, rhs, tt, oc, n_k, ps, stride):
    for kc in range(n_k):
        col = kc * stride + oc * 128
        ins = tensor.matmul(ps[:, 0:tt], lhsT_sb[:, col:col + 128], rhs(kc),
                            start=(kc == 0), stop=(kc == n_k - 1))
    return ins


def emit_stats(b, src, n_chunks, tt, rows, wait_sems=()):
    blk = b.blk
    waits = list(wait_sems) + [b.s_v]
    for row, kind in rows:
        if kind == "sumsq":
            wl = [(w, w.n) for w in waits]

            def v_sq(vector, _wl=wl, _tt=tt, _n=n_chunks, _src=src):
                for w, n in _wl:
                    w.wait(vector, n)
                for kc in range(_n):
                    ins = vector.tensor_mul(b.SQ[:, kc * _tt:(kc + 1) * _tt],
                                            _src(kc), _src(kc))
                b.s_sq.inc(ins)
            blk.vector(v_sq)

            def t_sq(tensor, _row=row, _tt=tt, _n=n_chunks, _sq=b.s_sq.n):
                tensor.wait_ge(b.s_sq.h, _sq)
                for kc in range(_n):
                    ins = tensor.matmul(b.PSST[_row:_row + 1, 0:_tt],
                                        b.ones_col[:, 0:1],
                                        b.SQ[:, kc * _tt:(kc + 1) * _tt],
                                        start=(kc == 0), stop=(kc == _n - 1))
                b.s_mm.inc(ins)
            blk.tensor(t_sq)
        else:
            wl = [(w, w.n) for w in waits]

            def t_sum(tensor, _wl=wl, _row=row, _tt=tt, _n=n_chunks, _src=src):
                for w, n in _wl:
                    w.wait(tensor, n)
                for kc in range(_n):
                    ins = tensor.matmul(b.PSST[_row:_row + 1, 0:_tt],
                                        b.ones_col[:, 0:1], _src(kc),
                                        start=(kc == 0), stop=(kc == _n - 1))
                b.s_mm.inc(ins)
            blk.tensor(t_sum)


def bcast(b, tensor, row_ap, ps, tt):
    return tensor.matmul(ps[:, 0:tt], b.ones_row[0:1, :], row_ap,
                         start=True, stop=True)


def emit_ln(b, src, out, tt, n_chunks=NC8, out_bf=None, wait_sems=(),
            add_src=None, eps=LN_EPS):
    blk = b.blk
    emit_stats(b, src, n_chunks, tt, [(0, "sum"), (32, "sumsq")], wait_sems)
    nfeat = float(128 * n_chunks)

    def v_rows(vector, _tt=tt, _mm=b.s_mm.n):
        vector.wait_ge(b.s_mm.h, _mm)
        vector.tensor_scalar_mul(b.row_m[0:1, 0:_tt], b.PSST[0:1, 0:_tt], 1.0 / nfeat)
        vector.tensor_scalar_mul(b.row_a[0:1, 0:_tt], b.PSST[32:33, 0:_tt], 1.0 / nfeat)
        vector.drain()
        vector.tensor_mul(b.row_b[0:1, 0:_tt], b.row_m[0:1, 0:_tt],
                          b.row_m[0:1, 0:_tt])
        vector.drain()
        ins = vector.tensor_sub(b.row_a[0:1, 0:_tt], b.row_a[0:1, 0:_tt],
                                b.row_b[0:1, 0:_tt])
        b.s_v.inc(ins)
    blk.vector(v_rows)

    def a_rstd(scalar, _tt=tt, _v=b.s_v.n):
        scalar.wait_ge(b.s_v.h, _v)
        ins = scalar.activation(b.row_b[0:1, 0:_tt], b.row_a[0:1, 0:_tt],
                                AF.Sqrt, bias=float(eps))
        b.s_a.inc(ins)
    blk.scalar(a_rstd)

    def v_recip(vector, _tt=tt, _a=b.s_a.n):
        vector.wait_ge(b.s_a.h, _a)
        vector.drain()
        ins = vector.reciprocal(b.row_b[0:1, 0:_tt], b.row_b[0:1, 0:_tt])
        b.s_v.inc(ins)
    blk.vector(v_recip)

    def t_bc(tensor, _tt=tt, _a=b.s_v.n):
        tensor.wait_ge(b.s_v.h, _a)
        bcast(b, tensor, b.row_m[0:1, 0:_tt], b.PSB, _tt)
        ins = bcast(b, tensor, b.row_b[0:1, 0:_tt], b.PSB2, _tt)
        b.s_mm.inc(ins)
    blk.tensor(t_bc)

    def v_norm(vector, _tt=tt, _n=n_chunks, _mm=b.s_mm.n):
        vector.wait_ge(b.s_mm.h, _mm)
        for kc in range(_n):
            vector.tensor_sub(out(kc), src(kc), b.PSB[:, 0:_tt])
        vector.drain()
        for kc in range(_n):
            ins = vector.tensor_mul(out(kc), out(kc), b.PSB2[:, 0:_tt])
        if add_src is not None:
            vector.drain()
            for kc in range(_n):
                ins = vector.tensor_add(out(kc), out(kc), add_src(kc))
        if out_bf is not None:
            vector.drain()
            for kc in range(_n):
                ins = vector.tensor_copy(out_bf(kc), out(kc))
        b.s_v.inc(ins)
    blk.vector(v_norm)


def mm_set(b, WT, rhs, TT, n_out, n_k, stride, pre_waits, data_sem,
           incs_per_oc=1):
    plan = []
    base = data_sem.n

    def t_go(tensor, _pw=[(w, w.n) for w in pre_waits]):
        for w, n in _pw:
            w.wait(tensor, n)
        for oc in range(n_out):
            ps, slot = b.pool.acquire(tensor)
            b.pool.expect(slot, data_sem, base + incs_per_oc * oc + 1)
            ins = mm_acc(b, tensor, WT, rhs, TT, oc, n_k, ps, stride)
            b.s_mm.inc(ins)
            plan.append((oc, ps, b.s_mm.n))
    b.blk.tensor(t_go)
    return plan



def build_program(S=4096, CH=1024, TT=512, TTF=512, QTILE=512, phases=99,
                  debug=False, scan_stub=False, scan_no_hist=False,
                  scan_no_mm=False):
    nc = bass.Bass("TRN2", target_bir_lowering=False)
    es = contextlib.ExitStack()
    nc._keepalive_es = es
    b = B()
    b.nc, b.es, b.S = nc, es, S
    b.wregs = {}
    b.scan_no_hist, b.scan_no_mm = scan_no_hist, scan_no_mm
    NT = S // TT
    NAL = QTILE // 128
    dk = "ExternalOutput" if debug else "Internal"

    def din(name, shape, dt=F32):
        return nc.dram_tensor(name, shape, dt, kind="ExternalInput").ap()

    def dint(name, shape, dt=F32):
        return nc.dram_tensor(name, shape, dt, kind=dk).ap()

    x_in = din("x", [D, S])
    gamma = din("gamma", [D])
    lWiT = [din(f"lWiT{i}", [D, D]) for i in range(2)]
    lWifT = [din(f"lWifT{i}", [D, D]) for i in range(2)]
    lWhfT = [din(f"lWhfT{i}", [D, D]) for i in range(2)]
    lbhf = [din(f"lbhf{i}", [D]) for i in range(2)]
    linit = [din(f"linit{i}", [D]) for i in range(2)]
    gWiT = din("gWiT", [D, D]); gWifT = din("gWifT", [D, D])
    gWhfT = din("gWhfT", [D, D]); gbhf = din("gbhf", [D])
    w1T = din("w1T", [D, DH]); w2T = din("w2T", [D, DH])
    h1T = din("h1T", [DH, DH]); h2T = din("h2T", [DH, DH])
    w3T = din("w3T", [DH, D])
    w1s = din("w1s", [DH]); w2s = din("w2s", [DH])
    h1s = din("h1s", [DH]); h2s = din("h2s", [DH])
    w3s = din("w3s", [D])
    WqkvT = din("WqkvT", [D, 3 * D])
    bqkv = din("bqkv", [3 * D])
    WoT = din("WoT", [D, D])
    bo = din("bo", [D])
    l1w = din("l1w", [D, D]); l1a1 = din("l1a1", [D, D]); l1a2 = din("l1a2", [D, D])
    l1bias = din("l1bias", [D])
    amask = din("amask", [NAL * 128, QTILE])

    y_out = nc.dram_tensor("y", [D, S], F32, kind="ExternalOutput").ap()

    xres = dint("xres", [D, S])
    xn_h = dint("xn_h", [D, S])
    xln_h = dint("xln_h", [D, S])
    A_h = [dint(f"A_h{i}", [D, S]) for i in range(2)]
    C_h = [dint(f"C_h{i}", [D, S]) for i in range(2)]
    hs_h = [dint(f"hs_h{i}", [D, S], BF16) for i in range(2)]
    qT_h = dint("qT_h", [D, S], BF16)
    kT_h = dint("kT_h", [D, S], BF16)
    v_h = dint("v_h", [NH, S, DHD])
    zT_h = dint("zT_h", [D, S], BF16)

    sb = nc.alloc_sbuf_tensor
    b.ones_col = sb("ones_col", [128, 1], F32)
    b.ones_row = sb("ones_row", [1, 128], F32)
    b.row_m = sb("row_m", [1, 512], F32)
    b.row_a = sb("row_a", [1, 512], F32)
    b.row_b = sb("row_b", [1, 512], F32)
    b.g1 = sb("g1", [128, 8], F32)
    b.v_lbhf = sb("v_lbhf", [128, 16], F32)
    b.v_linit = sb("v_linit", [128, 16], F32)
    b.v_gbhf = sb("v_gbhf", [128, 8], F32)
    b.v_w1s = sb("v_w1s", [128, 32], F32)
    b.v_w2s = sb("v_w2s", [128, 32], F32)
    b.v_h1s = sb("v_h1s", [128, 32], F32)
    b.v_h2s = sb("v_h2s", [128, 32], F32)
    b.v_w3s = sb("v_w3s", [128, 8], F32)
    b.v_bqkv = sb("v_bqkv", [128, 24], F32)
    b.v_bo = sb("v_bo", [128, 8], F32)
    b.v_l1b = sb("v_l1b", [128, 8], F32)
    b.SQ = sb("SQ", [128, NC8 * 512], F32)

    b.cst = sb("cst", [1, 8], F32)
    b.dum = sb("dum", [1, 8], F32)
    b.cvals = {float(LN_EPS): 0, float(RMS_EPS): 1, float(1.0 / D): 2}
    for val, idx in b.cvals.items():
        nc.const_aps.register(F32, val, b.cst[0:1, idx:idx + 1])

    b.PS = [nc.alloc_psum_tensor(f"PS{i}", [128, 512], F32) for i in range(4)]
    b.PSST = nc.alloc_psum_tensor("PSST", [128, 512], F32)
    b.PSB = nc.alloc_psum_tensor("PSB", [128, 512], F32)
    b.PSB2 = nc.alloc_psum_tensor("PSB2", [128, 512], F32)
    b.PSS = nc.alloc_psum_tensor("PSS", [128, 512], F32)

    for nm in ("bar", "s_ld", "s_ldg", "s_st", "s_w", "s_mm", "s_sq", "s_v",
               "s_a", "s_g", "s_ac", "s_hs", "s_smm", "s_su", "s_sg", "s_sh"):
        setattr(b, nm, Sem(es, nc, nm))
    b.s_ac2 = [Sem(es, nc, f"s_ac2_{i}") for i in range(2)]
    b.s_hs2 = [Sem(es, nc, f"s_hs2_{i}") for i in range(2)]
    b.s_wb = [Sem(es, nc, f"s_wb_{i}") for i in range(2)]
    b.s_w2 = Sem(es, nc, "s_w2")

    with nc.Block() as blk:
        b.blk = blk
        b.pool = PsumPool(b, b.PS, "psf")

        def p0_sync(sync):
            pairs = [(b.g1[:, 0:8], gamma)]
            for i in range(2):
                pairs.append((b.v_lbhf[:, 8 * i:8 * i + 8], lbhf[i]))
                pairs.append((b.v_linit[:, 8 * i:8 * i + 8], linit[i]))
            pairs += [(b.v_gbhf[:, 0:8], gbhf), (b.v_w3s[:, 0:8], w3s),
                      (b.v_bo[:, 0:8], bo), (b.v_l1b[:, 0:8], l1bias),
                      (b.v_w1s[:, 0:32], w1s), (b.v_w2s[:, 0:32], w2s),
                      (b.v_h1s[:, 0:32], h1s), (b.v_h2s[:, 0:32], h2s),
                      (b.v_bqkv[:, 0:24], bqkv)]
            with nc.allow_non_contiguous_dma(reason="tiny per-feature vectors"):
                for dstap, src in pairs:
                    ins = sync.dma_start(out=dstap,
                                         in_=src.rearrange("(kc p) -> p kc", p=128))
                    b.s_ld.inc(ins, 16)
            ins = sync.dma_start(out=xres[:, :], in_=x_in[:, :])
            b.s_ld.inc(ins, 16)
        blk.sync(p0_sync)

        def p0_v(vector):
            b.s_ld.wait(vector)
            vector.memset(b.ones_col[:, :], 1.0)
            vector.memset(b.ones_row[:, :], 1.0)
            for val, idx in b.cvals.items():
                vector.memset(b.cst[0:1, idx:idx + 1], val)
            vector.tensor_scalar_add(b.g1[:, 0:8], b.g1[:, 0:8], 1.0)
        blk.vector(p0_v)
        barrier(b)

        if phases >= 1:
            with contextlib.ExitStack() as ph:
                XC = ph.enter_context(nc.sbuf_tensor("XC1", [128, NC8 * TT], F32))
                XLN = ph.enter_context(nc.sbuf_tensor("XLN1", [128, NC8 * TT], F32))
                XN = ph.enter_context(nc.sbuf_tensor("XN1", [128, NC8 * TT], F32))
                XNB = ph.enter_context(nc.sbuf_tensor("XNB1", [128, NC8 * TT], BF16))
                ST1 = ph.enter_context(nc.sbuf_tensor("STa1", [128, NC8 * TT], F32))
                ST2 = ph.enter_context(nc.sbuf_tensor("STb1", [128, NC8 * TT], F32))
                WT1 = ph.enter_context(nc.sbuf_tensor("WT1a", [128, NC8 * D], BF16))
                WT2 = ph.enter_context(nc.sbuf_tensor("WT1b", [128, NC8 * D], BF16))
                load_weight_1024(b, WT1, lWiT[0], D)
                load_weight_1024(b, WT2, lWifT[0], D)
                for it in range(NT):
                    p1_tile(b, x_in, xln_h, xn_h, A_h[0], C_h[0],
                            XC, XLN, XN, XNB, ST1, ST2, WT1, WT2, it * TT, TT)
            barrier(b)

        if phases >= 2:
            if scan_stub:
                def stub2(sync, _st=b.s_st.n):
                    sync.wait_ge(b.s_st.h, _st)
                    ins = sync.dma_start(out=hs_h[0][:, :], in_=A_h[0][:, :])
                    b.s_st.inc(ins, 16)
                b.blk.sync(stub2)
            else:
                emit_scan(b, lWhfT[0], A_h[0], C_h[0], hs_h[0], 0, S, CH)
            barrier(b)

        if phases >= 3:
            with contextlib.ExitStack() as ph:
                XNB = ph.enter_context(nc.sbuf_tensor("XNB3", [128, NC8 * TT], BF16))
                ST1 = ph.enter_context(nc.sbuf_tensor("STa3", [128, NC8 * TT], F32))
                ST2 = ph.enter_context(nc.sbuf_tensor("STb3", [128, NC8 * TT], F32))
                WT1 = ph.enter_context(nc.sbuf_tensor("WT3a", [128, NC8 * D], BF16))
                WT2 = ph.enter_context(nc.sbuf_tensor("WT3b", [128, NC8 * D], BF16))
                load_weight_1024(b, WT1, lWiT[1], D)
                load_weight_1024(b, WT2, lWifT[1], D)
                for it in range(NT):
                    t0 = it * TT

                    def p3_ld(gpsimd, _t0=t0, _vn=b.s_v.n, _an=b.s_a.n):
                        gpsimd.wait_ge(b.s_v.h, _vn)
                        gpsimd.wait_ge(b.s_a.h, _an)
                        for kc in range(NC8):
                            ins = gpsimd.dma_start(
                                out=XNB[:, kc * TT:(kc + 1) * TT],
                                in_=hs_h[0][kc * 128:(kc + 1) * 128, _t0:_t0 + TT])
                            b.s_ldg.inc(ins, 16)
                    blk.gpsimd(p3_ld)
                    mixer_pre(b, XNB, ST1, ST2, WT1, WT2, A_h[1], C_h[1], 1,
                              t0, TT, in_sems=(b.s_ldg, b.s_w))
            barrier(b)

        if phases >= 4:
            emit_scan(b, lWhfT[1], A_h[1], C_h[1], hs_h[1], 1, S, CH)
            barrier(b)

        if phases >= 5:
            with contextlib.ExitStack() as ph:
                XC = ph.enter_context(nc.sbuf_tensor("XC5", [128, NC8 * TT], F32))
                XLN = ph.enter_context(nc.sbuf_tensor("XLN5", [128, NC8 * TT], F32))
                XN = ph.enter_context(nc.sbuf_tensor("XN5", [128, NC8 * TT], F32))
                XNB = ph.enter_context(nc.sbuf_tensor("XNB5", [128, NC8 * TT], BF16))
                H1B = ph.enter_context(nc.sbuf_tensor("H1B5", [128, NC8 * TT], BF16))
                ST1 = ph.enter_context(nc.sbuf_tensor("STa5", [128, NC8 * TT], F32))
                ST2 = ph.enter_context(nc.sbuf_tensor("STb5", [128, NC8 * TT], F32))
                GS = ph.enter_context(nc.sbuf_tensor("GS5", [128, NC8 * TT], F32))
                WT1 = ph.enter_context(nc.sbuf_tensor("WT5a", [128, NC8 * D], BF16))
                WT2 = ph.enter_context(nc.sbuf_tensor("WT5b", [128, NC8 * D], BF16))
                WT3 = ph.enter_context(nc.sbuf_tensor("WT5c", [128, NC8 * D], BF16))
                load_weight_1024(b, WT1, gWiT, D)
                load_weight_1024(b, WT2, gWifT, D)
                load_weight_1024(b, WT3, gWhfT, D)
                for it in range(NT):
                    p5_tile(b, hs_h[1], xn_h, xln_h, x_in, xres, XC, XLN, XN,
                            XNB, H1B, ST1, ST2, GS, WT1, WT2, WT3, it * TT, TT)
            barrier(b)

        if phases >= 6:
            emit_p6(b, xres, w1T, w2T, h1T, h2T, w3T, S, TTF)
            barrier(b)

        if phases >= 7:
            emit_p7(b, xres, WqkvT, WoT, bqkv, qT_h, kT_h, v_h, zT_h, amask,
                    S, TT, QTILE)
            barrier(b)

        if phases >= 8:
            with contextlib.ExitStack() as ph:
                XC = ph.enter_context(nc.sbuf_tensor("XC8", [128, NC8 * TT], F32))
                XLN = ph.enter_context(nc.sbuf_tensor("XLN8", [128, NC8 * TT], F32))
                XNB = ph.enter_context(nc.sbuf_tensor("XNB8", [128, NC8 * TT], BF16))
                STB = ph.enter_context(nc.sbuf_tensor("STB8", [128, NC8 * TT], BF16))
                ST1 = ph.enter_context(nc.sbuf_tensor("STa8", [128, NC8 * TT], F32))
                ST2 = ph.enter_context(nc.sbuf_tensor("STb8", [128, NC8 * TT], F32))
                WT1 = ph.enter_context(nc.sbuf_tensor("WT8a", [128, NC8 * D], BF16))
                WT2 = ph.enter_context(nc.sbuf_tensor("WT8b", [128, NC8 * D], BF16))
                WT3 = ph.enter_context(nc.sbuf_tensor("WT8c", [128, NC8 * D], BF16))
                load_weight_1024(b, WT1, l1w, D)
                load_weight_1024(b, WT2, l1a1, D)
                load_weight_1024(b, WT3, l1a2, D)
                for it in range(NT):
                    p8_tile(b, xres, y_out, XC, XLN, XNB, STB, ST1, ST2,
                            WT1, WT2, WT3, it * TT, TT)
        elif phases < 8:
            def cp_out(sync, _vn=b.s_v.n):
                sync.wait_ge(b.s_v.h, _vn)
                ins = sync.dma_start(out=y_out[:, :], in_=xres[:, :])
                b.s_st.inc(ins, 16)
            blk.sync(cp_out)

        def fin(gpsimd, _n=b.s_st.n):
            gpsimd.wait_ge(b.s_st.h, _n)
        blk.gpsimd(fin)

    return nc



def p1_tile(b, x_in, xln_h, xn_h, A_hb, C_hb, XC, XLN, XN, XNB, ST1, ST2,
            WT1, WT2, t0, TT):
    blk = b.blk

    def ld(sync, _vn=b.s_v.n, _an=b.s_a.n):
        sync.wait_ge(b.s_v.h, _vn)
        sync.wait_ge(b.s_a.h, _an)
        for kc in range(NC8):
            ins = sync.dma_start(out=XC[:, kc * TT:(kc + 1) * TT],
                                 in_=x_in[kc * 128:(kc + 1) * 128, t0:t0 + TT])
            b.s_ld.inc(ins, 16)
    blk.sync(ld)

    src = lambda kc: XC[:, kc * TT:(kc + 1) * TT]
    oln = lambda kc: XLN[:, kc * TT:(kc + 1) * TT]
    emit_ln(b, src, oln, TT, wait_sems=(b.s_ld,))
    emit_stats(b, oln, NC8, TT, [(0, "sumsq")])

    def v_mx(vector, _mm=b.s_mm.n):
        vector.wait_ge(b.s_mm.h, _mm)
        ins = vector.tensor_scalar_max(b.row_a[0:1, 0:TT], b.PSST[0:1, 0:TT],
                                       1e-24)
        b.s_v.inc(ins)
    blk.vector(v_mx)

    def a_rs(scalar, _v=b.s_v.n):
        scalar.wait_ge(b.s_v.h, _v)
        ins = scalar.activation(b.row_b[0:1, 0:TT], b.row_a[0:1, 0:TT],
                                AF.Sqrt, scale=1.0 / D)
        b.s_a.inc(ins)
    blk.scalar(a_rs)

    def v_rec2(vector, _a=b.s_a.n):
        vector.wait_ge(b.s_a.h, _a)
        vector.drain()
        ins = vector.reciprocal(b.row_b[0:1, 0:TT], b.row_b[0:1, 0:TT])
        b.s_v.inc(ins)
    blk.vector(v_rec2)

    def t_bc(tensor, _a=b.s_v.n):
        tensor.wait_ge(b.s_v.h, _a)
        ins = bcast(b, tensor, b.row_b[0:1, 0:TT], b.PSB, TT)
        b.s_mm.inc(ins)
    blk.tensor(t_bc)

    def v_xn(vector, _mm=b.s_mm.n):
        vector.wait_ge(b.s_mm.h, _mm)
        for kc in range(NC8):
            vector.tensor_mul(XN[:, kc * TT:(kc + 1) * TT],
                              XLN[:, kc * TT:(kc + 1) * TT], b.PSB[:, 0:TT])
        vector.drain()
        for kc in range(NC8):
            vector.tensor_scalar_mul(XN[:, kc * TT:(kc + 1) * TT],
                                     XN[:, kc * TT:(kc + 1) * TT],
                                     b.g1[:, kc:kc + 1])
        vector.drain()
        for kc in range(NC8):
            ins = vector.tensor_copy(XNB[:, kc * TT:(kc + 1) * TT],
                                     XN[:, kc * TT:(kc + 1) * TT])
        b.s_v.inc(ins)
    blk.vector(v_xn)

    def st(sync, _v=b.s_v.n):
        sync.wait_ge(b.s_v.h, _v)
        for kc in range(NC8):
            i1 = sync.dma_start(out=xln_h[kc * 128:(kc + 1) * 128, t0:t0 + TT],
                                in_=XLN[:, kc * TT:(kc + 1) * TT])
            b.s_st.inc(i1, 16)
            i2 = sync.dma_start(out=xn_h[kc * 128:(kc + 1) * 128, t0:t0 + TT],
                                in_=XN[:, kc * TT:(kc + 1) * TT])
            b.s_st.inc(i2, 16)
    blk.sync(st)

    mixer_pre(b, XNB, ST1, ST2, WT1, WT2, A_hb, C_hb, 0, t0, TT,
              in_sems=(b.s_v, b.s_w))


def mixer_pre(b, SRCB, ST1, ST2, WT1, WT2, A_hb, C_hb, layer, t0, TT,
              in_sems=()):
    blk = b.blk
    rhs = lambda kc: SRCB[:, kc * TT:(kc + 1) * TT]

    plan = mm_set(b, WT1, rhs, TT, NC8, NC8, NC8 * 128, in_sems, b.s_a,
                  incs_per_oc=2)

    def a_t(scalar):
        for oc, ps, n in plan:
            scalar.wait_ge(b.s_mm.h, n)
            i1 = scalar.activation(ST1[:, oc * TT:(oc + 1) * TT],
                                   ps[:, 0:TT], AF.Tanh)
            b.s_a.inc(i1)
            scalar.drain()
            i2 = scalar.activation(ST1[:, oc * TT:(oc + 1) * TT],
                                   ST1[:, oc * TT:(oc + 1) * TT], AF.Tanh)
            b.s_a.inc(i2)
    blk.scalar(a_t)

    def st_a(sync, _a=b.s_a.n):
        sync.wait_ge(b.s_a.h, _a)
        for oc in range(NC8):
            ins = sync.dma_start(out=A_hb[oc * 128:(oc + 1) * 128, t0:t0 + TT],
                                 in_=ST1[:, oc * TT:(oc + 1) * TT])
            b.s_st.inc(ins, 16)
    blk.sync(st_a)

    plan2 = mm_set(b, WT2, rhs, TT, NC8, NC8, NC8 * 128, (), b.s_v)

    def v_c(vector):
        for oc, ps, n in plan2:
            vector.wait_ge(b.s_mm.h, n)
            ins = vector.tensor_scalar_add(
                ST2[:, oc * TT:(oc + 1) * TT], ps[:, 0:TT],
                b.v_lbhf[:, 8 * layer + oc:8 * layer + oc + 1])
            b.s_v.inc(ins)
    blk.vector(v_c)

    def st_c(sync, _v=b.s_v.n):
        sync.wait_ge(b.s_v.h, _v)
        for oc in range(NC8):
            ins = sync.dma_start(out=C_hb[oc * 128:(oc + 1) * 128, t0:t0 + TT],
                                 in_=ST2[:, oc * TT:(oc + 1) * TT])
            b.s_st.inc(ins, 16)
    blk.sync(st_c)



def emit_scan(b, WhfT, A_hb, C_hb, hs_hb, layer, S, CH):
    blk, nc = b.blk, b.nc
    n_chunks = S // CH

    with contextlib.ExitStack() as ph:
        WSC = ph.enter_context(nc.sbuf_tensor(f"WSC{layer}", [128, 64 * 128], BF16))
        A_sb = [ph.enter_context(nc.sbuf_tensor(f"scA{layer}_{i}", [128, NC8 * CH], F32)) for i in range(2)]
        C_sb = [ph.enter_context(nc.sbuf_tensor(f"scC{layer}_{i}", [128, NC8 * CH], F32)) for i in range(2)]
        H_sb = [ph.enter_context(nc.sbuf_tensor(f"scH{layer}_{i}", [128, NC8 * CH], BF16)) for i in range(2)]
        hcur = ph.enter_context(nc.sbuf_tensor(f"hcur{layer}", [128, 8], F32))
        hbf = ph.enter_context(nc.sbuf_tensor(f"hbf{layer}", [128, 8], BF16))
        ut = ph.enter_context(nc.sbuf_tensor(f"ut{layer}", [128, 8], F32))
        gt = ph.enter_context(nc.sbuf_tensor(f"gt{layer}", [128, 8], F32))
        et = ph.enter_context(nc.sbuf_tensor(f"et{layer}", [128, 8], F32))

        load_weight_1024(b, WSC, WhfT, 128 * NC8)

        base_mm, base_h = b.s_smm.n, b.s_sh.n
        base_u, base_g = b.s_su.n, b.s_sg.n
        base_ac, base_hs = b.s_ac.n, b.s_hs.n

        def vinit(vector, _w=b.s_w.n):
            vector.wait_ge(b.s_w.h, _w)
            vector.tensor_copy(hcur[:, :], b.v_linit[:, 8 * layer:8 * layer + 8])
            vector.drain()
            ins = vector.tensor_copy(hbf[:, :], hcur[:, :])
            ins.then_inc(b.s_sh.h, 1)
        blk.vector(vinit)
        b.s_sh.n += 1

        base_ac2 = [b.s_ac2[i].n for i in range(2)]
        base_hs2 = [b.s_hs2[i].n for i in range(2)]
        st_base = b.s_st.n

        def dma_ac(sync):
            sync.wait_ge(b.s_st.h, st_base)
            for c in range(n_chunks):
                if c >= 2:
                    sync.wait_ge(b.s_sh.h, base_h + 1 + (c - 1) * CH)
                for kc in range(NC8):
                    i1 = sync.dma_start(
                        out=A_sb[c % 2][:, kc * CH:(kc + 1) * CH],
                        in_=A_hb[kc * 128:(kc + 1) * 128, c * CH:(c + 1) * CH])
                    b.s_ac2[c % 2].inc(i1, 16)
                    i2 = sync.dma_start(
                        out=C_sb[c % 2][:, kc * CH:(kc + 1) * CH],
                        in_=C_hb[kc * 128:(kc + 1) * 128, c * CH:(c + 1) * CH])
                    b.s_ac2[c % 2].inc(i2, 16)
        blk.sync(dma_ac)

        def dma_hist(eng):
            if getattr(b, 'scan_no_hist', False):
                return
            for c in range(n_chunks):
                eng.wait_ge(b.s_sh.h, base_h + 1 + (c + 1) * CH)
                for kc in range(NC8):
                    ins = eng.dma_start(
                        out=hs_hb[kc * 128:(kc + 1) * 128, c * CH:(c + 1) * CH],
                        in_=H_sb[c % 2][:, kc * CH:(kc + 1) * CH])
                    b.s_hs2[c % 2].inc(ins, 16)
        blk.scalar(dma_hist) if getattr(b, 'scan_hist_on_scalar', False) else blk.sync(dma_hist)

        for c in range(n_chunks):
            def t_chunk(tensor, _c=c, _w=b.s_w.n):
                if getattr(b, 'scan_no_mm', False):
                    return
                if _c == 0:
                    tensor.wait_ge(b.s_w.h, _w)
                with fori(nc, tensor, 0, CH) as i:
                    wait_reg(b, tensor, b.s_sh, i, base_h + 1 + _c * CH,
                             base_h + 1 + _c * CH, base_h + _c * CH + CH)
                    for dc in range(NC8):
                        for kc in range(NC8):
                            col = (kc * NC8 + dc) * 128
                            ins = tensor.matmul(b.PSS[:, dc:dc + 1],
                                                WSC[:, col:col + 128],
                                                hbf[:, kc:kc + 1],
                                                start=(kc == 0),
                                                stop=(kc == NC8 - 1))
                    ins.then_inc(b.s_smm.h, 1)
            blk.tensor(t_chunk)
            b.s_smm.n += CH

            def a_chunk(scalar, _c=c):
                with fori(nc, scalar, 0, CH) as i:
                    scalar.activation(b.dum[0:1, 0:1], b.cst[0:1, 0:1],
                                      AF.Sigmoid)
                    scalar.drain()
                    wait_reg(b, scalar, b.s_su, i, base_u + _c * CH + 1,
                             base_u + _c * CH + 1, base_u + _c * CH + CH)
                    ins = scalar.activation(gt[:, :], ut[:, :], AF.Sigmoid)
                    ins.then_inc(b.s_sg.h, 1)
            blk.scalar(a_chunk)
            b.s_sg.n += CH

            def v_chunk(vector, _c=c):
                vector.wait_ge(b.s_ac2[_c % 2].h,
                               base_ac2[_c % 2] + 16 * 2 * NC8 * (_c // 2 + 1))
                if _c >= 2 and not getattr(b, 'scan_no_hist', False):
                    vector.wait_ge(b.s_hs2[_c % 2].h,
                                   base_hs2[_c % 2] + 16 * NC8 * ((_c - 2) // 2 + 1))
                with fori(nc, vector, 0, CH) as i:
                    if not getattr(b, 'scan_no_mm', False):
                        wait_reg(b, vector, b.s_smm, i, base_mm + _c * CH + 1,
                                 base_mm + _c * CH + 1, base_mm + _c * CH + CH)
                    sl = DynSlice(i, 8, CH)
                    ins = vector.tensor_tensor(ut[:, :], b.PSS[:, 0:8],
                                               C_sb[_c % 2][:, sl], ALU.add)
                    ins.then_inc(b.s_su.h, 1)
                    vector.tensor_tensor(et[:, :], A_sb[_c % 2][:, sl],
                                         hcur[:, :], ALU.subtract)
                    wait_reg(b, vector, b.s_sg, i, base_g + _c * CH + 1,
                             base_g + _c * CH + 1, base_g + _c * CH + CH)
                    vector.drain()
                    vector.tensor_mul(et[:, :], gt[:, :], et[:, :])
                    vector.drain()
                    vector.tensor_add(hcur[:, :], hcur[:, :], et[:, :])
                    vector.drain()
                    vector.tensor_copy(hbf[:, :], hcur[:, :])
                    ins = vector.tensor_copy(H_sb[_c % 2][:, sl], hcur[:, :])
                    ins.then_inc(b.s_sh.h, 1)
            blk.vector(v_chunk)
            b.s_su.n += CH
            b.s_sh.n += CH



def p5_tile(b, hs1_h, xn_h, xln_h, x_in, xres, XC, XLN, XN, XNB, H1B,
            ST1, ST2, GS, WT1, WT2, WT3, t0, TT):
    blk = b.blk

    def ld(sync, _vn=b.s_v.n, _an=b.s_a.n):
        sync.wait_ge(b.s_v.h, _vn)
        sync.wait_ge(b.s_a.h, _an)
        for kc in range(NC8):
            for dst, src in ((XC, x_in), (XLN, xln_h), (XN, xn_h)):
                ins = sync.dma_start(out=dst[:, kc * TT:(kc + 1) * TT],
                                     in_=src[kc * 128:(kc + 1) * 128, t0:t0 + TT])
                b.s_ld.inc(ins, 16)
    blk.sync(ld)

    def ldg(gpsimd, _vn=b.s_v.n, _an=b.s_a.n):
        gpsimd.wait_ge(b.s_v.h, _vn)
        gpsimd.wait_ge(b.s_a.h, _an)
        for kc in range(NC8):
            i1 = gpsimd.dma_start(out=H1B[:, kc * TT:(kc + 1) * TT],
                                  in_=hs1_h[kc * 128:(kc + 1) * 128, t0:t0 + TT])
            b.s_ldg.inc(i1, 16)
            i2 = gpsimd.dma_start(out=XNB[:, kc * TT:(kc + 1) * TT],
                                  in_=xn_h[kc * 128:(kc + 1) * 128, t0:t0 + TT])
            b.s_ldg.inc(i2, 16)
    blk.gpsimd(ldg)

    rh = lambda kc: H1B[:, kc * TT:(kc + 1) * TT]
    rx = lambda kc: XNB[:, kc * TT:(kc + 1) * TT]

    plan1 = mm_set(b, WT1, rh, TT, NC8, NC8, NC8 * 128, (b.s_ldg, b.s_w),
                   b.s_a, incs_per_oc=2)

    def a_t(scalar):
        for oc, ps, n in plan1:
            scalar.wait_ge(b.s_mm.h, n)
            i1 = scalar.activation(ST1[:, oc * TT:(oc + 1) * TT],
                                   ps[:, 0:TT], AF.Tanh)
            b.s_a.inc(i1)
            scalar.drain()
            i2 = scalar.activation(ST1[:, oc * TT:(oc + 1) * TT],
                                   ST1[:, oc * TT:(oc + 1) * TT], AF.Tanh)
            b.s_a.inc(i2)
    blk.scalar(a_t)

    plan2 = mm_set(b, WT2, rh, TT, NC8, NC8, NC8 * 128, (), b.s_v)

    def v_c(vector):
        for oc, ps, n in plan2:
            vector.wait_ge(b.s_mm.h, n)
            ins = vector.tensor_copy(ST2[:, oc * TT:(oc + 1) * TT],
                                     ps[:, 0:TT])
            b.s_v.inc(ins)
    blk.vector(v_c)

    plan3 = mm_set(b, WT3, rx, TT, NC8, NC8, NC8 * 128, (), b.s_v)

    def v_u(vector):
        for oc, ps, n in plan3:
            vector.wait_ge(b.s_mm.h, n)
            ins = vector.scalar_tensor_tensor(
                GS[:, oc * TT:(oc + 1) * TT], ps[:, 0:TT],
                b.v_gbhf[:, oc:oc + 1], ST2[:, oc * TT:(oc + 1) * TT],
                ALU.add, ALU.add)
            b.s_v.inc(ins)
    blk.vector(v_u)

    def a_sig(scalar, _v=b.s_v.n):
        scalar.wait_ge(b.s_v.h, _v)
        for oc in range(NC8):
            ins = scalar.activation(GS[:, oc * TT:(oc + 1) * TT],
                                    GS[:, oc * TT:(oc + 1) * TT], AF.Sigmoid)
        b.s_a.inc(ins)
    blk.scalar(a_sig)

    def v_cell(vector, _a=b.s_a.n, _ld=b.s_ld.n):
        vector.wait_ge(b.s_a.h, _a)
        vector.wait_ge(b.s_ld.h, _ld)
        vector.drain()
        for oc in range(NC8):
            sl = slice(oc * TT, (oc + 1) * TT)
            vector.tensor_sub(ST1[:, sl], ST1[:, sl], XN[:, sl])
        vector.drain()
        for oc in range(NC8):
            sl = slice(oc * TT, (oc + 1) * TT)
            vector.tensor_mul(ST1[:, sl], GS[:, sl], ST1[:, sl])
        vector.drain()
        for oc in range(NC8):
            sl = slice(oc * TT, (oc + 1) * TT)
            vector.tensor_add(ST1[:, sl], ST1[:, sl], XN[:, sl])
        vector.drain()
        for oc in range(NC8):
            sl = slice(oc * TT, (oc + 1) * TT)
            ins = vector.tensor_add(ST1[:, sl], ST1[:, sl], XLN[:, sl])
        b.s_v.inc(ins)
    blk.vector(v_cell)

    emit_ln(b, lambda kc: ST1[:, kc * TT:(kc + 1) * TT],
            lambda kc: ST2[:, kc * TT:(kc + 1) * TT], TT,
            add_src=lambda kc: XC[:, kc * TT:(kc + 1) * TT])

    def st(sync, _v=b.s_v.n):
        sync.wait_ge(b.s_v.h, _v)
        for kc in range(NC8):
            ins = sync.dma_start(out=xres[kc * 128:(kc + 1) * 128, t0:t0 + TT],
                                 in_=ST2[:, kc * TT:(kc + 1) * TT])
            b.s_st.inc(ins, 16)
    blk.sync(st)



def emit_p6(b, xres, w1T, w2T, h1T, h2T, w3T, S, TTF):
    blk, nc = b.blk, b.nc
    NB = S // TTF
    NSUB = TTF // 512

    with contextlib.ExitStack() as ph:
        XC = ph.enter_context(nc.sbuf_tensor("XC6", [128, NC8 * 512], F32))
        XLN = ph.enter_context(nc.sbuf_tensor("XLN6", [128, NC8 * 512], F32))
        XNB = ph.enter_context(nc.sbuf_tensor("XNB6", [128, NC8 * TTF], BF16))
        U = ph.enter_context(nc.sbuf_tensor("U6", [128, 32 * TTF], BF16))
        HM = ph.enter_context(nc.sbuf_tensor("HM6", [128, 32 * TTF], BF16))
        ZS = ph.enter_context(nc.sbuf_tensor("ZS6", [128, 512], F32))
        OUT4 = XLN
        OLN = XLN
        WB = [ph.enter_context(nc.sbuf_tensor(f"WB6{i}", [128, 32 * 128], BF16))
              for i in range(2)]
        WB2 = [ph.enter_context(nc.sbuf_tensor(f"WBb6{i}", [128, NC8 * 128], BF16))
               for i in range(2)]

        for ib in range(NB):
            t0 = ib * TTF
            for sub in range(NSUB):
                ts = t0 + sub * 512

                def ld(sync, _ts=ts, _vn=b.s_v.n):
                    sync.wait_ge(b.s_v.h, _vn)
                    for kc in range(NC8):
                        ins = sync.dma_start(
                            out=XC[:, kc * 512:(kc + 1) * 512],
                            in_=xres[kc * 128:(kc + 1) * 128, _ts:_ts + 512])
                        b.s_ld.inc(ins, 16)
                blk.sync(ld)
                emit_ln(b, lambda kc: XC[:, kc * 512:(kc + 1) * 512],
                        lambda kc: XLN[:, kc * 512:(kc + 1) * 512], 512,
                        wait_sems=(b.s_ld,),
                        out_bf=lambda kc, _s=sub: XNB[:, kc * TTF + _s * 512:
                                                      kc * TTF + (_s + 1) * 512])

            guards = {}
            base_wb = [b.s_wb[i].n for i in range(2)]
            for oc in range(32):
                pw = oc % 2

                def wld(gpsimd, _oc=oc, _pw=pw, _mmn=b.s_mm.n):
                    if _oc >= 2:
                        gv, ga = guards[_oc - 2]
                        gpsimd.wait_ge(b.s_v.h, gv)
                        gpsimd.wait_ge(b.s_a.h, ga)
                    else:
                        gpsimd.wait_ge(b.s_mm.h, _mmn)
                    i1 = gpsimd.dma_start(
                        out=WB[_pw][:, 0:NC8 * 128].rearrange(
                            "p (kc o) -> p kc o", o=128),
                        in_=w1T[:, _oc * 128:(_oc + 1) * 128].rearrange(
                            "(kc p) o -> p kc o", p=128))
                    b.s_wb[_pw].inc(i1, 16)
                    i2 = gpsimd.dma_start(
                        out=WB2[_pw][:, 0:NC8 * 128].rearrange(
                            "p (kc o) -> p kc o", o=128),
                        in_=w2T[:, _oc * 128:(_oc + 1) * 128].rearrange(
                            "(kc p) o -> p kc o", p=128))
                    b.s_wb[_pw].inc(i2, 16)
                blk.gpsimd(wld)

                for sub in range(NSUB):
                    rhs = lambda kc, _s=sub: XNB[:, kc * TTF + _s * 512:
                                                 kc * TTF + (_s + 1) * 512]
                    plan = []

                    def t_z(tensor, _oc=oc, _pw=pw, _s=sub, _rhs=rhs,
                            _wn=base_wb[pw] + 32 * (oc // 2 + 1),
                            _vn=b.s_v.n, _plan=plan,
                            _a0=b.s_a.n, _v0=b.s_v.n):
                        tensor.wait_ge(b.s_wb[_pw].h, _wn)
                        if _oc == 0 and _s == 0:
                            tensor.wait_ge(b.s_v.h, _vn)
                        ps1, sl1 = b.pool.acquire(tensor)
                        b.pool.expect(sl1, b.s_a, _a0 + 1)
                        ins = mm_acc(b, tensor, WB[_pw], _rhs, 512, 0, NC8,
                                     ps1, 128)
                        b.s_mm.inc(ins)
                        _plan.append((ps1, b.s_mm.n))
                        ps2, sl2 = b.pool.acquire(tensor)
                        b.pool.expect(sl2, b.s_v, _v0 + 1)
                        ins = mm_acc(b, tensor, WB2[_pw], _rhs, 512, 0, NC8,
                                     ps2, 128)
                        b.s_mm.inc(ins)
                        _plan.append((ps2, b.s_mm.n))
                    blk.tensor(t_z)

                    def a_sil(scalar, _oc=oc, _s=sub, _plan=plan,
                              _v0=b.s_v.n):
                        scalar.wait_ge(b.s_v.h, _v0)
                        ps1, n1 = _plan[0]
                        scalar.wait_ge(b.s_mm.h, n1)
                        ins = scalar.activation(
                            ZS[:, 0:512], ps1[:, 0:512],
                            AF.Silu, scale=b.v_w1s[:, _oc:_oc + 1])
                        b.s_a.inc(ins)
                    blk.scalar(a_sil)

                    def v_u(vector, _oc=oc, _s=sub, _plan=plan, _an=b.s_a.n):
                        ps2, n2 = _plan[1]
                        vector.wait_ge(b.s_mm.h, n2)
                        vector.wait_ge(b.s_a.h, _an)
                        ins = vector.scalar_tensor_tensor(
                            U[:, _oc * TTF + _s * 512:_oc * TTF + (_s + 1) * 512],
                            ps2[:, 0:512], b.v_w2s[:, _oc:_oc + 1],
                            ZS[:, 0:512],
                            ALU.mult, ALU.mult)
                        b.s_v.inc(ins)
                    blk.vector(v_u)
                guards[oc] = (b.s_v.n, b.s_a.n)

            _p6_mid(b, U, HM, h1T, b.v_h1s, WB, TTF, NSUB)
            _p6_mid(b, HM, U, h2T, b.v_h2s, WB, TTF, NSUB)

            for sub in range(NSUB):
                ts = t0 + sub * 512
                for grp in range(2):
                    g_hist = {}
                    base_wb = [b.s_wb[i].n for i in range(2)]
                    for oc3 in range(32):
                        pw = oc3 % 2

                        def w3ld(gpsimd, _oc3=oc3, _grp=grp, _pw=pw,
                                 _mmn=b.s_mm.n):
                            if _oc3 >= 2:
                                gpsimd.wait_ge(b.s_g.h, g_hist[_oc3 - 2])
                            else:
                                gpsimd.wait_ge(b.s_mm.h, _mmn)
                            ins = gpsimd.dma_start(
                                out=WB[_pw][:, 0:512],
                                in_=w3T[_oc3 * 128:(_oc3 + 1) * 128,
                                        _grp * 512:(_grp + 1) * 512])
                            b.s_wb[_pw].inc(ins, 16)
                        blk.gpsimd(w3ld)

                        def t_w3(tensor, _oc3=oc3, _pw=pw, _s=sub,
                                 _wn=base_wb[oc3 % 2] + 16 * (oc3 // 2 + 1),
                                 _vn=b.s_v.n, _an=b.s_a.n):
                            tensor.wait_ge(b.s_wb[_pw].h, _wn)
                            if _oc3 == 0:
                                tensor.wait_ge(b.s_v.h, _vn)
                                tensor.wait_ge(b.s_a.h, _an)
                            for oc4 in range(4):
                                ins = tensor.matmul(
                                    b.PS[oc4][:, 0:512],
                                    WB[_pw][:, oc4 * 128:(oc4 + 1) * 128],
                                    U[:, _oc3 * TTF + _s * 512:
                                      _oc3 * TTF + (_s + 1) * 512],
                                    start=(_oc3 == 0), stop=(_oc3 == 31))
                            if _oc3 == 31:
                                b.s_mm.inc(ins)
                            else:
                                b.s_g.inc(ins)
                        blk.tensor(t_w3)
                        g_hist[oc3] = b.s_g.n

                    def v_ev(vector, _grp=grp, _n=b.s_mm.n):
                        vector.wait_ge(b.s_mm.h, _n)
                        for oc4 in range(4):
                            oc = _grp * 4 + oc4
                            ins = vector.tensor_scalar_mul(
                                OUT4[:, oc * 512:(oc + 1) * 512],
                                b.PS[oc4][:, 0:512], b.v_w3s[:, oc:oc + 1])
                        b.s_v.inc(ins)
                    blk.vector(v_ev)

                def ld2(sync, _ts=ts, _vn=b.s_v.n):
                    sync.wait_ge(b.s_v.h, _vn)
                    for kc in range(NC8):
                        ins = sync.dma_start(
                            out=XC[:, kc * 512:(kc + 1) * 512],
                            in_=xres[kc * 128:(kc + 1) * 128, _ts:_ts + 512])
                        b.s_ld.inc(ins, 16)
                blk.sync(ld2)
                emit_ln(b, lambda kc: OUT4[:, kc * 512:(kc + 1) * 512],
                        lambda kc: OLN[:, kc * 512:(kc + 1) * 512], 512,
                        wait_sems=(b.s_ld,),
                        add_src=lambda kc: XC[:, kc * 512:(kc + 1) * 512])

                def st2(sync, _ts=ts, _v=b.s_v.n):
                    sync.wait_ge(b.s_v.h, _v)
                    for kc in range(NC8):
                        ins = sync.dma_start(
                            out=xres[kc * 128:(kc + 1) * 128, _ts:_ts + 512],
                            in_=OLN[:, kc * 512:(kc + 1) * 512])
                        b.s_st.inc(ins, 16)
                blk.sync(st2)


def _p6_mid(b, SRC, DST, WT_h, scale_vec, WB, TTF, NSUB):
    blk = b.blk
    guards = {}
    base_wb = [b.s_wb[i].n for i in range(2)]
    for oc in range(32):
        pw = oc % 2

        def wld(gpsimd, _oc=oc, _pw=pw, _mmn=b.s_mm.n):
            if _oc >= 2:
                gpsimd.wait_ge(b.s_a.h, guards[_oc - 2])
            else:
                gpsimd.wait_ge(b.s_mm.h, _mmn)
            ins = gpsimd.dma_start(
                out=WB[_pw][:, :].rearrange("p (kc o) -> p kc o", o=128),
                in_=WT_h[:, _oc * 128:(_oc + 1) * 128].rearrange(
                    "(kc p) o -> p kc o", p=128))
            b.s_wb[_pw].inc(ins, 16)
        blk.gpsimd(wld)

        for sub in range(NSUB):
            plan = []

            def t_mm(tensor, _oc=oc, _pw=pw, _s=sub,
                     _wn=base_wb[pw] + 16 * (oc // 2 + 1),
                     _vn=b.s_v.n, _an=b.s_a.n, _plan=plan):
                tensor.wait_ge(b.s_wb[_pw].h, _wn)
                if _oc == 0 and _s == 0:
                    tensor.wait_ge(b.s_v.h, _vn)
                    tensor.wait_ge(b.s_a.h, _an)
                ps, sl = b.pool.acquire(tensor)
                b.pool.expect(sl, b.s_a, _an + 1)
                for kc in range(32):
                    ins = tensor.matmul(
                        ps[:, 0:512], WB[_pw][:, kc * 128:(kc + 1) * 128],
                        SRC[:, kc * TTF + _s * 512:kc * TTF + (_s + 1) * 512],
                        start=(kc == 0), stop=(kc == 31))
                b.s_mm.inc(ins)
                _plan.append((ps, b.s_mm.n))
            blk.tensor(t_mm)

            def a_g(scalar, _oc=oc, _s=sub, _plan=plan):
                ps, n = _plan[0]
                scalar.wait_ge(b.s_mm.h, n)
                ins = scalar.activation(
                    DST[:, _oc * TTF + _s * 512:_oc * TTF + (_s + 1) * 512],
                    ps[:, 0:512], AF.Gelu, scale=scale_vec[:, _oc:_oc + 1])
                b.s_a.inc(ins)
            blk.scalar(a_g)
        guards[oc] = b.s_a.n



def emit_p7(b, xres, WqkvT, WoT, bqkv, qT_h, kT_h, v_h, zT_h, amask,
            S, TT, QTILE):
    blk, nc = b.blk, b.nc
    NT = S // TT
    NKT = S // 128
    NQT = S // QTILE
    KPQ = QTILE // 128

    with contextlib.ExitStack() as ph:
        XC = ph.enter_context(nc.sbuf_tensor("XC7", [128, NC8 * 512], F32))
        b.row_q = ph.enter_context(nc.sbuf_tensor("row_q", [1, D], F32))
        XLN = ph.enter_context(nc.sbuf_tensor("XLN7", [128, NC8 * 512], F32))
        pha = ph.enter_context(contextlib.ExitStack())
        XNB = pha.enter_context(nc.sbuf_tensor("XNB7", [128, NC8 * TT], BF16))
        STB = pha.enter_context(nc.sbuf_tensor("STB7", [128, NC8 * TT], BF16))
        VS = pha.enter_context(nc.sbuf_tensor("VS7", [128, D], F32))
        WQ = pha.enter_context(nc.sbuf_tensor("WQ7", [128, NC8 * 3 * D], BF16))
        BIASV = pha.enter_context(nc.sbuf_tensor("BIASV7", [128, D], F32))
        MASK = pha.enter_context(nc.sbuf_tensor("MASK7", [128, KPQ * QTILE], F32))

        load_weight_1024(b, WQ, WqkvT, 3 * D)

        def mload(sync):
            ins0 = sync.dma_start(out=b.row_q[0:1, :], in_=bqkv[2 * D:3 * D])
            b.s_ld.inc(ins0, 16)
            ins = sync.dma_start(
                out=MASK[:, :].rearrange("p (a q) -> p a q", q=QTILE),
                in_=amask.rearrange("(a p) q -> p a q", p=128))
            b.s_ld.inc(ins, 16)
        blk.sync(mload)

        def t_bv(tensor, _ld=b.s_ld.n):
            tensor.wait_ge(b.s_ld.h, _ld)
            bcast(b, tensor, b.row_q[0:1, 0:512], b.PSB, 512)
            ins = bcast(b, tensor, b.row_q[0:1, 512:1024], b.PSB2, 512)
            b.s_mm.inc(ins)
        blk.tensor(t_bv)

        def v_bv(vector, _mm=b.s_mm.n):
            vector.wait_ge(b.s_mm.h, _mm)
            vector.tensor_copy(BIASV[:, 0:512], b.PSB[:, 0:512])
            ins = vector.tensor_copy(BIASV[:, 512:1024], b.PSB2[:, 0:512])
            b.s_v.inc(ins)
        blk.vector(v_bv)

        for it in range(NT):
            t0 = it * TT
            for sub in range(TT // 512):
                ts = t0 + sub * 512

                def ld(sync, _ts=ts, _vn=b.s_v.n):
                    sync.wait_ge(b.s_v.h, _vn)
                    for kc in range(NC8):
                        ins = sync.dma_start(
                            out=XC[:, kc * 512:(kc + 1) * 512],
                            in_=xres[kc * 128:(kc + 1) * 128, _ts:_ts + 512])
                        b.s_ld.inc(ins, 16)
                blk.sync(ld)

                emit_ln(b, lambda kc: XC[:, kc * 512:(kc + 1) * 512],
                        lambda kc: XLN[:, kc * 512:(kc + 1) * 512], 512,
                        wait_sems=(b.s_ld,))
                emit_stats(b, lambda kc: XLN[:, kc * 512:(kc + 1) * 512],
                           NC8, 512, [(0, "sumsq")])

                def a_rs(scalar, _mm=b.s_mm.n):
                    scalar.wait_ge(b.s_mm.h, _mm)
                    ins = scalar.activation(b.row_b[0:1, 0:512],
                                            b.PSST[0:1, 0:512], AF.Sqrt,
                                            bias=RMS_EPS, scale=1.0 / D)
                    b.s_a.inc(ins)
                blk.scalar(a_rs)

                def v_rec3(vector, _a=b.s_a.n):
                    vector.wait_ge(b.s_a.h, _a)
                    vector.drain()
                    ins = vector.reciprocal(b.row_b[0:1, 0:512],
                                            b.row_b[0:1, 0:512])
                    b.s_v.inc(ins)
                blk.vector(v_rec3)

                def t_bc(tensor, _a=b.s_v.n):
                    tensor.wait_ge(b.s_v.h, _a)
                    ins = bcast(b, tensor, b.row_b[0:1, 0:512], b.PSB, 512)
                    b.s_mm.inc(ins)
                blk.tensor(t_bc)

                def v_n(vector, _s=sub, _mm=b.s_mm.n):
                    vector.wait_ge(b.s_mm.h, _mm)
                    for kc in range(NC8):
                        ins = vector.tensor_mul(
                            XNB[:, kc * TT + _s * 512:kc * TT + (_s + 1) * 512],
                            XLN[:, kc * 512:(kc + 1) * 512], b.PSB[:, 0:512])
                    b.s_v.inc(ins)
                blk.vector(v_n)

            for part, dst_h in ((0, qT_h), (1, kT_h)):
                for sub in range(TT // 512):
                    rhs = lambda kc, _s=sub: XNB[:, kc * TT + _s * 512:
                                                 kc * TT + (_s + 1) * 512]
                    plan = []

                    def t_qk(tensor, _part=part, _s=sub, _rhs=rhs,
                             _wn=b.s_w.n, _vn=b.s_v.n, _plan=plan,
                             _v0=b.s_v.n):
                        tensor.wait_ge(b.s_w.h, _wn)
                        tensor.wait_ge(b.s_v.h, _vn)
                        for oc in range(NC8):
                            ps, sl = b.pool.acquire(tensor)
                            b.pool.expect(sl, b.s_v, _v0 + oc + 1)
                            ins = mm_acc(b, tensor, WQ, _rhs, 512,
                                         _part * 8 + oc, NC8, ps, 3 * D)
                            b.s_mm.inc(ins)
                            _plan.append((oc, ps, b.s_mm.n))
                    blk.tensor(t_qk)

                    def v_qk(vector, _part=part, _s=sub, _plan=plan,
                             _stn=b.s_st.n):
                        vector.wait_ge(b.s_st.h, _stn)
                        for oc, ps, n in _plan:
                            vector.wait_ge(b.s_mm.h, n)
                            ins = vector.tensor_scalar_add(
                                STB[:, oc * TT + _s * 512:oc * TT + (_s + 1) * 512],
                                ps[:, 0:512],
                                b.v_bqkv[:, _part * 8 + oc:_part * 8 + oc + 1])
                            b.s_v.inc(ins)
                    blk.vector(v_qk)

                def stqk(sync, _dst=dst_h, _t0=t0, _v=b.s_v.n):
                    sync.wait_ge(b.s_v.h, _v)
                    for oc in range(NC8):
                        ins = sync.dma_start(
                            out=_dst[oc * 128:(oc + 1) * 128, _t0:_t0 + TT],
                            in_=STB[:, oc * TT:(oc + 1) * TT])
                        b.s_st.inc(ins, 16)
                blk.sync(stqk)

            for tsub in range(TT // 128):
                tks = t0 + tsub * 128
                lsub = tsub * 128 // 512
                loff = (tsub * 128) % 512
                plan = []

                def t_v(tensor, _lsub=lsub, _loff=loff, _plan=plan,
                        _wn=b.s_w.n, _vn=b.s_v.n, _v0=b.s_v.n):
                    tensor.wait_ge(b.s_w.h, _wn)
                    tensor.wait_ge(b.s_v.h, _vn)
                    for half in range(2):
                        ps, sl = b.pool.acquire(tensor)
                        b.pool.expect(sl, b.s_v, _v0 + half + 1)
                        for kc in range(NC8):
                            lhs = XNB[:, kc * TT + _lsub * 512 + _loff:
                                      kc * TT + _lsub * 512 + _loff + 128]
                            rhscol = kc * 3 * D + 2 * D + half * 512
                            ins = tensor.matmul(
                                ps[:, 0:512], lhs,
                                WQ[:, rhscol:rhscol + 512],
                                start=(kc == 0), stop=(kc == NC8 - 1))
                        b.s_mm.inc(ins)
                        _plan.append((half, ps, b.s_mm.n))
                blk.tensor(t_v)

                def v_v(vector, _plan=plan, _stn=b.s_st.n):
                    vector.wait_ge(b.s_st.h, _stn)
                    for half, ps, n in _plan:
                        vector.wait_ge(b.s_mm.h, n)
                        ins = vector.tensor_add(
                            VS[:, half * 512:(half + 1) * 512], ps[:, 0:512],
                            BIASV[:, half * 512:(half + 1) * 512])
                        b.s_v.inc(ins)
                blk.vector(v_v)

                def st_v(sync, _tks=tks, _v=b.s_v.n):
                    sync.wait_ge(b.s_v.h, _v)
                    ins = sync.dma_start(
                        out=v_h[:, _tks:_tks + 128, :].rearrange("h t dh -> t h dh"),
                        in_=VS[:, :].rearrange("p (h dh) -> p h dh", dh=DHD))
                    b.s_st.inc(ins, 16)
                blk.sync(st_v)

        QT = pha.enter_context(nc.sbuf_tensor("QT7", [64, S], BF16))
        KT = pha.enter_context(nc.sbuf_tensor("KT7", [64, S], BF16))
        VT = pha.enter_context(nc.sbuf_tensor("VT7", [128, NKT * DHD], F32))
        VNB = pha.enter_context(nc.sbuf_tensor("VNB7", [128, NKT * DHD], BF16))
        MC = pha.enter_context(nc.sbuf_tensor("MC7", [128, NKT], F32))
        ES = pha.enter_context(nc.sbuf_tensor("ES7", [128, QTILE], BF16))
        SD = pha.enter_context(nc.sbuf_tensor("SD7", [128, QTILE], F32))
        CL = pha.enter_context(nc.sbuf_tensor("CL7", [128, 8], F32))
        ZB = pha.enter_context(nc.sbuf_tensor("ZB7", [64, S], BF16))

        for h in range(NH):
            def ldh(sync, _h=h, _vn=b.s_v.n, _an=b.s_a.n, _mmn=b.s_mm.n,
                    _stn=b.s_st.n):
                sync.wait_ge(b.s_v.h, _vn)
                sync.wait_ge(b.s_a.h, _an)
                sync.wait_ge(b.s_mm.h, _mmn)
                sync.wait_ge(b.s_st.h, _stn)
                i1 = sync.dma_start(out=QT[:, :],
                                    in_=qT_h[_h * 64:(_h + 1) * 64, :])
                b.s_ld.inc(i1, 16)
                i2 = sync.dma_start(out=KT[:, :],
                                    in_=kT_h[_h * 64:(_h + 1) * 64, :])
                b.s_ld.inc(i2, 16)
                i3 = sync.dma_start(
                    out=VT[:, :].rearrange("p (kt dh) -> p kt dh", dh=DHD),
                    in_=v_h[_h, :, :].rearrange("(kt p) dh -> p kt dh", p=128))
                b.s_ld.inc(i3, 16)
            blk.sync(ldh)

            for kt in range(NKT):
                qt0 = kt // KPQ
                al = kt % KPQ

                def v_init(vector, _ld=b.s_ld.n):
                    vector.wait_ge(b.s_ld.h, _ld)
                    vector.drain()
                    vector.memset(CL[:, 0:1], -1e30)
                    vector.memset(CL[:, 1:2], 0.0)
                    vector.drain()
                blk.vector(v_init)

                for qt in range(qt0, NQT):
                    diag = (qt == qt0)
                    plan = []

                    def t_s(tensor, _kt=kt, _qt=qt, _plan=plan, _diag=diag,
                            _first=(kt == 0 and qt == qt0), _ldn=b.s_ld.n,
                            _v0=b.s_v.n, _a0=b.s_a.n):
                        if _first:
                            tensor.wait_ge(b.s_ld.h, _ldn)
                        ps, sl = b.pool.acquire(tensor)
                        if _diag:
                            b.pool.expect(sl, b.s_v, _v0 + 1)
                        else:
                            b.pool.expect(sl, b.s_a, _a0 + 2)
                        ins = tensor.matmul(
                            ps[:, 0:QTILE],
                            KT[:, _kt * 128:(_kt + 1) * 128],
                            QT[:, _qt * QTILE:(_qt + 1) * QTILE],
                            start=True, stop=True)
                        b.s_mm.inc(ins)
                        _plan.append((ps, b.s_mm.n))
                    blk.tensor(t_s)

                    def v_flash(vector, _diag=diag, _al=al, _plan=plan):
                        ps, n = _plan[0]
                        vector.wait_ge(b.s_mm.h, n)
                        vector.drain()
                        if _diag:
                            ins = vector.tensor_add(
                                SD[:, :], ps[:, 0:QTILE],
                                MASK[:, _al * QTILE:(_al + 1) * QTILE])
                            b.s_v.inc(ins)
                            src = SD[:, :]
                            vector.drain()
                        else:
                            src = ps[:, 0:QTILE]
                        vector.reduce_max(CL[:, 2:3], src, axis=mybir.AxisListType.X)
                        vector.drain()
                        vector.tensor_max(CL[:, 3:4], CL[:, 0:1], CL[:, 2:3])
                        vector.drain()
                        vector.tensor_sub(CL[:, 4:5], CL[:, 0:1], CL[:, 3:4])
                        ins = vector.tensor_scalar_mul(CL[:, 7:8], CL[:, 3:4], -1.0)
                        b.s_v.inc(ins)
                    blk.vector(v_flash)

                    def a_flash(scalar, _diag=diag, _plan=plan, _v=b.s_v.n):
                        scalar.activation(b.dum[0:1, 0:1], b.cst[0:1, 0:1],
                                          AF.Exp)
                        scalar.drain()
                        scalar.wait_ge(b.s_v.h, _v)
                        ins = scalar.activation(CL[:, 5:6], CL[:, 4:5], AF.Exp)
                        b.s_a.inc(ins)
                        ps, n = _plan[0]
                        src = SD[:, :] if _diag else ps[:, 0:QTILE]
                        ins = scalar.activation(ES[:, :], src, AF.Exp,
                                                bias=CL[:, 7:8])
                        b.s_a.inc(ins)
                    blk.scalar(a_flash)

                    def v_merge(vector, _a=b.s_a.n):
                        vector.wait_ge(b.s_a.h, _a)
                        vector.drain()
                        vector.reduce_sum(CL[:, 6:7], ES[:, :], axis=mybir.AxisListType.X)
                        vector.tensor_mul(CL[:, 1:2], CL[:, 1:2], CL[:, 5:6])
                        vector.drain()
                        vector.tensor_add(CL[:, 1:2], CL[:, 1:2], CL[:, 6:7])
                        ins = vector.tensor_copy(CL[:, 0:1], CL[:, 3:4])
                        b.s_v.inc(ins)
                    blk.vector(v_merge)

                def v_fin(vector, _kt=kt, _v=b.s_v.n):
                    vector.wait_ge(b.s_v.h, _v)
                    vector.drain()
                    vector.reciprocal(CL[:, 6:7], CL[:, 1:2])
                    vector.tensor_scalar_mul(MC[:, _kt:_kt + 1], CL[:, 0:1], -1.0)
                    vector.drain()
                    ins = vector.tensor_scalar_mul(
                        VNB[:, _kt * DHD:(_kt + 1) * DHD],
                        VT[:, _kt * DHD:(_kt + 1) * DHD], CL[:, 6:7])
                    b.s_v.inc(ins)
                blk.vector(v_fin)

            for qt in range(NQT):
                nk = (qt + 1) * KPQ
                for kt in range(nk):
                    diag = (kt // KPQ == qt)
                    al = kt % KPQ
                    plan = []

                    def t_s2(tensor, _kt=kt, _qt=qt, _plan=plan, _diag=diag,
                             _v0=b.s_v.n, _a0=b.s_a.n):
                        ps, sl = b.pool.acquire(tensor)
                        if _diag:
                            b.pool.expect(sl, b.s_v, _v0 + 1)
                        else:
                            b.pool.expect(sl, b.s_a, _a0 + 1)
                        ins = tensor.matmul(
                            ps[:, 0:QTILE],
                            KT[:, _kt * 128:(_kt + 1) * 128],
                            QT[:, _qt * QTILE:(_qt + 1) * QTILE],
                            start=True, stop=True)
                        b.s_mm.inc(ins)
                        _plan.append((ps, b.s_mm.n))
                    blk.tensor(t_s2)

                    if diag:
                        def v_m(vector, _al=al, _plan=plan):
                            ps, n = _plan[0]
                            vector.wait_ge(b.s_mm.h, n)
                            ins = vector.tensor_add(
                                SD[:, :], ps[:, 0:QTILE],
                                MASK[:, _al * QTILE:(_al + 1) * QTILE])
                            b.s_v.inc(ins)
                        blk.vector(v_m)

                    def a_e2(scalar, _kt=kt, _diag=diag, _plan=plan,
                             _v=b.s_v.n, _g=b.s_g.n):
                        scalar.activation(b.dum[0:1, 0:1], b.cst[0:1, 0:1],
                                          AF.Exp)
                        scalar.drain()
                        scalar.wait_ge(b.s_g.h, _g)
                        ps, n = _plan[0]
                        if _diag:
                            scalar.wait_ge(b.s_v.h, _v)
                            src = SD[:, :]
                        else:
                            scalar.wait_ge(b.s_mm.h, n)
                            src = ps[:, 0:QTILE]
                        ins = scalar.activation(ES[:, :], src, AF.Exp,
                                                bias=MC[:, _kt:_kt + 1])
                        b.s_a.inc(ins)
                    blk.scalar(a_e2)

                    def t_av(tensor, _kt=kt, _first=(kt == 0),
                             _last=(kt == nk - 1), _a=b.s_a.n, _vn=b.s_v.n):
                        tensor.wait_ge(b.s_a.h, _a)
                        if _first:
                            tensor.wait_ge(b.s_v.h, _vn)
                        ins = tensor.matmul(
                            b.PSS[0:64, 0:QTILE],
                            VNB[:, _kt * DHD:(_kt + 1) * DHD],
                            ES[:, :], start=_first, stop=_last)
                        b.s_g.inc(ins)
                    blk.tensor(t_av)

                def v_z(vector, _qt=qt, _mm=b.s_g.n, _stn=b.s_st.n):
                    vector.wait_ge(b.s_g.h, _mm)
                    vector.wait_ge(b.s_st.h, _stn)
                    ins = vector.tensor_copy(
                        ZB[:, _qt * QTILE:(_qt + 1) * QTILE],
                        b.PSS[0:64, 0:QTILE])
                    b.s_v.inc(ins)
                blk.vector(v_z)

            def st_z(sync, _h=h, _v=b.s_v.n):
                sync.wait_ge(b.s_v.h, _v)
                ins = sync.dma_start(out=zT_h[_h * 64:(_h + 1) * 64, :],
                                     in_=ZB[:, :])
                b.s_st.inc(ins, 16)
            blk.sync(st_z)

        pha.close()

        def gate_wob(gpsimd, _vn=b.s_v.n, _an=b.s_a.n, _gn=b.s_g.n):
            gpsimd.wait_ge(b.s_v.h, _vn)
            gpsimd.wait_ge(b.s_a.h, _an)
            gpsimd.wait_ge(b.s_g.h, _gn)
        blk.gpsimd(gate_wob)
        WOB = ph.enter_context(nc.sbuf_tensor("WOB7", [128, NC8 * D], BF16))
        ZNB = ph.enter_context(nc.sbuf_tensor("ZNB7", [128, NC8 * 512], BF16))
        OP = ph.enter_context(nc.sbuf_tensor("OP7", [128, NC8 * 512], F32))
        OL = ph.enter_context(nc.sbuf_tensor("OL7", [128, NC8 * 512], F32))
        load_weight_1024(b, WOB, WoT, D, sem=b.s_w2)

        for it in range(S // 512):
            t0 = it * 512

            def ldz(gpsimd, _t0=t0, _vn=b.s_v.n, _stn=b.s_st.n):
                gpsimd.wait_ge(b.s_v.h, _vn)
                gpsimd.wait_ge(b.s_st.h, _stn)
                for kc in range(NC8):
                    ins = gpsimd.dma_start(
                        out=ZNB[:, kc * 512:(kc + 1) * 512],
                        in_=zT_h[kc * 128:(kc + 1) * 128, _t0:_t0 + 512])
                    b.s_ldg.inc(ins, 16)
            blk.gpsimd(ldz)

            def ldx(sync, _t0=t0, _vn=b.s_v.n):
                sync.wait_ge(b.s_v.h, _vn)
                for kc in range(NC8):
                    ins = sync.dma_start(
                        out=XC[:, kc * 512:(kc + 1) * 512],
                        in_=xres[kc * 128:(kc + 1) * 128, _t0:_t0 + 512])
                    b.s_ld.inc(ins, 16)
            blk.sync(ldx)

            rhs = lambda kc: ZNB[:, kc * 512:(kc + 1) * 512]
            plan = mm_set(b, WOB, rhs, 512, NC8, NC8, NC8 * 128,
                          (b.s_ldg, b.s_w2), b.s_v)

            def v_o(vector, _plan=plan):
                for oc, ps, n in _plan:
                    vector.wait_ge(b.s_mm.h, n)
                    ins = vector.tensor_scalar_add(
                        OP[:, oc * 512:(oc + 1) * 512], ps[:, 0:512],
                        b.v_bo[:, oc:oc + 1])
                    b.s_v.inc(ins)
            blk.vector(v_o)

            emit_ln(b, lambda kc: OP[:, kc * 512:(kc + 1) * 512],
                    lambda kc: OL[:, kc * 512:(kc + 1) * 512], 512,
                    wait_sems=(b.s_ld,),
                    add_src=lambda kc: XC[:, kc * 512:(kc + 1) * 512])

            def sto(sync, _t0=t0, _v=b.s_v.n):
                sync.wait_ge(b.s_v.h, _v)
                for kc in range(NC8):
                    ins = sync.dma_start(
                        out=xres[kc * 128:(kc + 1) * 128, _t0:_t0 + 512],
                        in_=OL[:, kc * 512:(kc + 1) * 512])
                    b.s_st.inc(ins, 16)
            blk.sync(sto)



def p8_tile(b, xres, y_out, XC, XLN, XNB, STB, ST1, ST2, WT1, WT2, WT3,
            t0, TT):
    blk = b.blk

    def ld(sync, _vn=b.s_v.n, _an=b.s_a.n):
        sync.wait_ge(b.s_v.h, _vn)
        sync.wait_ge(b.s_a.h, _an)
        for kc in range(NC8):
            ins = sync.dma_start(out=XC[:, kc * TT:(kc + 1) * TT],
                                 in_=xres[kc * 128:(kc + 1) * 128, t0:t0 + TT])
            b.s_ld.inc(ins, 16)
    blk.sync(ld)

    emit_ln(b, lambda kc: XC[:, kc * TT:(kc + 1) * TT],
            lambda kc: XLN[:, kc * TT:(kc + 1) * TT], TT,
            wait_sems=(b.s_ld,),
            out_bf=lambda kc: XNB[:, kc * TT:(kc + 1) * TT])

    for wi, (WT, srcbuf, dstbuf) in enumerate(((WT1, XNB, STB),
                                               (WT2, STB, XNB),
                                               (WT3, XNB, STB))):
        rhs = lambda kc, _s=srcbuf: _s[:, kc * TT:(kc + 1) * TT]
        last = (wi == 2)
        dsem = b.s_a if last else b.s_v
        plan = mm_set(b, WT, rhs, TT, NC8, NC8, NC8 * 128, (b.s_v, b.s_w),
                      dsem)
        if not last:
            def v_c(vector, _dst=dstbuf, _plan=plan):
                for oc, ps, n in _plan:
                    vector.wait_ge(b.s_mm.h, n)
                    ins = vector.tensor_copy(
                        _dst[:, oc * TT:(oc + 1) * TT], ps[:, 0:TT])
                    b.s_v.inc(ins)
            blk.vector(v_c)
        else:
            def a_c(scalar, _plan=plan):
                for oc, ps, n in _plan:
                    scalar.wait_ge(b.s_mm.h, n)
                    ins = scalar.activation(
                        ST1[:, oc * TT:(oc + 1) * TT], ps[:, 0:TT],
                        AF.Tanh, bias=b.v_l1b[:, oc:oc + 1])
                    b.s_a.inc(ins)
            blk.scalar(a_c)

    def w_t(vector, _a=b.s_a.n):
        vector.wait_ge(b.s_a.h, _a)
        ins = vector.engine_nop()
        b.s_v.inc(ins)
    blk.vector(w_t)

    emit_ln(b, lambda kc: ST1[:, kc * TT:(kc + 1) * TT],
            lambda kc: ST2[:, kc * TT:(kc + 1) * TT], TT,
            add_src=lambda kc: XC[:, kc * TT:(kc + 1) * TT])

    def st(sync, _v=b.s_v.n):
        sync.wait_ge(b.s_v.h, _v)
        for kc in range(NC8):
            ins = sync.dma_start(out=y_out[kc * 128:(kc + 1) * 128, t0:t0 + TT],
                                 in_=ST2[:, kc * TT:(kc + 1) * TT])
            b.s_st.inc(ins, 16)
    blk.sync(st)



def make_masks(QTILE):
    NAL = QTILE // 128
    m = np.full((NAL * 128, QTILE), -1e30, np.float32)
    for al in range(NAL):
        for p in range(128):
            k = al * 128 + p
            m[al * 128 + p, k:] = 0.0
    return m


def make_in_map(inputs, bidx, S=4096, QTILE=512):
    T = np.ascontiguousarray
    x = np.asarray(inputs["L0_attn_x"])[bidx]
    Wq = np.asarray(inputs["L1_attn_W_qkv"])
    Wq3 = Wq.reshape(NH, DHD, 3, D)
    Wq_de = np.concatenate([Wq3[:, :, c, :].reshape(NH * DHD, D)
                            for c in range(3)], axis=0)
    bq = np.asarray(inputs["L1_attn_b_qkv"]).reshape(NH, DHD, 3)
    bq_de = np.concatenate([bq[:, :, c].reshape(-1) for c in range(3)])

    m = {
        "x": T(x.T),
        "gamma": np.asarray(inputs["L0_attn_gamma"]),
        "gWiT": T(np.asarray(inputs["L0_attn_gWi"]).T),
        "gWifT": T(np.asarray(inputs["L0_attn_gWif"]).T),
        "gWhfT": T(np.asarray(inputs["L0_attn_gWhf"]).T),
        "gbhf": np.asarray(inputs["L0_attn_gbhf"]),
        "w1T": T(np.asarray(inputs["L0_ffn_w1_w"]).T),
        "w2T": T(np.asarray(inputs["L0_ffn_w2_w"]).T),
        "h1T": T(np.asarray(inputs["L0_ffn_h1_w"]).T),
        "h2T": T(np.asarray(inputs["L0_ffn_h2_w"]).T),
        "w3T": T(np.asarray(inputs["L0_ffn_w3_w"]).T),
        "w1s": np.asarray(inputs["L0_ffn_w1_s"]),
        "w2s": np.asarray(inputs["L0_ffn_w2_s"]),
        "h1s": np.asarray(inputs["L0_ffn_h1_s"]),
        "h2s": np.asarray(inputs["L0_ffn_h2_s"]),
        "w3s": np.asarray(inputs["L0_ffn_w3_s"]),
        "WqkvT": T(Wq_de.T),
        "bqkv": T(bq_de),
        "WoT": T(np.asarray(inputs["L1_attn_W_o"]).T),
        "bo": np.asarray(inputs["L1_attn_b_o"]),
        "l1w": np.asarray(inputs["L1_ffn_weight"]),
        "l1a1": np.asarray(inputs["L1_ffn_aux_weight1"]),
        "l1a2": np.asarray(inputs["L1_ffn_aux_weight2"]),
        "l1bias": np.asarray(inputs["L1_ffn_bias"]),
        "amask": make_masks(QTILE),
    }
    for i in range(2):
        m[f"lWiT{i}"] = T(np.asarray(inputs["L0_attn_lWi"])[i].T)
        m[f"lWifT{i}"] = T(np.asarray(inputs["L0_attn_lWif"])[i].T)
        m[f"lWhfT{i}"] = T(np.asarray(inputs["L0_attn_lWhf"])[i].T)
        m[f"lbhf{i}"] = np.asarray(inputs["L0_attn_lbhf"])[i]
        m[f"linit{i}"] = np.asarray(inputs["L0_attn_l_init_hidden"])[i]
    return {k: np.ascontiguousarray(np.asarray(v, dtype=np.float32))
            for k, v in m.items()}


_CACHED = {}


def kernel(**inputs):
    S = 4096
    if "nc" not in _CACHED:
        _CACHED["nc"] = build_program(S=S)
    nc = _CACHED["nc"]
    in_maps = [make_in_map(inputs, c % 4, S=S) for c in range(8)]
    res = run_bass_kernel_spmd(nc, in_maps, core_ids=list(range(8)))
    out = np.stack([res.results[bidx]["y"].T for bidx in range(4)])
    return out.astype(np.float32)


# version 04
# speedup vs baseline: 1.2560x; 1.0985x over previous
import contextlib

import numpy as np
import ml_dtypes
import concourse.bass as bass
import concourse.mybir as mybir
from concourse.bass_types import DynSlice
from concourse.bass_utils import run_bass_kernel_spmd

F32 = mybir.dt.float32
BF16 = mybir.dt.bfloat16
AF = mybir.ActivationFunctionType
ALU = mybir.AluOpType

D = 1024
NC8 = 8
DH = 4096
NH = 16
DHD = 64
LN_EPS = 1e-5
RMS_EPS = 1.1920929e-07


class Sem:
    def __init__(self, es, nc, name):
        self.h = es.enter_context(nc.semaphore(name))
        self.n = 0

    def inc(self, instr, k=1):
        instr.then_inc(self.h, k)
        self.n += k
        return instr

    def wait(self, eng, val=None):
        eng.wait_ge(self.h, self.n if val is None else val)


class PsumPool:

    def __init__(self, b, tiles, name):
        self.tiles = tiles
        self.pending = [None] * len(tiles)
        self.count = 0

    def acquire(self, tensor):
        slot = self.count % len(self.tiles)
        self.count += 1
        if self.pending[slot] is not None:
            sem, val = self.pending[slot]
            tensor.wait_ge(sem.h, val)
            self.pending[slot] = None
        return self.tiles[slot], slot

    def expect(self, slot, sem, val):
        self.pending[slot] = (sem, val)


class B:
    pass


def wait_reg(b, eng, sem, iv, const, lo, hi):
    from concourse.expressions import RuntimeValue
    from concourse.bass_primitives_rust import RegisterHandles
    et = eng.engine
    reg = b.wregs.get(et)
    if reg is None:
        reg = b.nc.alloc_register(et, f"wreg{len(b.wregs)}")
        b.wregs[et] = reg
    eng.reg_add(reg, iv, const)
    eng.wait_ge(sem.h, reg)


@contextlib.contextmanager
def fori(nc, eng, start, end, step=1):
    from concourse.ordered_engine_set import OrderedEngineSet
    engines = OrderedEngineSet([eng.engine])
    loop_id = nc.next_id()
    loop_start = f"ffori_{loop_id}_loop"
    loop_end = f"ffori_{loop_id}_end"
    regs = nc.alloc_registers(f"ffori_{loop_id}_i", engines=engines)
    nc.regs_mov(regs, start)

    nc.br(loop_start, engines=engines)
    iv = None
    with nc.body(loop_start, valid_engines=engines):
        iv = nc.snap(regs, min_val=start, max_val=end - 1)
        yield iv
        nc.regs_alu(regs, regs, step, op=ALU.add)
        nc.br_lt(regs, end, on_true=loop_start, on_false=loop_end,
                 engines=engines)
    nc.switch_bb(loop_end)
    try:
        for h in iv.val.handles:
            nc.free_register(h)
    except Exception:
        pass
    for h in regs:
        nc.free_register(h)


def barrier(b):
    blk, s = b.blk, b.bar
    dma_waits = [(sem, sem.n) for sem in
                 (b.s_st, b.s_hs, b.s_ld, b.s_ldg, b.s_ac, b.s_w,
                  b.s_ac2[0], b.s_ac2[1], b.s_acb[0], b.s_acb[1],
                  b.s_hs2[0], b.s_hs2[1], b.s_wb[0], b.s_wb[1], b.s_w2)]

    def _inc_sync(eng, _w=dma_waits, _s=s):
        for sem, n in _w:
            eng.wait_ge(sem.h, n)
        eng.drain()
        eng.sem_inc(_s.h, 1)
    blk.sync(_inc_sync)

    def _inc(eng, _s=s):
        eng.drain()
        eng.sem_inc(_s.h, 1)
    for name in ("tensor", "vector", "scalar", "gpsimd"):
        getattr(blk, name)(_inc)
    s.n += 5
    t = s.n
    for name in ("sync", "tensor", "vector", "scalar", "gpsimd"):
        getattr(blk, name)(lambda eng, _s=s, _t=t: eng.wait_ge(_s.h, _t))


def load_weight_1024(b, dst, src, n_out_cols, sem=None):
    s = sem if sem is not None else b.s_w
    def go(gpsimd):
        ins = gpsimd.dma_start(
            out=dst[:, 0:NC8 * n_out_cols].rearrange("p (kc o) -> p kc o",
                                                     o=n_out_cols),
            in_=src.rearrange("(kc p) o -> p kc o", p=128))
        s.inc(ins, 16)
    b.blk.gpsimd(go)


def mm_acc(b, tensor, lhsT_sb, rhs, tt, oc, n_k, ps, stride):
    for kc in range(n_k):
        col = kc * stride + oc * 128
        ins = tensor.matmul(ps[:, 0:tt], lhsT_sb[:, col:col + 128], rhs(kc),
                            start=(kc == 0), stop=(kc == n_k - 1))
    return ins


def emit_stats(b, src, n_chunks, tt, rows, wait_sems=()):
    blk = b.blk
    waits = list(wait_sems) + [b.s_v]
    for row, kind in rows:
        if kind == "sumsq":
            wl = [(w, w.n) for w in waits]

            def v_sq(vector, _wl=wl, _tt=tt, _n=n_chunks, _src=src):
                for w, n in _wl:
                    w.wait(vector, n)
                for kc in range(_n):
                    ins = vector.tensor_mul(b.SQ[:, kc * _tt:(kc + 1) * _tt],
                                            _src(kc), _src(kc))
                b.s_sq.inc(ins)
            blk.vector(v_sq)

            def t_sq(tensor, _row=row, _tt=tt, _n=n_chunks, _sq=b.s_sq.n):
                tensor.wait_ge(b.s_sq.h, _sq)
                for kc in range(_n):
                    ins = tensor.matmul(b.PSST[_row:_row + 1, 0:_tt],
                                        b.ones_col[:, 0:1],
                                        b.SQ[:, kc * _tt:(kc + 1) * _tt],
                                        start=(kc == 0), stop=(kc == _n - 1))
                b.s_mm.inc(ins)
            blk.tensor(t_sq)
        else:
            wl = [(w, w.n) for w in waits]

            def t_sum(tensor, _wl=wl, _row=row, _tt=tt, _n=n_chunks, _src=src):
                for w, n in _wl:
                    w.wait(tensor, n)
                for kc in range(_n):
                    ins = tensor.matmul(b.PSST[_row:_row + 1, 0:_tt],
                                        b.ones_col[:, 0:1], _src(kc),
                                        start=(kc == 0), stop=(kc == _n - 1))
                b.s_mm.inc(ins)
            blk.tensor(t_sum)


def bcast(b, tensor, row_ap, ps, tt):
    return tensor.matmul(ps[:, 0:tt], b.ones_row[0:1, :], row_ap,
                         start=True, stop=True)


def emit_ln(b, src, out, tt, n_chunks=NC8, out_bf=None, wait_sems=(),
            add_src=None, eps=LN_EPS):
    blk = b.blk
    emit_stats(b, src, n_chunks, tt, [(0, "sum"), (32, "sumsq")], wait_sems)
    nfeat = float(128 * n_chunks)

    def v_rows(vector, _tt=tt, _mm=b.s_mm.n):
        vector.wait_ge(b.s_mm.h, _mm)
        vector.tensor_scalar_mul(b.row_m[0:1, 0:_tt], b.PSST[0:1, 0:_tt], 1.0 / nfeat)
        vector.tensor_scalar_mul(b.row_a[0:1, 0:_tt], b.PSST[32:33, 0:_tt], 1.0 / nfeat)
        vector.drain()
        vector.tensor_mul(b.row_b[0:1, 0:_tt], b.row_m[0:1, 0:_tt],
                          b.row_m[0:1, 0:_tt])
        vector.drain()
        ins = vector.tensor_sub(b.row_a[0:1, 0:_tt], b.row_a[0:1, 0:_tt],
                                b.row_b[0:1, 0:_tt])
        b.s_v.inc(ins)
    blk.vector(v_rows)

    def a_rstd(scalar, _tt=tt, _v=b.s_v.n):
        scalar.wait_ge(b.s_v.h, _v)
        ins = scalar.activation(b.row_b[0:1, 0:_tt], b.row_a[0:1, 0:_tt],
                                AF.Sqrt, bias=float(eps))
        b.s_a.inc(ins)
    blk.scalar(a_rstd)

    def v_recip(vector, _tt=tt, _a=b.s_a.n):
        vector.wait_ge(b.s_a.h, _a)
        vector.drain()
        ins = vector.reciprocal(b.row_b[0:1, 0:_tt], b.row_b[0:1, 0:_tt])
        b.s_v.inc(ins)
    blk.vector(v_recip)

    def t_bc(tensor, _tt=tt, _a=b.s_v.n):
        tensor.wait_ge(b.s_v.h, _a)
        bcast(b, tensor, b.row_m[0:1, 0:_tt], b.PSB, _tt)
        ins = bcast(b, tensor, b.row_b[0:1, 0:_tt], b.PSB2, _tt)
        b.s_mm.inc(ins)
    blk.tensor(t_bc)

    def v_norm(vector, _tt=tt, _n=n_chunks, _mm=b.s_mm.n):
        vector.wait_ge(b.s_mm.h, _mm)
        for kc in range(_n):
            vector.tensor_sub(out(kc), src(kc), b.PSB[:, 0:_tt])
        vector.drain()
        for kc in range(_n):
            ins = vector.tensor_mul(out(kc), out(kc), b.PSB2[:, 0:_tt])
        if add_src is not None:
            vector.drain()
            for kc in range(_n):
                ins = vector.tensor_add(out(kc), out(kc), add_src(kc))
        if out_bf is not None:
            vector.drain()
            for kc in range(_n):
                ins = vector.tensor_copy(out_bf(kc), out(kc))
        b.s_v.inc(ins)
    blk.vector(v_norm)


def mm_set(b, WT, rhs, TT, n_out, n_k, stride, pre_waits, data_sem,
           incs_per_oc=1):
    plan = []
    base = data_sem.n

    def t_go(tensor, _pw=[(w, w.n) for w in pre_waits]):
        for w, n in _pw:
            w.wait(tensor, n)
        for oc in range(n_out):
            ps, slot = b.pool.acquire(tensor)
            b.pool.expect(slot, data_sem, base + incs_per_oc * oc + 1)
            ins = mm_acc(b, tensor, WT, rhs, TT, oc, n_k, ps, stride)
            b.s_mm.inc(ins)
            plan.append((oc, ps, b.s_mm.n))
    b.blk.tensor(t_go)
    return plan



def build_program(S=4096, CH=1024, TT=512, TTF=512, QTILE=512, phases=99,
                  debug=False, scan_stub=False, scan_no_hist=False,
                  scan_no_mm=False):
    nc = bass.Bass("TRN2", target_bir_lowering=False)
    es = contextlib.ExitStack()
    nc._keepalive_es = es
    b = B()
    b.nc, b.es, b.S = nc, es, S
    b.wregs = {}
    b.scan_no_hist, b.scan_no_mm = scan_no_hist, scan_no_mm
    NT = S // TT
    NAL = QTILE // 128
    dk = "ExternalOutput" if debug else "Internal"

    def din(name, shape, dt=F32):
        return nc.dram_tensor(name, shape, dt, kind="ExternalInput").ap()

    def dint(name, shape, dt=F32):
        return nc.dram_tensor(name, shape, dt, kind=dk).ap()

    x_in = din("x", [D, S])
    gamma = din("gamma", [D])
    lWiT = [din(f"lWiT{i}", [D, D]) for i in range(2)]
    lWifT = [din(f"lWifT{i}", [D, D]) for i in range(2)]
    lWhfT = [din(f"lWhfT{i}", [D, D]) for i in range(2)]
    lbhf = [din(f"lbhf{i}", [D]) for i in range(2)]
    linit = [din(f"linit{i}", [D]) for i in range(2)]
    gWiT = din("gWiT", [D, D]); gWifT = din("gWifT", [D, D])
    gWhfT = din("gWhfT", [D, D]); gbhf = din("gbhf", [D])
    w1T = din("w1T", [D, DH]); w2T = din("w2T", [D, DH])
    h1T = din("h1T", [DH, DH]); h2T = din("h2T", [DH, DH])
    w3T = din("w3T", [DH, D])
    w1s = din("w1s", [DH]); w2s = din("w2s", [DH])
    h1s = din("h1s", [DH]); h2s = din("h2s", [DH])
    w3s = din("w3s", [D])
    WqkvT = din("WqkvT", [D, 3 * D])
    bqkv = din("bqkv", [3 * D])
    WoT = din("WoT", [D, D])
    bo = din("bo", [D])
    l1w = din("l1w", [D, D]); l1a1 = din("l1a1", [D, D]); l1a2 = din("l1a2", [D, D])
    l1bias = din("l1bias", [D])
    amask = din("amask", [NAL * 128, QTILE])

    y_out = nc.dram_tensor("y", [D, S], F32, kind="ExternalOutput").ap()

    xres = dint("xres", [D, S])
    xn_h = dint("xn_h", [D, S])
    xln_h = dint("xln_h", [D, S])
    A_h = [dint(f"A_h{i}", [D, S]) for i in range(2)]
    C_h = [dint(f"C_h{i}", [D, S]) for i in range(2)]
    hs_h = [dint(f"hs_h{i}", [D, S], BF16) for i in range(2)]
    qT_h = dint("qT_h", [D, S], BF16)
    kT_h = dint("kT_h", [D, S], BF16)
    v_h = dint("v_h", [NH, S, DHD])
    zT_h = dint("zT_h", [D, S], BF16)

    sb = nc.alloc_sbuf_tensor
    b.ones_col = sb("ones_col", [128, 1], F32)
    b.ones_row = sb("ones_row", [1, 128], F32)
    b.row_m = sb("row_m", [1, 512], F32)
    b.row_a = sb("row_a", [1, 512], F32)
    b.row_b = sb("row_b", [1, 512], F32)
    b.g1 = sb("g1", [128, 8], F32)
    b.v_lbhf = sb("v_lbhf", [128, 16], F32)
    b.v_linit = sb("v_linit", [128, 16], F32)
    b.v_gbhf = sb("v_gbhf", [128, 8], F32)
    b.v_w1s = sb("v_w1s", [128, 32], F32)
    b.v_w2s = sb("v_w2s", [128, 32], F32)
    b.v_h1s = sb("v_h1s", [128, 32], F32)
    b.v_h2s = sb("v_h2s", [128, 32], F32)
    b.v_w3s = sb("v_w3s", [128, 8], F32)
    b.v_bqkv = sb("v_bqkv", [128, 24], F32)
    b.v_bo = sb("v_bo", [128, 8], F32)
    b.v_l1b = sb("v_l1b", [128, 8], F32)
    b.SQ = sb("SQ", [128, NC8 * 512], F32)

    b.cst = sb("cst", [1, 8], F32)
    b.dum = sb("dum", [1, 8], F32)
    b.cvals = {float(LN_EPS): 0, float(RMS_EPS): 1, float(1.0 / D): 2}
    for val, idx in b.cvals.items():
        nc.const_aps.register(F32, val, b.cst[0:1, idx:idx + 1])

    b.PS = [nc.alloc_psum_tensor(f"PS{i}", [128, 512], F32) for i in range(4)]
    b.PSST = nc.alloc_psum_tensor("PSST", [128, 512], F32)
    b.PSB = nc.alloc_psum_tensor("PSB", [128, 512], F32)
    b.PSB2 = nc.alloc_psum_tensor("PSB2", [128, 512], F32)
    b.PSS = nc.alloc_psum_tensor("PSS", [128, 512], F32)

    for nm in ("bar", "s_ld", "s_ldg", "s_st", "s_w", "s_mm", "s_sq", "s_v",
               "s_a", "s_g", "s_ac", "s_hs", "s_smm", "s_su", "s_sg", "s_sh"):
        setattr(b, nm, Sem(es, nc, nm))
    b.s_ac2 = [Sem(es, nc, f"s_ac2_{i}") for i in range(2)]
    b.s_acb = [Sem(es, nc, f"s_acb_{i}") for i in range(2)]
    b.s_hs2 = [Sem(es, nc, f"s_hs2_{i}") for i in range(2)]
    b.s_wb = [Sem(es, nc, f"s_wb_{i}") for i in range(2)]
    b.s_w2 = Sem(es, nc, "s_w2")

    with nc.Block() as blk:
        b.blk = blk
        b.pool = PsumPool(b, b.PS, "psf")

        def p0_sync(sync):
            pairs = [(b.g1[:, 0:8], gamma)]
            for i in range(2):
                pairs.append((b.v_lbhf[:, 8 * i:8 * i + 8], lbhf[i]))
                pairs.append((b.v_linit[:, 8 * i:8 * i + 8], linit[i]))
            pairs += [(b.v_gbhf[:, 0:8], gbhf), (b.v_w3s[:, 0:8], w3s),
                      (b.v_bo[:, 0:8], bo), (b.v_l1b[:, 0:8], l1bias),
                      (b.v_w1s[:, 0:32], w1s), (b.v_w2s[:, 0:32], w2s),
                      (b.v_h1s[:, 0:32], h1s), (b.v_h2s[:, 0:32], h2s),
                      (b.v_bqkv[:, 0:24], bqkv)]
            with nc.allow_non_contiguous_dma(reason="tiny per-feature vectors"):
                for dstap, src in pairs:
                    ins = sync.dma_start(out=dstap,
                                         in_=src.rearrange("(kc p) -> p kc", p=128))
                    b.s_ld.inc(ins, 16)
            ins = sync.dma_start(out=xres[:, :], in_=x_in[:, :])
            b.s_ld.inc(ins, 16)
        blk.sync(p0_sync)

        def p0_v(vector):
            b.s_ld.wait(vector)
            vector.memset(b.ones_col[:, :], 1.0)
            vector.memset(b.ones_row[:, :], 1.0)
            for val, idx in b.cvals.items():
                vector.memset(b.cst[0:1, idx:idx + 1], val)
            vector.tensor_scalar_add(b.g1[:, 0:8], b.g1[:, 0:8], 1.0)
        blk.vector(p0_v)
        barrier(b)

        if phases >= 1:
            with contextlib.ExitStack() as ph:
                XC = ph.enter_context(nc.sbuf_tensor("XC1", [128, NC8 * TT], F32))
                XLN = ph.enter_context(nc.sbuf_tensor("XLN1", [128, NC8 * TT], F32))
                XN = ph.enter_context(nc.sbuf_tensor("XN1", [128, NC8 * TT], F32))
                XNB = ph.enter_context(nc.sbuf_tensor("XNB1", [128, NC8 * TT], BF16))
                ST1 = ph.enter_context(nc.sbuf_tensor("STa1", [128, NC8 * TT], F32))
                ST2 = ph.enter_context(nc.sbuf_tensor("STb1", [128, NC8 * TT], F32))
                WT1 = ph.enter_context(nc.sbuf_tensor("WT1a", [128, NC8 * D], BF16))
                WT2 = ph.enter_context(nc.sbuf_tensor("WT1b", [128, NC8 * D], BF16))
                load_weight_1024(b, WT1, lWiT[0], D)
                load_weight_1024(b, WT2, lWifT[0], D)
                for it in range(NT):
                    p1_tile(b, x_in, xln_h, xn_h, A_h[0], C_h[0],
                            XC, XLN, XN, XNB, ST1, ST2, WT1, WT2, it * TT, TT)
            barrier(b)

        if phases >= 2:
            if scan_stub:
                def stub2(sync, _st=b.s_st.n):
                    sync.wait_ge(b.s_st.h, _st)
                    ins = sync.dma_start(out=hs_h[0][:, :], in_=A_h[0][:, :])
                    b.s_st.inc(ins, 16)
                b.blk.sync(stub2)
            else:
                emit_scan(b, lWhfT[0], A_h[0], C_h[0], hs_h[0], 0, S, CH)
            barrier(b)

        if phases >= 3:
            with contextlib.ExitStack() as ph:
                XNB = ph.enter_context(nc.sbuf_tensor("XNB3", [128, NC8 * TT], BF16))
                ST1 = ph.enter_context(nc.sbuf_tensor("STa3", [128, NC8 * TT], F32))
                ST2 = ph.enter_context(nc.sbuf_tensor("STb3", [128, NC8 * TT], F32))
                WT1 = ph.enter_context(nc.sbuf_tensor("WT3a", [128, NC8 * D], BF16))
                WT2 = ph.enter_context(nc.sbuf_tensor("WT3b", [128, NC8 * D], BF16))
                load_weight_1024(b, WT1, lWiT[1], D)
                load_weight_1024(b, WT2, lWifT[1], D)
                for it in range(NT):
                    t0 = it * TT

                    def p3_ld(gpsimd, _t0=t0, _vn=b.s_v.n, _an=b.s_a.n):
                        gpsimd.wait_ge(b.s_v.h, _vn)
                        gpsimd.wait_ge(b.s_a.h, _an)
                        for kc in range(NC8):
                            ins = gpsimd.dma_start(
                                out=XNB[:, kc * TT:(kc + 1) * TT],
                                in_=hs_h[0][kc * 128:(kc + 1) * 128, _t0:_t0 + TT])
                            b.s_ldg.inc(ins, 16)
                    blk.gpsimd(p3_ld)
                    mixer_pre(b, XNB, ST1, ST2, WT1, WT2, A_h[1], C_h[1], 1,
                              t0, TT, in_sems=(b.s_ldg, b.s_w))
            barrier(b)

        if phases >= 4:
            emit_scan(b, lWhfT[1], A_h[1], C_h[1], hs_h[1], 1, S, CH)
            barrier(b)

        if phases >= 5:
            with contextlib.ExitStack() as ph:
                XC = ph.enter_context(nc.sbuf_tensor("XC5", [128, NC8 * TT], F32))
                XLN = ph.enter_context(nc.sbuf_tensor("XLN5", [128, NC8 * TT], F32))
                XN = ph.enter_context(nc.sbuf_tensor("XN5", [128, NC8 * TT], F32))
                XNB = ph.enter_context(nc.sbuf_tensor("XNB5", [128, NC8 * TT], BF16))
                H1B = ph.enter_context(nc.sbuf_tensor("H1B5", [128, NC8 * TT], BF16))
                ST1 = ph.enter_context(nc.sbuf_tensor("STa5", [128, NC8 * TT], F32))
                ST2 = ph.enter_context(nc.sbuf_tensor("STb5", [128, NC8 * TT], F32))
                GS = ph.enter_context(nc.sbuf_tensor("GS5", [128, NC8 * TT], F32))
                WT1 = ph.enter_context(nc.sbuf_tensor("WT5a", [128, NC8 * D], BF16))
                WT2 = ph.enter_context(nc.sbuf_tensor("WT5b", [128, NC8 * D], BF16))
                WT3 = ph.enter_context(nc.sbuf_tensor("WT5c", [128, NC8 * D], BF16))
                load_weight_1024(b, WT1, gWiT, D)
                load_weight_1024(b, WT2, gWifT, D)
                load_weight_1024(b, WT3, gWhfT, D)
                for it in range(NT):
                    p5_tile(b, hs_h[1], xn_h, xln_h, x_in, xres, XC, XLN, XN,
                            XNB, H1B, ST1, ST2, GS, WT1, WT2, WT3, it * TT, TT)
            barrier(b)

        if phases >= 6:
            emit_p6(b, xres, w1T, w2T, h1T, h2T, w3T, S, TTF)
            barrier(b)

        if phases >= 7:
            emit_p7(b, xres, WqkvT, WoT, bqkv, qT_h, kT_h, v_h, zT_h, amask,
                    S, TT, QTILE)
            barrier(b)

        if phases >= 8:
            with contextlib.ExitStack() as ph:
                XC = ph.enter_context(nc.sbuf_tensor("XC8", [128, NC8 * TT], F32))
                XLN = ph.enter_context(nc.sbuf_tensor("XLN8", [128, NC8 * TT], F32))
                XNB = ph.enter_context(nc.sbuf_tensor("XNB8", [128, NC8 * TT], BF16))
                STB = ph.enter_context(nc.sbuf_tensor("STB8", [128, NC8 * TT], BF16))
                ST1 = ph.enter_context(nc.sbuf_tensor("STa8", [128, NC8 * TT], F32))
                ST2 = ph.enter_context(nc.sbuf_tensor("STb8", [128, NC8 * TT], F32))
                WT1 = ph.enter_context(nc.sbuf_tensor("WT8a", [128, NC8 * D], BF16))
                WT2 = ph.enter_context(nc.sbuf_tensor("WT8b", [128, NC8 * D], BF16))
                WT3 = ph.enter_context(nc.sbuf_tensor("WT8c", [128, NC8 * D], BF16))
                load_weight_1024(b, WT1, l1w, D)
                load_weight_1024(b, WT2, l1a1, D)
                load_weight_1024(b, WT3, l1a2, D)
                for it in range(NT):
                    p8_tile(b, xres, y_out, XC, XLN, XNB, STB, ST1, ST2,
                            WT1, WT2, WT3, it * TT, TT)
        elif phases < 8:
            def cp_out(sync, _vn=b.s_v.n):
                sync.wait_ge(b.s_v.h, _vn)
                ins = sync.dma_start(out=y_out[:, :], in_=xres[:, :])
                b.s_st.inc(ins, 16)
            blk.sync(cp_out)

        def fin(gpsimd, _n=b.s_st.n):
            gpsimd.wait_ge(b.s_st.h, _n)
        blk.gpsimd(fin)

    return nc



def p1_tile(b, x_in, xln_h, xn_h, A_hb, C_hb, XC, XLN, XN, XNB, ST1, ST2,
            WT1, WT2, t0, TT):
    blk = b.blk

    def ld(sync, _vn=b.s_v.n, _an=b.s_a.n):
        sync.wait_ge(b.s_v.h, _vn)
        sync.wait_ge(b.s_a.h, _an)
        for kc in range(NC8):
            ins = sync.dma_start(out=XC[:, kc * TT:(kc + 1) * TT],
                                 in_=x_in[kc * 128:(kc + 1) * 128, t0:t0 + TT])
            b.s_ld.inc(ins, 16)
    blk.sync(ld)

    src = lambda kc: XC[:, kc * TT:(kc + 1) * TT]
    oln = lambda kc: XLN[:, kc * TT:(kc + 1) * TT]
    emit_ln(b, src, oln, TT, wait_sems=(b.s_ld,))
    emit_stats(b, oln, NC8, TT, [(0, "sumsq")])

    def v_mx(vector, _mm=b.s_mm.n):
        vector.wait_ge(b.s_mm.h, _mm)
        ins = vector.tensor_scalar_max(b.row_a[0:1, 0:TT], b.PSST[0:1, 0:TT],
                                       1e-24)
        b.s_v.inc(ins)
    blk.vector(v_mx)

    def a_rs(scalar, _v=b.s_v.n):
        scalar.wait_ge(b.s_v.h, _v)
        ins = scalar.activation(b.row_b[0:1, 0:TT], b.row_a[0:1, 0:TT],
                                AF.Sqrt, scale=1.0 / D)
        b.s_a.inc(ins)
    blk.scalar(a_rs)

    def v_rec2(vector, _a=b.s_a.n):
        vector.wait_ge(b.s_a.h, _a)
        vector.drain()
        ins = vector.reciprocal(b.row_b[0:1, 0:TT], b.row_b[0:1, 0:TT])
        b.s_v.inc(ins)
    blk.vector(v_rec2)

    def t_bc(tensor, _a=b.s_v.n):
        tensor.wait_ge(b.s_v.h, _a)
        ins = bcast(b, tensor, b.row_b[0:1, 0:TT], b.PSB, TT)
        b.s_mm.inc(ins)
    blk.tensor(t_bc)

    def v_xn(vector, _mm=b.s_mm.n):
        vector.wait_ge(b.s_mm.h, _mm)
        for kc in range(NC8):
            vector.tensor_mul(XN[:, kc * TT:(kc + 1) * TT],
                              XLN[:, kc * TT:(kc + 1) * TT], b.PSB[:, 0:TT])
        vector.drain()
        for kc in range(NC8):
            vector.tensor_scalar_mul(XN[:, kc * TT:(kc + 1) * TT],
                                     XN[:, kc * TT:(kc + 1) * TT],
                                     b.g1[:, kc:kc + 1])
        vector.drain()
        for kc in range(NC8):
            ins = vector.tensor_copy(XNB[:, kc * TT:(kc + 1) * TT],
                                     XN[:, kc * TT:(kc + 1) * TT])
        b.s_v.inc(ins)
    blk.vector(v_xn)

    def st(sync, _v=b.s_v.n):
        sync.wait_ge(b.s_v.h, _v)
        for kc in range(NC8):
            i1 = sync.dma_start(out=xln_h[kc * 128:(kc + 1) * 128, t0:t0 + TT],
                                in_=XLN[:, kc * TT:(kc + 1) * TT])
            b.s_st.inc(i1, 16)
            i2 = sync.dma_start(out=xn_h[kc * 128:(kc + 1) * 128, t0:t0 + TT],
                                in_=XN[:, kc * TT:(kc + 1) * TT])
            b.s_st.inc(i2, 16)
    blk.sync(st)

    mixer_pre(b, XNB, ST1, ST2, WT1, WT2, A_hb, C_hb, 0, t0, TT,
              in_sems=(b.s_v, b.s_w))


def mixer_pre(b, SRCB, ST1, ST2, WT1, WT2, A_hb, C_hb, layer, t0, TT,
              in_sems=()):
    blk = b.blk
    rhs = lambda kc: SRCB[:, kc * TT:(kc + 1) * TT]

    plan = mm_set(b, WT1, rhs, TT, NC8, NC8, NC8 * 128, in_sems, b.s_a,
                  incs_per_oc=2)

    def a_t(scalar):
        for oc, ps, n in plan:
            scalar.wait_ge(b.s_mm.h, n)
            i1 = scalar.activation(ST1[:, oc * TT:(oc + 1) * TT],
                                   ps[:, 0:TT], AF.Tanh)
            b.s_a.inc(i1)
            scalar.drain()
            i2 = scalar.activation(ST1[:, oc * TT:(oc + 1) * TT],
                                   ST1[:, oc * TT:(oc + 1) * TT], AF.Tanh)
            b.s_a.inc(i2)
    blk.scalar(a_t)

    def st_a(sync, _a=b.s_a.n):
        sync.wait_ge(b.s_a.h, _a)
        for oc in range(NC8):
            ins = sync.dma_start(out=A_hb[oc * 128:(oc + 1) * 128, t0:t0 + TT],
                                 in_=ST1[:, oc * TT:(oc + 1) * TT])
            b.s_st.inc(ins, 16)
    blk.sync(st_a)

    plan2 = mm_set(b, WT2, rhs, TT, NC8, NC8, NC8 * 128, (), b.s_v)

    def v_c(vector):
        for oc, ps, n in plan2:
            vector.wait_ge(b.s_mm.h, n)
            ins = vector.tensor_scalar_add(
                ST2[:, oc * TT:(oc + 1) * TT], ps[:, 0:TT],
                b.v_lbhf[:, 8 * layer + oc:8 * layer + oc + 1])
            b.s_v.inc(ins)
    blk.vector(v_c)

    def st_c(sync, _v=b.s_v.n):
        sync.wait_ge(b.s_v.h, _v)
        for oc in range(NC8):
            ins = sync.dma_start(out=C_hb[oc * 128:(oc + 1) * 128, t0:t0 + TT],
                                 in_=ST2[:, oc * TT:(oc + 1) * TT])
            b.s_st.inc(ins, 16)
    blk.sync(st_c)



def emit_scan(b, WhfT, A_hb, C_hb, hs_hb, layer, S, CH):
    blk, nc = b.blk, b.nc
    n_chunks = S // CH

    with contextlib.ExitStack() as ph:
        WSC = ph.enter_context(nc.sbuf_tensor(f"WSC{layer}", [128, 64 * 128], BF16))
        A_sb = [ph.enter_context(nc.sbuf_tensor(f"scA{layer}_{i}", [128, NC8 * CH], F32)) for i in range(2)]
        C_sb = [ph.enter_context(nc.sbuf_tensor(f"scC{layer}_{i}", [128, NC8 * CH], F32)) for i in range(2)]
        H_sb = [ph.enter_context(nc.sbuf_tensor(f"scH{layer}_{i}", [128, NC8 * CH], BF16)) for i in range(2)]
        hcur = ph.enter_context(nc.sbuf_tensor(f"hcur{layer}", [128, 8], F32))
        hbf = ph.enter_context(nc.sbuf_tensor(f"hbf{layer}", [128, 8], BF16))
        ut = ph.enter_context(nc.sbuf_tensor(f"ut{layer}", [128, 8], F32))
        gt = ph.enter_context(nc.sbuf_tensor(f"gt{layer}", [128, 8], F32))
        et = ph.enter_context(nc.sbuf_tensor(f"et{layer}", [128, 8], F32))

        load_weight_1024(b, WSC, WhfT, 128 * NC8)

        base_mm, base_h = b.s_smm.n, b.s_sh.n
        base_u, base_g = b.s_su.n, b.s_sg.n
        base_ac, base_hs = b.s_ac.n, b.s_hs.n

        def vinit(vector, _w=b.s_w.n):
            vector.wait_ge(b.s_w.h, _w)
            vector.tensor_copy(hcur[:, :], b.v_linit[:, 8 * layer:8 * layer + 8])
            vector.drain()
            ins = vector.tensor_copy(hbf[:, :], hcur[:, :])
            ins.then_inc(b.s_sh.h, 1)
        blk.vector(vinit)
        b.s_sh.n += 1

        base_ac2 = [b.s_ac2[i].n for i in range(2)]
        base_acb = [b.s_acb[i].n for i in range(2)]
        base_hs2 = [b.s_hs2[i].n for i in range(2)]
        st_base = b.s_st.n

        H2 = CH // 2

        def dma_ac(sync):
            sync.wait_ge(b.s_st.h, st_base)
            for c in range(n_chunks):
                for half in range(2):
                    if c >= 2:
                        if half == 0:
                            sync.wait_ge(b.s_sh.h,
                                         base_h + 1 + (c - 2) * CH + H2)
                        else:
                            sync.wait_ge(b.s_sh.h, base_h + 1 + (c - 1) * CH)
                    sem = b.s_ac2[c % 2] if half == 0 else b.s_acb[c % 2]
                    for kc in range(NC8):
                        t0c = c * CH + half * H2
                        sb0 = kc * CH + half * H2
                        i1 = sync.dma_start(
                            out=A_sb[c % 2][:, sb0:sb0 + H2],
                            in_=A_hb[kc * 128:(kc + 1) * 128, t0c:t0c + H2])
                        sem.inc(i1, 16)
                        i2 = sync.dma_start(
                            out=C_sb[c % 2][:, sb0:sb0 + H2],
                            in_=C_hb[kc * 128:(kc + 1) * 128, t0c:t0c + H2])
                        sem.inc(i2, 16)
        blk.sync(dma_ac)

        def dma_hist(eng):
            if getattr(b, 'scan_no_hist', False):
                return
            for c in range(n_chunks):
                eng.wait_ge(b.s_sh.h, base_h + 1 + (c + 1) * CH)
                for kc in range(NC8):
                    ins = eng.dma_start(
                        out=hs_hb[kc * 128:(kc + 1) * 128, c * CH:(c + 1) * CH],
                        in_=H_sb[c % 2][:, kc * CH:(kc + 1) * CH])
                    b.s_hs2[c % 2].inc(ins, 16)
        blk.scalar(dma_hist) if getattr(b, 'scan_hist_on_scalar', False) else blk.sync(dma_hist)

        for c in range(n_chunks):
            def t_chunk(tensor, _c=c, _w=b.s_w.n):
                if getattr(b, 'scan_no_mm', False):
                    return
                if _c == 0:
                    tensor.wait_ge(b.s_w.h, _w)
                from concourse.expressions import RuntimeValue
                from concourse.bass_primitives_rust import RegisterHandles
                zr = b.wregs.get(("zero", tensor.engine))
                if zr is None:
                    zr = nc.alloc_register(tensor.engine, "zreg")
                    b.wregs[("zero", tensor.engine)] = zr
                tensor.reg_mov(zr, 0)
                zrv = RuntimeValue(val=RegisterHandles(zr), min_val=0, max_val=0)
                sw = tensor.Switch(zrv, 1)
                for _case in sw:
                    _emit_scan_tensor_body(b, tensor, _c, CH, WSC, hbf)
                return

            def _emit_scan_tensor_body(b, tensor, _c, CH, WSC, hbf):
                nc = b.nc
                with fori(nc, tensor, 0, CH) as i:
                    wait_reg(b, tensor, b.s_sh, i, base_h + 1 + _c * CH,
                             base_h + 1 + _c * CH, base_h + _c * CH + CH)
                    for dc in range(NC8):
                        for kc in range(NC8):
                            col = (kc * NC8 + dc) * 128
                            ins = tensor.matmul(b.PSS[:, dc:dc + 1],
                                                WSC[:, col:col + 128],
                                                hbf[:, kc:kc + 1],
                                                start=(kc == 0),
                                                stop=(kc == NC8 - 1))
                    ins.then_inc(b.s_smm.h, 1)
            blk.tensor(t_chunk)
            b.s_smm.n += CH

            def a_chunk(scalar, _c=c):
                with fori(nc, scalar, 0, CH) as i:
                    scalar.activation(b.dum[0:1, 0:1], b.cst[0:1, 0:1],
                                      AF.Sigmoid)
                    scalar.drain()
                    wait_reg(b, scalar, b.s_su, i, base_u + _c * CH + 1,
                             base_u + _c * CH + 1, base_u + _c * CH + CH)
                    ins = scalar.activation(gt[:, :], ut[:, :], AF.Sigmoid)
                    ins.then_inc(b.s_sg.h, 1)
            blk.scalar(a_chunk)
            b.s_sg.n += CH

            def v_chunk(vector, _c=c):
                vector.wait_ge(b.s_ac2[_c % 2].h,
                               base_ac2[_c % 2] + 16 * 2 * NC8 * (_c // 2 + 1))
                if _c >= 2 and not getattr(b, 'scan_no_hist', False):
                    vector.wait_ge(b.s_hs2[_c % 2].h,
                                   base_hs2[_c % 2] + 16 * NC8 * ((_c - 2) // 2 + 1))
                with fori(nc, vector, 0, CH) as i:
                    et2 = vector.engine
                    hreg = b.wregs.get(("half", et2))
                    if hreg is None:
                        hreg = nc.alloc_register(et2, "halfreg")
                        b.wregs[("half", et2)] = hreg
                    vector.reg_alu(hreg, i, CH // 2, op=ALU.is_ge)
                    vector.reg_mul(hreg, hreg, 16 * 2 * NC8)
                    vector.reg_add(hreg, hreg,
                                   base_acb[_c % 2] + 16 * 2 * NC8 * (_c // 2))
                    vector.wait_ge(b.s_acb[_c % 2].h, hreg)
                    if not getattr(b, 'scan_no_mm', False):
                        wait_reg(b, vector, b.s_smm, i, base_mm + _c * CH + 1,
                                 base_mm + _c * CH + 1, base_mm + _c * CH + CH)
                    sl = DynSlice(i, 8, CH)
                    ins = vector.tensor_tensor(ut[:, :], b.PSS[:, 0:8],
                                               C_sb[_c % 2][:, sl], ALU.add)
                    ins.then_inc(b.s_su.h, 1)
                    vector.tensor_tensor(et[:, :], A_sb[_c % 2][:, sl],
                                         hcur[:, :], ALU.subtract)
                    wait_reg(b, vector, b.s_sg, i, base_g + _c * CH + 1,
                             base_g + _c * CH + 1, base_g + _c * CH + CH)
                    vector.drain()
                    vector.tensor_mul(et[:, :], gt[:, :], et[:, :])
                    vector.drain()
                    vector.tensor_add(hcur[:, :], hcur[:, :], et[:, :])
                    vector.drain()
                    vector.tensor_copy(hbf[:, :], hcur[:, :])
                    ins = vector.tensor_copy(H_sb[_c % 2][:, sl], hcur[:, :])
                    ins.then_inc(b.s_sh.h, 1)
            blk.vector(v_chunk)
            b.s_su.n += CH
            b.s_sh.n += CH



def p5_tile(b, hs1_h, xn_h, xln_h, x_in, xres, XC, XLN, XN, XNB, H1B,
            ST1, ST2, GS, WT1, WT2, WT3, t0, TT):
    blk = b.blk

    def ld(sync, _vn=b.s_v.n, _an=b.s_a.n):
        sync.wait_ge(b.s_v.h, _vn)
        sync.wait_ge(b.s_a.h, _an)
        for kc in range(NC8):
            for dst, src in ((XC, x_in), (XLN, xln_h), (XN, xn_h)):
                ins = sync.dma_start(out=dst[:, kc * TT:(kc + 1) * TT],
                                     in_=src[kc * 128:(kc + 1) * 128, t0:t0 + TT])
                b.s_ld.inc(ins, 16)
    blk.sync(ld)

    def ldg(gpsimd, _vn=b.s_v.n, _an=b.s_a.n):
        gpsimd.wait_ge(b.s_v.h, _vn)
        gpsimd.wait_ge(b.s_a.h, _an)
        for kc in range(NC8):
            i1 = gpsimd.dma_start(out=H1B[:, kc * TT:(kc + 1) * TT],
                                  in_=hs1_h[kc * 128:(kc + 1) * 128, t0:t0 + TT])
            b.s_ldg.inc(i1, 16)
            i2 = gpsimd.dma_start(out=XNB[:, kc * TT:(kc + 1) * TT],
                                  in_=xn_h[kc * 128:(kc + 1) * 128, t0:t0 + TT])
            b.s_ldg.inc(i2, 16)
    blk.gpsimd(ldg)

    rh = lambda kc: H1B[:, kc * TT:(kc + 1) * TT]
    rx = lambda kc: XNB[:, kc * TT:(kc + 1) * TT]

    plan1 = mm_set(b, WT1, rh, TT, NC8, NC8, NC8 * 128, (b.s_ldg, b.s_w),
                   b.s_a, incs_per_oc=2)

    def a_t(scalar):
        for oc, ps, n in plan1:
            scalar.wait_ge(b.s_mm.h, n)
            i1 = scalar.activation(ST1[:, oc * TT:(oc + 1) * TT],
                                   ps[:, 0:TT], AF.Tanh)
            b.s_a.inc(i1)
            scalar.drain()
            i2 = scalar.activation(ST1[:, oc * TT:(oc + 1) * TT],
                                   ST1[:, oc * TT:(oc + 1) * TT], AF.Tanh)
            b.s_a.inc(i2)
    blk.scalar(a_t)

    plan2 = mm_set(b, WT2, rh, TT, NC8, NC8, NC8 * 128, (), b.s_v)

    def v_c(vector):
        for oc, ps, n in plan2:
            vector.wait_ge(b.s_mm.h, n)
            ins = vector.tensor_copy(ST2[:, oc * TT:(oc + 1) * TT],
                                     ps[:, 0:TT])
            b.s_v.inc(ins)
    blk.vector(v_c)

    plan3 = mm_set(b, WT3, rx, TT, NC8, NC8, NC8 * 128, (), b.s_v)

    def v_u(vector):
        for oc, ps, n in plan3:
            vector.wait_ge(b.s_mm.h, n)
            ins = vector.scalar_tensor_tensor(
                GS[:, oc * TT:(oc + 1) * TT], ps[:, 0:TT],
                b.v_gbhf[:, oc:oc + 1], ST2[:, oc * TT:(oc + 1) * TT],
                ALU.add, ALU.add)
            b.s_v.inc(ins)
    blk.vector(v_u)

    def a_sig(scalar, _v=b.s_v.n):
        scalar.wait_ge(b.s_v.h, _v)
        for oc in range(NC8):
            ins = scalar.activation(GS[:, oc * TT:(oc + 1) * TT],
                                    GS[:, oc * TT:(oc + 1) * TT], AF.Sigmoid)
        b.s_a.inc(ins)
    blk.scalar(a_sig)

    def v_cell(vector, _a=b.s_a.n, _ld=b.s_ld.n):
        vector.wait_ge(b.s_a.h, _a)
        vector.wait_ge(b.s_ld.h, _ld)
        vector.drain()
        for oc in range(NC8):
            sl = slice(oc * TT, (oc + 1) * TT)
            vector.tensor_sub(ST1[:, sl], ST1[:, sl], XN[:, sl])
        vector.drain()
        for oc in range(NC8):
            sl = slice(oc * TT, (oc + 1) * TT)
            vector.tensor_mul(ST1[:, sl], GS[:, sl], ST1[:, sl])
        vector.drain()
        for oc in range(NC8):
            sl = slice(oc * TT, (oc + 1) * TT)
            vector.tensor_add(ST1[:, sl], ST1[:, sl], XN[:, sl])
        vector.drain()
        for oc in range(NC8):
            sl = slice(oc * TT, (oc + 1) * TT)
            ins = vector.tensor_add(ST1[:, sl], ST1[:, sl], XLN[:, sl])
        b.s_v.inc(ins)
    blk.vector(v_cell)

    emit_ln(b, lambda kc: ST1[:, kc * TT:(kc + 1) * TT],
            lambda kc: ST2[:, kc * TT:(kc + 1) * TT], TT,
            add_src=lambda kc: XC[:, kc * TT:(kc + 1) * TT])

    def st(sync, _v=b.s_v.n):
        sync.wait_ge(b.s_v.h, _v)
        for kc in range(NC8):
            ins = sync.dma_start(out=xres[kc * 128:(kc + 1) * 128, t0:t0 + TT],
                                 in_=ST2[:, kc * TT:(kc + 1) * TT])
            b.s_st.inc(ins, 16)
    blk.sync(st)



def emit_p6(b, xres, w1T, w2T, h1T, h2T, w3T, S, TTF):
    blk, nc = b.blk, b.nc
    NB = S // TTF
    NSUB = TTF // 512

    with contextlib.ExitStack() as ph:
        XC = ph.enter_context(nc.sbuf_tensor("XC6", [128, NC8 * 512], F32))
        XLN = ph.enter_context(nc.sbuf_tensor("XLN6", [128, NC8 * 512], F32))
        XNB = ph.enter_context(nc.sbuf_tensor("XNB6", [128, NC8 * TTF], BF16))
        U = ph.enter_context(nc.sbuf_tensor("U6", [128, 32 * TTF], BF16))
        HM = ph.enter_context(nc.sbuf_tensor("HM6", [128, 32 * TTF], BF16))
        ZS = ph.enter_context(nc.sbuf_tensor("ZS6", [128, 512], F32))
        OUT4 = XLN
        OLN = XLN
        WB = [ph.enter_context(nc.sbuf_tensor(f"WB6{i}", [128, 32 * 128], BF16))
              for i in range(2)]
        WB2 = [ph.enter_context(nc.sbuf_tensor(f"WBb6{i}", [128, NC8 * 128], BF16))
               for i in range(2)]

        for ib in range(NB):
            t0 = ib * TTF
            for sub in range(NSUB):
                ts = t0 + sub * 512

                def ld(sync, _ts=ts, _vn=b.s_v.n):
                    sync.wait_ge(b.s_v.h, _vn)
                    for kc in range(NC8):
                        ins = sync.dma_start(
                            out=XC[:, kc * 512:(kc + 1) * 512],
                            in_=xres[kc * 128:(kc + 1) * 128, _ts:_ts + 512])
                        b.s_ld.inc(ins, 16)
                blk.sync(ld)
                emit_ln(b, lambda kc: XC[:, kc * 512:(kc + 1) * 512],
                        lambda kc: XLN[:, kc * 512:(kc + 1) * 512], 512,
                        wait_sems=(b.s_ld,),
                        out_bf=lambda kc, _s=sub: XNB[:, kc * TTF + _s * 512:
                                                      kc * TTF + (_s + 1) * 512])

            guards = {}
            base_wb = [b.s_wb[i].n for i in range(2)]
            for oc in range(32):
                pw = oc % 2

                def wld(gpsimd, _oc=oc, _pw=pw, _mmn=b.s_mm.n):
                    if _oc >= 2:
                        gv, ga = guards[_oc - 2]
                        gpsimd.wait_ge(b.s_v.h, gv)
                        gpsimd.wait_ge(b.s_a.h, ga)
                    else:
                        gpsimd.wait_ge(b.s_mm.h, _mmn)
                    i1 = gpsimd.dma_start(
                        out=WB[_pw][:, 0:NC8 * 128].rearrange(
                            "p (kc o) -> p kc o", o=128),
                        in_=w1T[:, _oc * 128:(_oc + 1) * 128].rearrange(
                            "(kc p) o -> p kc o", p=128))
                    b.s_wb[_pw].inc(i1, 16)
                    i2 = gpsimd.dma_start(
                        out=WB2[_pw][:, 0:NC8 * 128].rearrange(
                            "p (kc o) -> p kc o", o=128),
                        in_=w2T[:, _oc * 128:(_oc + 1) * 128].rearrange(
                            "(kc p) o -> p kc o", p=128))
                    b.s_wb[_pw].inc(i2, 16)
                blk.gpsimd(wld)

                for sub in range(NSUB):
                    rhs = lambda kc, _s=sub: XNB[:, kc * TTF + _s * 512:
                                                 kc * TTF + (_s + 1) * 512]
                    plan = []

                    def t_z(tensor, _oc=oc, _pw=pw, _s=sub, _rhs=rhs,
                            _wn=base_wb[pw] + 32 * (oc // 2 + 1),
                            _vn=b.s_v.n, _plan=plan,
                            _a0=b.s_a.n, _v0=b.s_v.n):
                        tensor.wait_ge(b.s_wb[_pw].h, _wn)
                        if _oc == 0 and _s == 0:
                            tensor.wait_ge(b.s_v.h, _vn)
                        ps1, sl1 = b.pool.acquire(tensor)
                        b.pool.expect(sl1, b.s_a, _a0 + 1)
                        ins = mm_acc(b, tensor, WB[_pw], _rhs, 512, 0, NC8,
                                     ps1, 128)
                        b.s_mm.inc(ins)
                        _plan.append((ps1, b.s_mm.n))
                        ps2, sl2 = b.pool.acquire(tensor)
                        b.pool.expect(sl2, b.s_v, _v0 + 1)
                        ins = mm_acc(b, tensor, WB2[_pw], _rhs, 512, 0, NC8,
                                     ps2, 128)
                        b.s_mm.inc(ins)
                        _plan.append((ps2, b.s_mm.n))
                    blk.tensor(t_z)

                    def a_sil(scalar, _oc=oc, _s=sub, _plan=plan,
                              _v0=b.s_v.n):
                        scalar.wait_ge(b.s_v.h, _v0)
                        ps1, n1 = _plan[0]
                        scalar.wait_ge(b.s_mm.h, n1)
                        ins = scalar.activation(
                            ZS[:, 0:512], ps1[:, 0:512],
                            AF.Silu, scale=b.v_w1s[:, _oc:_oc + 1])
                        b.s_a.inc(ins)
                    blk.scalar(a_sil)

                    def v_u(vector, _oc=oc, _s=sub, _plan=plan, _an=b.s_a.n):
                        ps2, n2 = _plan[1]
                        vector.wait_ge(b.s_mm.h, n2)
                        vector.wait_ge(b.s_a.h, _an)
                        ins = vector.scalar_tensor_tensor(
                            U[:, _oc * TTF + _s * 512:_oc * TTF + (_s + 1) * 512],
                            ps2[:, 0:512], b.v_w2s[:, _oc:_oc + 1],
                            ZS[:, 0:512],
                            ALU.mult, ALU.mult)
                        b.s_v.inc(ins)
                    blk.vector(v_u)
                guards[oc] = (b.s_v.n, b.s_a.n)

            _p6_mid(b, U, HM, h1T, b.v_h1s, WB, TTF, NSUB)
            _p6_mid(b, HM, U, h2T, b.v_h2s, WB, TTF, NSUB)

            for sub in range(NSUB):
                ts = t0 + sub * 512
                for grp in range(2):
                    g_hist = {}
                    base_wb = [b.s_wb[i].n for i in range(2)]
                    for oc3 in range(32):
                        pw = oc3 % 2

                        def w3ld(gpsimd, _oc3=oc3, _grp=grp, _pw=pw,
                                 _mmn=b.s_mm.n):
                            if _oc3 >= 2:
                                gpsimd.wait_ge(b.s_g.h, g_hist[_oc3 - 2])
                            else:
                                gpsimd.wait_ge(b.s_mm.h, _mmn)
                            ins = gpsimd.dma_start(
                                out=WB[_pw][:, 0:512],
                                in_=w3T[_oc3 * 128:(_oc3 + 1) * 128,
                                        _grp * 512:(_grp + 1) * 512])
                            b.s_wb[_pw].inc(ins, 16)
                        blk.gpsimd(w3ld)

                        def t_w3(tensor, _oc3=oc3, _pw=pw, _s=sub,
                                 _wn=base_wb[oc3 % 2] + 16 * (oc3 // 2 + 1),
                                 _vn=b.s_v.n, _an=b.s_a.n):
                            tensor.wait_ge(b.s_wb[_pw].h, _wn)
                            if _oc3 == 0:
                                tensor.wait_ge(b.s_v.h, _vn)
                                tensor.wait_ge(b.s_a.h, _an)
                            for oc4 in range(4):
                                ins = tensor.matmul(
                                    b.PS[oc4][:, 0:512],
                                    WB[_pw][:, oc4 * 128:(oc4 + 1) * 128],
                                    U[:, _oc3 * TTF + _s * 512:
                                      _oc3 * TTF + (_s + 1) * 512],
                                    start=(_oc3 == 0), stop=(_oc3 == 31))
                            if _oc3 == 31:
                                b.s_mm.inc(ins)
                            else:
                                b.s_g.inc(ins)
                        blk.tensor(t_w3)
                        g_hist[oc3] = b.s_g.n

                    def v_ev(vector, _grp=grp, _n=b.s_mm.n):
                        vector.wait_ge(b.s_mm.h, _n)
                        for oc4 in range(4):
                            oc = _grp * 4 + oc4
                            ins = vector.tensor_scalar_mul(
                                OUT4[:, oc * 512:(oc + 1) * 512],
                                b.PS[oc4][:, 0:512], b.v_w3s[:, oc:oc + 1])
                        b.s_v.inc(ins)
                    blk.vector(v_ev)

                def ld2(sync, _ts=ts, _vn=b.s_v.n):
                    sync.wait_ge(b.s_v.h, _vn)
                    for kc in range(NC8):
                        ins = sync.dma_start(
                            out=XC[:, kc * 512:(kc + 1) * 512],
                            in_=xres[kc * 128:(kc + 1) * 128, _ts:_ts + 512])
                        b.s_ld.inc(ins, 16)
                blk.sync(ld2)
                emit_ln(b, lambda kc: OUT4[:, kc * 512:(kc + 1) * 512],
                        lambda kc: OLN[:, kc * 512:(kc + 1) * 512], 512,
                        wait_sems=(b.s_ld,),
                        add_src=lambda kc: XC[:, kc * 512:(kc + 1) * 512])

                def st2(sync, _ts=ts, _v=b.s_v.n):
                    sync.wait_ge(b.s_v.h, _v)
                    for kc in range(NC8):
                        ins = sync.dma_start(
                            out=xres[kc * 128:(kc + 1) * 128, _ts:_ts + 512],
                            in_=OLN[:, kc * 512:(kc + 1) * 512])
                        b.s_st.inc(ins, 16)
                blk.sync(st2)


def _p6_mid(b, SRC, DST, WT_h, scale_vec, WB, TTF, NSUB):
    blk = b.blk
    guards = {}
    base_wb = [b.s_wb[i].n for i in range(2)]
    for oc in range(32):
        pw = oc % 2

        def wld(gpsimd, _oc=oc, _pw=pw, _mmn=b.s_mm.n):
            if _oc >= 2:
                gpsimd.wait_ge(b.s_a.h, guards[_oc - 2])
            else:
                gpsimd.wait_ge(b.s_mm.h, _mmn)
            ins = gpsimd.dma_start(
                out=WB[_pw][:, :].rearrange("p (kc o) -> p kc o", o=128),
                in_=WT_h[:, _oc * 128:(_oc + 1) * 128].rearrange(
                    "(kc p) o -> p kc o", p=128))
            b.s_wb[_pw].inc(ins, 16)
        blk.gpsimd(wld)

        for sub in range(NSUB):
            plan = []

            def t_mm(tensor, _oc=oc, _pw=pw, _s=sub,
                     _wn=base_wb[pw] + 16 * (oc // 2 + 1),
                     _vn=b.s_v.n, _an=b.s_a.n, _plan=plan):
                tensor.wait_ge(b.s_wb[_pw].h, _wn)
                if _oc == 0 and _s == 0:
                    tensor.wait_ge(b.s_v.h, _vn)
                    tensor.wait_ge(b.s_a.h, _an)
                ps, sl = b.pool.acquire(tensor)
                b.pool.expect(sl, b.s_a, _an + 1)
                for kc in range(32):
                    ins = tensor.matmul(
                        ps[:, 0:512], WB[_pw][:, kc * 128:(kc + 1) * 128],
                        SRC[:, kc * TTF + _s * 512:kc * TTF + (_s + 1) * 512],
                        start=(kc == 0), stop=(kc == 31))
                b.s_mm.inc(ins)
                _plan.append((ps, b.s_mm.n))
            blk.tensor(t_mm)

            def a_g(scalar, _oc=oc, _s=sub, _plan=plan):
                ps, n = _plan[0]
                scalar.wait_ge(b.s_mm.h, n)
                ins = scalar.activation(
                    DST[:, _oc * TTF + _s * 512:_oc * TTF + (_s + 1) * 512],
                    ps[:, 0:512], AF.Gelu, scale=scale_vec[:, _oc:_oc + 1])
                b.s_a.inc(ins)
            blk.scalar(a_g)
        guards[oc] = b.s_a.n



def emit_p7(b, xres, WqkvT, WoT, bqkv, qT_h, kT_h, v_h, zT_h, amask,
            S, TT, QTILE):
    blk, nc = b.blk, b.nc
    NT = S // TT
    NKT = S // 128
    NQT = S // QTILE
    KPQ = QTILE // 128

    with contextlib.ExitStack() as ph:
        XC = ph.enter_context(nc.sbuf_tensor("XC7", [128, NC8 * 512], F32))
        b.row_q = ph.enter_context(nc.sbuf_tensor("row_q", [1, D], F32))
        XLN = ph.enter_context(nc.sbuf_tensor("XLN7", [128, NC8 * 512], F32))
        pha = ph.enter_context(contextlib.ExitStack())
        XNB = pha.enter_context(nc.sbuf_tensor("XNB7", [128, NC8 * TT], BF16))
        STB = pha.enter_context(nc.sbuf_tensor("STB7", [128, NC8 * TT], BF16))
        VS = pha.enter_context(nc.sbuf_tensor("VS7", [128, D], F32))
        WQ = pha.enter_context(nc.sbuf_tensor("WQ7", [128, NC8 * 3 * D], BF16))
        BIASV = pha.enter_context(nc.sbuf_tensor("BIASV7", [128, D], F32))
        MASK = pha.enter_context(nc.sbuf_tensor("MASK7", [128, KPQ * QTILE], F32))

        load_weight_1024(b, WQ, WqkvT, 3 * D)

        def mload(sync):
            ins0 = sync.dma_start(out=b.row_q[0:1, :], in_=bqkv[2 * D:3 * D])
            b.s_ld.inc(ins0, 16)
            ins = sync.dma_start(
                out=MASK[:, :].rearrange("p (a q) -> p a q", q=QTILE),
                in_=amask.rearrange("(a p) q -> p a q", p=128))
            b.s_ld.inc(ins, 16)
        blk.sync(mload)

        def t_bv(tensor, _ld=b.s_ld.n):
            tensor.wait_ge(b.s_ld.h, _ld)
            bcast(b, tensor, b.row_q[0:1, 0:512], b.PSB, 512)
            ins = bcast(b, tensor, b.row_q[0:1, 512:1024], b.PSB2, 512)
            b.s_mm.inc(ins)
        blk.tensor(t_bv)

        def v_bv(vector, _mm=b.s_mm.n):
            vector.wait_ge(b.s_mm.h, _mm)
            vector.tensor_copy(BIASV[:, 0:512], b.PSB[:, 0:512])
            ins = vector.tensor_copy(BIASV[:, 512:1024], b.PSB2[:, 0:512])
            b.s_v.inc(ins)
        blk.vector(v_bv)

        for it in range(NT):
            t0 = it * TT
            for sub in range(TT // 512):
                ts = t0 + sub * 512

                def ld(sync, _ts=ts, _vn=b.s_v.n):
                    sync.wait_ge(b.s_v.h, _vn)
                    for kc in range(NC8):
                        ins = sync.dma_start(
                            out=XC[:, kc * 512:(kc + 1) * 512],
                            in_=xres[kc * 128:(kc + 1) * 128, _ts:_ts + 512])
                        b.s_ld.inc(ins, 16)
                blk.sync(ld)

                emit_ln(b, lambda kc: XC[:, kc * 512:(kc + 1) * 512],
                        lambda kc: XLN[:, kc * 512:(kc + 1) * 512], 512,
                        wait_sems=(b.s_ld,))
                emit_stats(b, lambda kc: XLN[:, kc * 512:(kc + 1) * 512],
                           NC8, 512, [(0, "sumsq")])

                def a_rs(scalar, _mm=b.s_mm.n):
                    scalar.wait_ge(b.s_mm.h, _mm)
                    ins = scalar.activation(b.row_b[0:1, 0:512],
                                            b.PSST[0:1, 0:512], AF.Sqrt,
                                            bias=RMS_EPS, scale=1.0 / D)
                    b.s_a.inc(ins)
                blk.scalar(a_rs)

                def v_rec3(vector, _a=b.s_a.n):
                    vector.wait_ge(b.s_a.h, _a)
                    vector.drain()
                    ins = vector.reciprocal(b.row_b[0:1, 0:512],
                                            b.row_b[0:1, 0:512])
                    b.s_v.inc(ins)
                blk.vector(v_rec3)

                def t_bc(tensor, _a=b.s_v.n):
                    tensor.wait_ge(b.s_v.h, _a)
                    ins = bcast(b, tensor, b.row_b[0:1, 0:512], b.PSB, 512)
                    b.s_mm.inc(ins)
                blk.tensor(t_bc)

                def v_n(vector, _s=sub, _mm=b.s_mm.n):
                    vector.wait_ge(b.s_mm.h, _mm)
                    for kc in range(NC8):
                        ins = vector.tensor_mul(
                            XNB[:, kc * TT + _s * 512:kc * TT + (_s + 1) * 512],
                            XLN[:, kc * 512:(kc + 1) * 512], b.PSB[:, 0:512])
                    b.s_v.inc(ins)
                blk.vector(v_n)

            for part, dst_h in ((0, qT_h), (1, kT_h)):
                for sub in range(TT // 512):
                    rhs = lambda kc, _s=sub: XNB[:, kc * TT + _s * 512:
                                                 kc * TT + (_s + 1) * 512]
                    plan = []

                    def t_qk(tensor, _part=part, _s=sub, _rhs=rhs,
                             _wn=b.s_w.n, _vn=b.s_v.n, _plan=plan,
                             _v0=b.s_v.n):
                        tensor.wait_ge(b.s_w.h, _wn)
                        tensor.wait_ge(b.s_v.h, _vn)
                        for oc in range(NC8):
                            ps, sl = b.pool.acquire(tensor)
                            b.pool.expect(sl, b.s_v, _v0 + oc + 1)
                            ins = mm_acc(b, tensor, WQ, _rhs, 512,
                                         _part * 8 + oc, NC8, ps, 3 * D)
                            b.s_mm.inc(ins)
                            _plan.append((oc, ps, b.s_mm.n))
                    blk.tensor(t_qk)

                    def v_qk(vector, _part=part, _s=sub, _plan=plan,
                             _stn=b.s_st.n):
                        vector.wait_ge(b.s_st.h, _stn)
                        for oc, ps, n in _plan:
                            vector.wait_ge(b.s_mm.h, n)
                            ins = vector.tensor_scalar_add(
                                STB[:, oc * TT + _s * 512:oc * TT + (_s + 1) * 512],
                                ps[:, 0:512],
                                b.v_bqkv[:, _part * 8 + oc:_part * 8 + oc + 1])
                            b.s_v.inc(ins)
                    blk.vector(v_qk)

                def stqk(sync, _dst=dst_h, _t0=t0, _v=b.s_v.n):
                    sync.wait_ge(b.s_v.h, _v)
                    for oc in range(NC8):
                        ins = sync.dma_start(
                            out=_dst[oc * 128:(oc + 1) * 128, _t0:_t0 + TT],
                            in_=STB[:, oc * TT:(oc + 1) * TT])
                        b.s_st.inc(ins, 16)
                blk.sync(stqk)

            for tsub in range(TT // 128):
                tks = t0 + tsub * 128
                lsub = tsub * 128 // 512
                loff = (tsub * 128) % 512
                plan = []

                def t_v(tensor, _lsub=lsub, _loff=loff, _plan=plan,
                        _wn=b.s_w.n, _vn=b.s_v.n, _v0=b.s_v.n):
                    tensor.wait_ge(b.s_w.h, _wn)
                    tensor.wait_ge(b.s_v.h, _vn)
                    for half in range(2):
                        ps, sl = b.pool.acquire(tensor)
                        b.pool.expect(sl, b.s_v, _v0 + half + 1)
                        for kc in range(NC8):
                            lhs = XNB[:, kc * TT + _lsub * 512 + _loff:
                                      kc * TT + _lsub * 512 + _loff + 128]
                            rhscol = kc * 3 * D + 2 * D + half * 512
                            ins = tensor.matmul(
                                ps[:, 0:512], lhs,
                                WQ[:, rhscol:rhscol + 512],
                                start=(kc == 0), stop=(kc == NC8 - 1))
                        b.s_mm.inc(ins)
                        _plan.append((half, ps, b.s_mm.n))
                blk.tensor(t_v)

                def v_v(vector, _plan=plan, _stn=b.s_st.n):
                    vector.wait_ge(b.s_st.h, _stn)
                    for half, ps, n in _plan:
                        vector.wait_ge(b.s_mm.h, n)
                        ins = vector.tensor_add(
                            VS[:, half * 512:(half + 1) * 512], ps[:, 0:512],
                            BIASV[:, half * 512:(half + 1) * 512])
                        b.s_v.inc(ins)
                blk.vector(v_v)

                def st_v(sync, _tks=tks, _v=b.s_v.n):
                    sync.wait_ge(b.s_v.h, _v)
                    ins = sync.dma_start(
                        out=v_h[:, _tks:_tks + 128, :].rearrange("h t dh -> t h dh"),
                        in_=VS[:, :].rearrange("p (h dh) -> p h dh", dh=DHD))
                    b.s_st.inc(ins, 16)
                blk.sync(st_v)

        QT = pha.enter_context(nc.sbuf_tensor("QT7", [64, S], BF16))
        KT = pha.enter_context(nc.sbuf_tensor("KT7", [64, S], BF16))
        VT = pha.enter_context(nc.sbuf_tensor("VT7", [128, NKT * DHD], F32))
        VNB = pha.enter_context(nc.sbuf_tensor("VNB7", [128, NKT * DHD], BF16))
        MC = pha.enter_context(nc.sbuf_tensor("MC7", [128, NKT], F32))
        ES = pha.enter_context(nc.sbuf_tensor("ES7", [128, QTILE], BF16))
        SD = pha.enter_context(nc.sbuf_tensor("SD7", [128, QTILE], F32))
        CL = pha.enter_context(nc.sbuf_tensor("CL7", [128, 8], F32))
        ZB = pha.enter_context(nc.sbuf_tensor("ZB7", [64, S], BF16))

        for h in range(NH):
            def ldh(sync, _h=h, _vn=b.s_v.n, _an=b.s_a.n, _mmn=b.s_mm.n,
                    _stn=b.s_st.n):
                sync.wait_ge(b.s_v.h, _vn)
                sync.wait_ge(b.s_a.h, _an)
                sync.wait_ge(b.s_mm.h, _mmn)
                sync.wait_ge(b.s_st.h, _stn)
                i1 = sync.dma_start(out=QT[:, :],
                                    in_=qT_h[_h * 64:(_h + 1) * 64, :])
                b.s_ld.inc(i1, 16)
                i2 = sync.dma_start(out=KT[:, :],
                                    in_=kT_h[_h * 64:(_h + 1) * 64, :])
                b.s_ld.inc(i2, 16)
                i3 = sync.dma_start(
                    out=VT[:, :].rearrange("p (kt dh) -> p kt dh", dh=DHD),
                    in_=v_h[_h, :, :].rearrange("(kt p) dh -> p kt dh", p=128))
                b.s_ld.inc(i3, 16)
            blk.sync(ldh)

            for kt in range(NKT):
                qt0 = kt // KPQ
                al = kt % KPQ

                def v_init(vector, _ld=b.s_ld.n):
                    vector.wait_ge(b.s_ld.h, _ld)
                    vector.drain()
                    vector.memset(CL[:, 0:1], -1e30)
                    vector.memset(CL[:, 1:2], 0.0)
                    vector.drain()
                blk.vector(v_init)

                for qt in range(qt0, NQT):
                    diag = (qt == qt0)
                    plan = []

                    def t_s(tensor, _kt=kt, _qt=qt, _plan=plan, _diag=diag,
                            _first=(kt == 0 and qt == qt0), _ldn=b.s_ld.n,
                            _v0=b.s_v.n, _a0=b.s_a.n):
                        if _first:
                            tensor.wait_ge(b.s_ld.h, _ldn)
                        ps, sl = b.pool.acquire(tensor)
                        if _diag:
                            b.pool.expect(sl, b.s_v, _v0 + 1)
                        else:
                            b.pool.expect(sl, b.s_a, _a0 + 2)
                        ins = tensor.matmul(
                            ps[:, 0:QTILE],
                            KT[:, _kt * 128:(_kt + 1) * 128],
                            QT[:, _qt * QTILE:(_qt + 1) * QTILE],
                            start=True, stop=True)
                        b.s_mm.inc(ins)
                        _plan.append((ps, b.s_mm.n))
                    blk.tensor(t_s)

                    def v_flash(vector, _diag=diag, _al=al, _plan=plan):
                        ps, n = _plan[0]
                        vector.wait_ge(b.s_mm.h, n)
                        vector.drain()
                        if _diag:
                            ins = vector.tensor_add(
                                SD[:, :], ps[:, 0:QTILE],
                                MASK[:, _al * QTILE:(_al + 1) * QTILE])
                            b.s_v.inc(ins)
                            src = SD[:, :]
                            vector.drain()
                        else:
                            src = ps[:, 0:QTILE]
                        vector.reduce_max(CL[:, 2:3], src, axis=mybir.AxisListType.X)
                        vector.drain()
                        vector.tensor_max(CL[:, 3:4], CL[:, 0:1], CL[:, 2:3])
                        vector.drain()
                        vector.tensor_sub(CL[:, 4:5], CL[:, 0:1], CL[:, 3:4])
                        ins = vector.tensor_scalar_mul(CL[:, 7:8], CL[:, 3:4], -1.0)
                        b.s_v.inc(ins)
                    blk.vector(v_flash)

                    def a_flash(scalar, _diag=diag, _plan=plan, _v=b.s_v.n):
                        scalar.activation(b.dum[0:1, 0:1], b.cst[0:1, 0:1],
                                          AF.Exp)
                        scalar.drain()
                        scalar.wait_ge(b.s_v.h, _v)
                        ins = scalar.activation(CL[:, 5:6], CL[:, 4:5], AF.Exp)
                        b.s_a.inc(ins)
                        ps, n = _plan[0]
                        src = SD[:, :] if _diag else ps[:, 0:QTILE]
                        ins = scalar.activation(ES[:, :], src, AF.Exp,
                                                bias=CL[:, 7:8])
                        b.s_a.inc(ins)
                    blk.scalar(a_flash)

                    def v_merge(vector, _a=b.s_a.n):
                        vector.wait_ge(b.s_a.h, _a)
                        vector.drain()
                        vector.reduce_sum(CL[:, 6:7], ES[:, :], axis=mybir.AxisListType.X)
                        vector.tensor_mul(CL[:, 1:2], CL[:, 1:2], CL[:, 5:6])
                        vector.drain()
                        vector.tensor_add(CL[:, 1:2], CL[:, 1:2], CL[:, 6:7])
                        ins = vector.tensor_copy(CL[:, 0:1], CL[:, 3:4])
                        b.s_v.inc(ins)
                    blk.vector(v_merge)

                def v_fin(vector, _kt=kt, _v=b.s_v.n):
                    vector.wait_ge(b.s_v.h, _v)
                    vector.drain()
                    vector.reciprocal(CL[:, 6:7], CL[:, 1:2])
                    vector.tensor_scalar_mul(MC[:, _kt:_kt + 1], CL[:, 0:1], -1.0)
                    vector.drain()
                    ins = vector.tensor_scalar_mul(
                        VNB[:, _kt * DHD:(_kt + 1) * DHD],
                        VT[:, _kt * DHD:(_kt + 1) * DHD], CL[:, 6:7])
                    b.s_v.inc(ins)
                blk.vector(v_fin)

            for qt in range(NQT):
                nk = (qt + 1) * KPQ
                for kt in range(nk):
                    diag = (kt // KPQ == qt)
                    al = kt % KPQ
                    plan = []

                    def t_s2(tensor, _kt=kt, _qt=qt, _plan=plan, _diag=diag,
                             _v0=b.s_v.n, _a0=b.s_a.n):
                        ps, sl = b.pool.acquire(tensor)
                        if _diag:
                            b.pool.expect(sl, b.s_v, _v0 + 1)
                        else:
                            b.pool.expect(sl, b.s_a, _a0 + 1)
                        ins = tensor.matmul(
                            ps[:, 0:QTILE],
                            KT[:, _kt * 128:(_kt + 1) * 128],
                            QT[:, _qt * QTILE:(_qt + 1) * QTILE],
                            start=True, stop=True)
                        b.s_mm.inc(ins)
                        _plan.append((ps, b.s_mm.n))
                    blk.tensor(t_s2)

                    if diag:
                        def v_m(vector, _al=al, _plan=plan):
                            ps, n = _plan[0]
                            vector.wait_ge(b.s_mm.h, n)
                            ins = vector.tensor_add(
                                SD[:, :], ps[:, 0:QTILE],
                                MASK[:, _al * QTILE:(_al + 1) * QTILE])
                            b.s_v.inc(ins)
                        blk.vector(v_m)

                    def a_e2(scalar, _kt=kt, _diag=diag, _plan=plan,
                             _v=b.s_v.n, _g=b.s_g.n):
                        scalar.activation(b.dum[0:1, 0:1], b.cst[0:1, 0:1],
                                          AF.Exp)
                        scalar.drain()
                        scalar.wait_ge(b.s_g.h, _g)
                        ps, n = _plan[0]
                        if _diag:
                            scalar.wait_ge(b.s_v.h, _v)
                            src = SD[:, :]
                        else:
                            scalar.wait_ge(b.s_mm.h, n)
                            src = ps[:, 0:QTILE]
                        ins = scalar.activation(ES[:, :], src, AF.Exp,
                                                bias=MC[:, _kt:_kt + 1])
                        b.s_a.inc(ins)
                    blk.scalar(a_e2)

                    def t_av(tensor, _kt=kt, _first=(kt == 0),
                             _last=(kt == nk - 1), _a=b.s_a.n, _vn=b.s_v.n):
                        tensor.wait_ge(b.s_a.h, _a)
                        if _first:
                            tensor.wait_ge(b.s_v.h, _vn)
                        ins = tensor.matmul(
                            b.PSS[0:64, 0:QTILE],
                            VNB[:, _kt * DHD:(_kt + 1) * DHD],
                            ES[:, :], start=_first, stop=_last)
                        b.s_g.inc(ins)
                    blk.tensor(t_av)

                def v_z(vector, _qt=qt, _mm=b.s_g.n, _stn=b.s_st.n):
                    vector.wait_ge(b.s_g.h, _mm)
                    vector.wait_ge(b.s_st.h, _stn)
                    ins = vector.tensor_copy(
                        ZB[:, _qt * QTILE:(_qt + 1) * QTILE],
                        b.PSS[0:64, 0:QTILE])
                    b.s_v.inc(ins)
                blk.vector(v_z)

            def st_z(sync, _h=h, _v=b.s_v.n):
                sync.wait_ge(b.s_v.h, _v)
                ins = sync.dma_start(out=zT_h[_h * 64:(_h + 1) * 64, :],
                                     in_=ZB[:, :])
                b.s_st.inc(ins, 16)
            blk.sync(st_z)

        pha.close()

        def gate_wob(gpsimd, _vn=b.s_v.n, _an=b.s_a.n, _gn=b.s_g.n):
            gpsimd.wait_ge(b.s_v.h, _vn)
            gpsimd.wait_ge(b.s_a.h, _an)
            gpsimd.wait_ge(b.s_g.h, _gn)
        blk.gpsimd(gate_wob)
        WOB = ph.enter_context(nc.sbuf_tensor("WOB7", [128, NC8 * D], BF16))
        ZNB = ph.enter_context(nc.sbuf_tensor("ZNB7", [128, NC8 * 512], BF16))
        OP = ph.enter_context(nc.sbuf_tensor("OP7", [128, NC8 * 512], F32))
        OL = ph.enter_context(nc.sbuf_tensor("OL7", [128, NC8 * 512], F32))
        load_weight_1024(b, WOB, WoT, D, sem=b.s_w2)

        for it in range(S // 512):
            t0 = it * 512

            def ldz(gpsimd, _t0=t0, _vn=b.s_v.n, _stn=b.s_st.n):
                gpsimd.wait_ge(b.s_v.h, _vn)
                gpsimd.wait_ge(b.s_st.h, _stn)
                for kc in range(NC8):
                    ins = gpsimd.dma_start(
                        out=ZNB[:, kc * 512:(kc + 1) * 512],
                        in_=zT_h[kc * 128:(kc + 1) * 128, _t0:_t0 + 512])
                    b.s_ldg.inc(ins, 16)
            blk.gpsimd(ldz)

            def ldx(sync, _t0=t0, _vn=b.s_v.n):
                sync.wait_ge(b.s_v.h, _vn)
                for kc in range(NC8):
                    ins = sync.dma_start(
                        out=XC[:, kc * 512:(kc + 1) * 512],
                        in_=xres[kc * 128:(kc + 1) * 128, _t0:_t0 + 512])
                    b.s_ld.inc(ins, 16)
            blk.sync(ldx)

            rhs = lambda kc: ZNB[:, kc * 512:(kc + 1) * 512]
            plan = mm_set(b, WOB, rhs, 512, NC8, NC8, NC8 * 128,
                          (b.s_ldg, b.s_w2), b.s_v)

            def v_o(vector, _plan=plan):
                for oc, ps, n in _plan:
                    vector.wait_ge(b.s_mm.h, n)
                    ins = vector.tensor_scalar_add(
                        OP[:, oc * 512:(oc + 1) * 512], ps[:, 0:512],
                        b.v_bo[:, oc:oc + 1])
                    b.s_v.inc(ins)
            blk.vector(v_o)

            emit_ln(b, lambda kc: OP[:, kc * 512:(kc + 1) * 512],
                    lambda kc: OL[:, kc * 512:(kc + 1) * 512], 512,
                    wait_sems=(b.s_ld,),
                    add_src=lambda kc: XC[:, kc * 512:(kc + 1) * 512])

            def sto(sync, _t0=t0, _v=b.s_v.n):
                sync.wait_ge(b.s_v.h, _v)
                for kc in range(NC8):
                    ins = sync.dma_start(
                        out=xres[kc * 128:(kc + 1) * 128, _t0:_t0 + 512],
                        in_=OL[:, kc * 512:(kc + 1) * 512])
                    b.s_st.inc(ins, 16)
            blk.sync(sto)



def p8_tile(b, xres, y_out, XC, XLN, XNB, STB, ST1, ST2, WT1, WT2, WT3,
            t0, TT):
    blk = b.blk

    def ld(sync, _vn=b.s_v.n, _an=b.s_a.n):
        sync.wait_ge(b.s_v.h, _vn)
        sync.wait_ge(b.s_a.h, _an)
        for kc in range(NC8):
            ins = sync.dma_start(out=XC[:, kc * TT:(kc + 1) * TT],
                                 in_=xres[kc * 128:(kc + 1) * 128, t0:t0 + TT])
            b.s_ld.inc(ins, 16)
    blk.sync(ld)

    emit_ln(b, lambda kc: XC[:, kc * TT:(kc + 1) * TT],
            lambda kc: XLN[:, kc * TT:(kc + 1) * TT], TT,
            wait_sems=(b.s_ld,),
            out_bf=lambda kc: XNB[:, kc * TT:(kc + 1) * TT])

    for wi, (WT, srcbuf, dstbuf) in enumerate(((WT1, XNB, STB),
                                               (WT2, STB, XNB),
                                               (WT3, XNB, STB))):
        rhs = lambda kc, _s=srcbuf: _s[:, kc * TT:(kc + 1) * TT]
        last = (wi == 2)
        dsem = b.s_a if last else b.s_v
        plan = mm_set(b, WT, rhs, TT, NC8, NC8, NC8 * 128, (b.s_v, b.s_w),
                      dsem)
        if not last:
            def v_c(vector, _dst=dstbuf, _plan=plan):
                for oc, ps, n in _plan:
                    vector.wait_ge(b.s_mm.h, n)
                    ins = vector.tensor_copy(
                        _dst[:, oc * TT:(oc + 1) * TT], ps[:, 0:TT])
                    b.s_v.inc(ins)
            blk.vector(v_c)
        else:
            def a_c(scalar, _plan=plan):
                for oc, ps, n in _plan:
                    scalar.wait_ge(b.s_mm.h, n)
                    ins = scalar.activation(
                        ST1[:, oc * TT:(oc + 1) * TT], ps[:, 0:TT],
                        AF.Tanh, bias=b.v_l1b[:, oc:oc + 1])
                    b.s_a.inc(ins)
            blk.scalar(a_c)

    def w_t(vector, _a=b.s_a.n):
        vector.wait_ge(b.s_a.h, _a)
        ins = vector.engine_nop()
        b.s_v.inc(ins)
    blk.vector(w_t)

    emit_ln(b, lambda kc: ST1[:, kc * TT:(kc + 1) * TT],
            lambda kc: ST2[:, kc * TT:(kc + 1) * TT], TT,
            add_src=lambda kc: XC[:, kc * TT:(kc + 1) * TT])

    def st(sync, _v=b.s_v.n):
        sync.wait_ge(b.s_v.h, _v)
        for kc in range(NC8):
            ins = sync.dma_start(out=y_out[kc * 128:(kc + 1) * 128, t0:t0 + TT],
                                 in_=ST2[:, kc * TT:(kc + 1) * TT])
            b.s_st.inc(ins, 16)
    blk.sync(st)



def make_masks(QTILE):
    NAL = QTILE // 128
    m = np.full((NAL * 128, QTILE), -1e30, np.float32)
    for al in range(NAL):
        for p in range(128):
            k = al * 128 + p
            m[al * 128 + p, k:] = 0.0
    return m


def make_in_map(inputs, bidx, S=4096, QTILE=512):
    T = np.ascontiguousarray
    x = np.asarray(inputs["L0_attn_x"])[bidx]
    Wq = np.asarray(inputs["L1_attn_W_qkv"])
    Wq3 = Wq.reshape(NH, DHD, 3, D)
    Wq_de = np.concatenate([Wq3[:, :, c, :].reshape(NH * DHD, D)
                            for c in range(3)], axis=0)
    bq = np.asarray(inputs["L1_attn_b_qkv"]).reshape(NH, DHD, 3)
    bq_de = np.concatenate([bq[:, :, c].reshape(-1) for c in range(3)])

    m = {
        "x": T(x.T),
        "gamma": np.asarray(inputs["L0_attn_gamma"]),
        "gWiT": T(np.asarray(inputs["L0_attn_gWi"]).T),
        "gWifT": T(np.asarray(inputs["L0_attn_gWif"]).T),
        "gWhfT": T(np.asarray(inputs["L0_attn_gWhf"]).T),
        "gbhf": np.asarray(inputs["L0_attn_gbhf"]),
        "w1T": T(np.asarray(inputs["L0_ffn_w1_w"]).T),
        "w2T": T(np.asarray(inputs["L0_ffn_w2_w"]).T),
        "h1T": T(np.asarray(inputs["L0_ffn_h1_w"]).T),
        "h2T": T(np.asarray(inputs["L0_ffn_h2_w"]).T),
        "w3T": T(np.asarray(inputs["L0_ffn_w3_w"]).T),
        "w1s": np.asarray(inputs["L0_ffn_w1_s"]),
        "w2s": np.asarray(inputs["L0_ffn_w2_s"]),
        "h1s": np.asarray(inputs["L0_ffn_h1_s"]),
        "h2s": np.asarray(inputs["L0_ffn_h2_s"]),
        "w3s": np.asarray(inputs["L0_ffn_w3_s"]),
        "WqkvT": T(Wq_de.T),
        "bqkv": T(bq_de),
        "WoT": T(np.asarray(inputs["L1_attn_W_o"]).T),
        "bo": np.asarray(inputs["L1_attn_b_o"]),
        "l1w": np.asarray(inputs["L1_ffn_weight"]),
        "l1a1": np.asarray(inputs["L1_ffn_aux_weight1"]),
        "l1a2": np.asarray(inputs["L1_ffn_aux_weight2"]),
        "l1bias": np.asarray(inputs["L1_ffn_bias"]),
        "amask": make_masks(QTILE),
    }
    for i in range(2):
        m[f"lWiT{i}"] = T(np.asarray(inputs["L0_attn_lWi"])[i].T)
        m[f"lWifT{i}"] = T(np.asarray(inputs["L0_attn_lWif"])[i].T)
        m[f"lWhfT{i}"] = T(np.asarray(inputs["L0_attn_lWhf"])[i].T)
        m[f"lbhf{i}"] = np.asarray(inputs["L0_attn_lbhf"])[i]
        m[f"linit{i}"] = np.asarray(inputs["L0_attn_l_init_hidden"])[i]
    return {k: np.ascontiguousarray(np.asarray(v, dtype=np.float32))
            for k, v in m.items()}


_CACHED = {}


def kernel(**inputs):
    S = 4096
    if "nc" not in _CACHED:
        _CACHED["nc"] = build_program(S=S)
    nc = _CACHED["nc"]
    in_maps = [make_in_map(inputs, c % 4, S=S) for c in range(8)]
    res = run_bass_kernel_spmd(nc, in_maps, core_ids=list(range(8)))
    out = np.stack([res.results[bidx]["y"].T for bidx in range(4)])
    return out.astype(np.float32)


# version 06 (final)
# speedup vs baseline: 1.4378x; 1.1447x over previous
import contextlib

import numpy as np
import ml_dtypes
import concourse.bass as bass
import concourse.mybir as mybir
from concourse.bass_types import DynSlice
from concourse.bass_utils import run_bass_kernel_spmd

F32 = mybir.dt.float32
BF16 = mybir.dt.bfloat16
AF = mybir.ActivationFunctionType
ALU = mybir.AluOpType

D = 1024
NC8 = 8
DH = 4096
NH = 16
DHD = 64
LN_EPS = 1e-5
RMS_EPS = 1.1920929e-07


class Sem:
    def __init__(self, es, nc, name):
        self.h = es.enter_context(nc.semaphore(name))
        self.n = 0

    def inc(self, instr, k=1):
        instr.then_inc(self.h, k)
        self.n += k
        return instr

    def wait(self, eng, val=None):
        eng.wait_ge(self.h, self.n if val is None else val)


class PsumPool:

    def __init__(self, b, tiles, name):
        self.tiles = tiles
        self.pending = [None] * len(tiles)
        self.count = 0

    def acquire(self, tensor):
        slot = self.count % len(self.tiles)
        self.count += 1
        if self.pending[slot] is not None:
            sem, val = self.pending[slot]
            tensor.wait_ge(sem.h, val)
            self.pending[slot] = None
        return self.tiles[slot], slot

    def expect(self, slot, sem, val):
        self.pending[slot] = (sem, val)


class B:
    pass


def wait_reg(b, eng, sem, iv, const, lo, hi):
    from concourse.expressions import RuntimeValue
    from concourse.bass_primitives_rust import RegisterHandles
    et = eng.engine
    reg = b.wregs.get(et)
    if reg is None:
        reg = b.nc.alloc_register(et, f"wreg{len(b.wregs)}")
        b.wregs[et] = reg
    eng.reg_add(reg, iv, const)
    eng.wait_ge(sem.h, reg)


@contextlib.contextmanager
def fori(nc, eng, start, end, step=1):
    from concourse.ordered_engine_set import OrderedEngineSet
    engines = OrderedEngineSet([eng.engine])
    loop_id = nc.next_id()
    loop_start = f"ffori_{loop_id}_loop"
    loop_end = f"ffori_{loop_id}_end"
    regs = nc.alloc_registers(f"ffori_{loop_id}_i", engines=engines)
    nc.regs_mov(regs, start)

    nc.br(loop_start, engines=engines)
    iv = None
    with nc.body(loop_start, valid_engines=engines):
        iv = nc.snap(regs, min_val=start, max_val=end - 1)
        yield iv
        nc.regs_alu(regs, regs, step, op=ALU.add)
        nc.br_lt(regs, end, on_true=loop_start, on_false=loop_end,
                 engines=engines)
    nc.switch_bb(loop_end)
    try:
        for h in iv.val.handles:
            nc.free_register(h)
    except Exception:
        pass
    for h in regs:
        nc.free_register(h)


def barrier(b):
    blk, s = b.blk, b.bar
    dma_waits = [(sem, sem.n) for sem in
                 (b.s_st, b.s_hs, b.s_ld, b.s_ldg, b.s_ac, b.s_w,
                  b.s_ac2[0], b.s_ac2[1], b.s_acb[0], b.s_acb[1],
                  b.s_hs2[0], b.s_hs2[1], b.s_wb[0], b.s_wb[1], b.s_w2)]

    def _inc_sync(eng, _w=dma_waits, _s=s):
        for sem, n in _w:
            eng.wait_ge(sem.h, n)
        eng.drain()
        eng.sem_inc(_s.h, 1)
    blk.sync(_inc_sync)

    def _inc(eng, _s=s):
        eng.drain()
        eng.sem_inc(_s.h, 1)
    for name in ("tensor", "vector", "scalar", "gpsimd"):
        getattr(blk, name)(_inc)
    s.n += 5
    t = s.n
    for name in ("sync", "tensor", "vector", "scalar", "gpsimd"):
        getattr(blk, name)(lambda eng, _s=s, _t=t: eng.wait_ge(_s.h, _t))


def load_weight_1024(b, dst, src, n_out_cols, sem=None):
    s = sem if sem is not None else b.s_w
    def go(gpsimd):
        ins = gpsimd.dma_start(
            out=dst[:, 0:NC8 * n_out_cols].rearrange("p (kc o) -> p kc o",
                                                     o=n_out_cols),
            in_=src.rearrange("(kc p) o -> p kc o", p=128))
        s.inc(ins, 16)
    b.blk.gpsimd(go)


def mm_acc(b, tensor, lhsT_sb, rhs, tt, oc, n_k, ps, stride):
    for kc in range(n_k):
        col = kc * stride + oc * 128
        ins = tensor.matmul(ps[:, 0:tt], lhsT_sb[:, col:col + 128], rhs(kc),
                            start=(kc == 0), stop=(kc == n_k - 1))
    return ins


def emit_stats(b, src, n_chunks, tt, rows, wait_sems=()):
    blk = b.blk
    waits = list(wait_sems) + [b.s_v]
    for row, kind in rows:
        if kind == "sumsq":
            wl = [(w, w.n) for w in waits]

            def v_sq(vector, _wl=wl, _tt=tt, _n=n_chunks, _src=src):
                for w, n in _wl:
                    w.wait(vector, n)
                for kc in range(_n):
                    ins = vector.tensor_mul(b.SQ[:, kc * _tt:(kc + 1) * _tt],
                                            _src(kc), _src(kc))
                b.s_sq.inc(ins)
            blk.vector(v_sq)

            def t_sq(tensor, _row=row, _tt=tt, _n=n_chunks, _sq=b.s_sq.n):
                tensor.wait_ge(b.s_sq.h, _sq)
                for kc in range(_n):
                    ins = tensor.matmul(b.PSST[_row:_row + 1, 0:_tt],
                                        b.ones_col[:, 0:1],
                                        b.SQ[:, kc * _tt:(kc + 1) * _tt],
                                        start=(kc == 0), stop=(kc == _n - 1))
                b.s_mm.inc(ins)
            blk.tensor(t_sq)
        else:
            wl = [(w, w.n) for w in waits]

            def t_sum(tensor, _wl=wl, _row=row, _tt=tt, _n=n_chunks, _src=src):
                for w, n in _wl:
                    w.wait(tensor, n)
                for kc in range(_n):
                    ins = tensor.matmul(b.PSST[_row:_row + 1, 0:_tt],
                                        b.ones_col[:, 0:1], _src(kc),
                                        start=(kc == 0), stop=(kc == _n - 1))
                b.s_mm.inc(ins)
            blk.tensor(t_sum)


def bcast(b, tensor, row_ap, ps, tt):
    return tensor.matmul(ps[:, 0:tt], b.ones_row[0:1, :], row_ap,
                         start=True, stop=True)


def emit_ln(b, src, out, tt, n_chunks=NC8, out_bf=None, wait_sems=(),
            add_src=None, eps=LN_EPS):
    blk = b.blk
    emit_stats(b, src, n_chunks, tt, [(0, "sum"), (32, "sumsq")], wait_sems)
    nfeat = float(128 * n_chunks)

    def v_rows(vector, _tt=tt, _mm=b.s_mm.n):
        vector.wait_ge(b.s_mm.h, _mm)
        vector.tensor_scalar_mul(b.row_m[0:1, 0:_tt], b.PSST[0:1, 0:_tt], 1.0 / nfeat)
        vector.tensor_scalar_mul(b.row_a[0:1, 0:_tt], b.PSST[32:33, 0:_tt], 1.0 / nfeat)
        vector.drain()
        vector.tensor_mul(b.row_b[0:1, 0:_tt], b.row_m[0:1, 0:_tt],
                          b.row_m[0:1, 0:_tt])
        vector.drain()
        ins = vector.tensor_sub(b.row_a[0:1, 0:_tt], b.row_a[0:1, 0:_tt],
                                b.row_b[0:1, 0:_tt])
        b.s_v.inc(ins)
    blk.vector(v_rows)

    def a_rstd(scalar, _tt=tt, _v=b.s_v.n):
        scalar.wait_ge(b.s_v.h, _v)
        ins = scalar.activation(b.row_b[0:1, 0:_tt], b.row_a[0:1, 0:_tt],
                                AF.Sqrt, bias=float(eps))
        b.s_a.inc(ins)
    blk.scalar(a_rstd)

    def v_recip(vector, _tt=tt, _a=b.s_a.n):
        vector.wait_ge(b.s_a.h, _a)
        vector.drain()
        ins = vector.reciprocal(b.row_b[0:1, 0:_tt], b.row_b[0:1, 0:_tt])
        b.s_v.inc(ins)
    blk.vector(v_recip)

    def t_bc(tensor, _tt=tt, _a=b.s_v.n):
        tensor.wait_ge(b.s_v.h, _a)
        bcast(b, tensor, b.row_m[0:1, 0:_tt], b.PSB, _tt)
        ins = bcast(b, tensor, b.row_b[0:1, 0:_tt], b.PSB2, _tt)
        b.s_mm.inc(ins)
    blk.tensor(t_bc)

    def v_norm(vector, _tt=tt, _n=n_chunks, _mm=b.s_mm.n):
        vector.wait_ge(b.s_mm.h, _mm)
        for kc in range(_n):
            vector.tensor_sub(out(kc), src(kc), b.PSB[:, 0:_tt])
        vector.drain()
        for kc in range(_n):
            ins = vector.tensor_mul(out(kc), out(kc), b.PSB2[:, 0:_tt])
        if add_src is not None:
            vector.drain()
            for kc in range(_n):
                ins = vector.tensor_add(out(kc), out(kc), add_src(kc))
        if out_bf is not None:
            vector.drain()
            for kc in range(_n):
                ins = vector.tensor_copy(out_bf(kc), out(kc))
        b.s_v.inc(ins)
    blk.vector(v_norm)


def mm_set(b, WT, rhs, TT, n_out, n_k, stride, pre_waits, data_sem,
           incs_per_oc=1):
    plan = []
    base = data_sem.n

    def t_go(tensor, _pw=[(w, w.n) for w in pre_waits]):
        for w, n in _pw:
            w.wait(tensor, n)
        for oc in range(n_out):
            ps, slot = b.pool.acquire(tensor)
            b.pool.expect(slot, data_sem, base + incs_per_oc * oc + 1)
            ins = mm_acc(b, tensor, WT, rhs, TT, oc, n_k, ps, stride)
            b.s_mm.inc(ins)
            plan.append((oc, ps, b.s_mm.n))
    b.blk.tensor(t_go)
    return plan



def build_program(S=4096, CH=1024, TT=512, TTF=512, QTILE=512, phases=99,
                  debug=False, scan_stub=False, scan_no_hist=False,
                  scan_no_mm=False):
    nc = bass.Bass("TRN2", target_bir_lowering=False)
    es = contextlib.ExitStack()
    nc._keepalive_es = es
    b = B()
    b.nc, b.es, b.S = nc, es, S
    b.wregs = {}
    b.scan_no_hist, b.scan_no_mm = scan_no_hist, scan_no_mm
    NT = S // TT
    NAL = QTILE // 128
    dk = "ExternalOutput" if debug else "Internal"

    def din(name, shape, dt=F32):
        return nc.dram_tensor(name, shape, dt, kind="ExternalInput").ap()

    def dint(name, shape, dt=F32):
        return nc.dram_tensor(name, shape, dt, kind=dk).ap()

    x_in = din("x", [D, S])
    gamma = din("gamma", [D])
    lWiT = [din(f"lWiT{i}", [D, D]) for i in range(2)]
    lWifT = [din(f"lWifT{i}", [D, D]) for i in range(2)]
    lWhfT = [din(f"lWhfT{i}", [D, D]) for i in range(2)]
    lbhf = [din(f"lbhf{i}", [D]) for i in range(2)]
    linit = [din(f"linit{i}", [D]) for i in range(2)]
    gWiT = din("gWiT", [D, D]); gWifT = din("gWifT", [D, D])
    gWhfT = din("gWhfT", [D, D]); gbhf = din("gbhf", [D])
    w1T = din("w1T", [D, DH]); w2T = din("w2T", [D, DH])
    h1T = din("h1T", [DH, DH]); h2T = din("h2T", [DH, DH])
    w3T = din("w3T", [DH, D])
    w1s = din("w1s", [DH]); w2s = din("w2s", [DH])
    h1s = din("h1s", [DH]); h2s = din("h2s", [DH])
    w3s = din("w3s", [D])
    WqkvT = din("WqkvT", [D, 3 * D])
    bqkv = din("bqkv", [3 * D])
    WoT = din("WoT", [D, D])
    bo = din("bo", [D])
    l1w = din("l1w", [D, D]); l1a1 = din("l1a1", [D, D]); l1a2 = din("l1a2", [D, D])
    l1bias = din("l1bias", [D])
    amask = din("amask", [NAL * 128, QTILE])

    y_out = nc.dram_tensor("y", [D, S], F32, kind="ExternalOutput").ap()

    xres = dint("xres", [D, S])
    xn_h = dint("xn_h", [D, S])
    xln_h = dint("xln_h", [D, S])
    A_h = [dint(f"A_h{i}", [D, S]) for i in range(2)]
    C_h = [dint(f"C_h{i}", [D, S]) for i in range(2)]
    hs_h = [dint(f"hs_h{i}", [D, S], BF16) for i in range(2)]
    qT_h = dint("qT_h", [D, S], BF16)
    kT_h = dint("kT_h", [D, S], BF16)
    v_h = dint("v_h", [NH, S, DHD])
    zT_h = dint("zT_h", [D, S], BF16)

    sb = nc.alloc_sbuf_tensor
    b.ones_col = sb("ones_col", [128, 1], F32)
    b.ones_row = sb("ones_row", [1, 128], F32)
    b.row_m = sb("row_m", [1, 512], F32)
    b.row_a = sb("row_a", [1, 512], F32)
    b.row_b = sb("row_b", [1, 512], F32)
    b.g1 = sb("g1", [128, 8], F32)
    b.v_lbhf = sb("v_lbhf", [128, 16], F32)
    b.v_linit = sb("v_linit", [128, 16], F32)
    b.v_gbhf = sb("v_gbhf", [128, 8], F32)
    b.v_w1s = sb("v_w1s", [128, 32], F32)
    b.v_w2s = sb("v_w2s", [128, 32], F32)
    b.v_h1s = sb("v_h1s", [128, 32], F32)
    b.v_h2s = sb("v_h2s", [128, 32], F32)
    b.v_w3s = sb("v_w3s", [128, 8], F32)
    b.v_bqkv = sb("v_bqkv", [128, 24], F32)
    b.v_bo = sb("v_bo", [128, 8], F32)
    b.v_l1b = sb("v_l1b", [128, 8], F32)
    b.SQ = sb("SQ", [128, NC8 * 512], F32)

    b.cst = sb("cst", [1, 8], F32)
    b.dum = sb("dum", [1, 8], F32)
    b.cvals = {float(LN_EPS): 0, float(RMS_EPS): 1, float(1.0 / D): 2}
    for val, idx in b.cvals.items():
        nc.const_aps.register(F32, val, b.cst[0:1, idx:idx + 1])

    b.PS = [nc.alloc_psum_tensor(f"PS{i}", [128, 512], F32) for i in range(4)]
    b.PSST = nc.alloc_psum_tensor("PSST", [128, 512], F32)
    b.PSB = nc.alloc_psum_tensor("PSB", [128, 512], F32)
    b.PSB2 = nc.alloc_psum_tensor("PSB2", [128, 512], F32)
    b.PSS = nc.alloc_psum_tensor("PSS", [128, 512], F32)

    for nm in ("bar", "s_ld", "s_ldg", "s_st", "s_w", "s_mm", "s_sq", "s_v",
               "s_a", "s_g", "s_ac", "s_hs", "s_smm", "s_su", "s_sg", "s_sh"):
        setattr(b, nm, Sem(es, nc, nm))
    b.s_ac2 = [Sem(es, nc, f"s_ac2_{i}") for i in range(2)]
    b.s_acb = [Sem(es, nc, f"s_acb_{i}") for i in range(2)]
    b.s_hs2 = [Sem(es, nc, f"s_hs2_{i}") for i in range(2)]
    b.s_wb = [Sem(es, nc, f"s_wb_{i}") for i in range(2)]
    b.s_w2 = Sem(es, nc, "s_w2")

    with nc.Block() as blk:
        b.blk = blk
        b.pool = PsumPool(b, b.PS, "psf")

        def p0_sync(sync):
            pairs = [(b.g1[:, 0:8], gamma)]
            for i in range(2):
                pairs.append((b.v_lbhf[:, 8 * i:8 * i + 8], lbhf[i]))
                pairs.append((b.v_linit[:, 8 * i:8 * i + 8], linit[i]))
            pairs += [(b.v_gbhf[:, 0:8], gbhf), (b.v_w3s[:, 0:8], w3s),
                      (b.v_bo[:, 0:8], bo), (b.v_l1b[:, 0:8], l1bias),
                      (b.v_w1s[:, 0:32], w1s), (b.v_w2s[:, 0:32], w2s),
                      (b.v_h1s[:, 0:32], h1s), (b.v_h2s[:, 0:32], h2s),
                      (b.v_bqkv[:, 0:24], bqkv)]
            with nc.allow_non_contiguous_dma(reason="tiny per-feature vectors"):
                for dstap, src in pairs:
                    ins = sync.dma_start(out=dstap,
                                         in_=src.rearrange("(kc p) -> p kc", p=128))
                    b.s_ld.inc(ins, 16)
            ins = sync.dma_start(out=xres[:, :], in_=x_in[:, :])
            b.s_ld.inc(ins, 16)
        blk.sync(p0_sync)

        def p0_v(vector):
            b.s_ld.wait(vector)
            vector.memset(b.ones_col[:, :], 1.0)
            vector.memset(b.ones_row[:, :], 1.0)
            for val, idx in b.cvals.items():
                vector.memset(b.cst[0:1, idx:idx + 1], val)
            vector.tensor_scalar_add(b.g1[:, 0:8], b.g1[:, 0:8], 1.0)
        blk.vector(p0_v)
        barrier(b)

        if phases >= 1:
            with contextlib.ExitStack() as ph:
                XC = ph.enter_context(nc.sbuf_tensor("XC1", [128, NC8 * TT], F32))
                XLN = ph.enter_context(nc.sbuf_tensor("XLN1", [128, NC8 * TT], F32))
                XN = ph.enter_context(nc.sbuf_tensor("XN1", [128, NC8 * TT], F32))
                XNB = ph.enter_context(nc.sbuf_tensor("XNB1", [128, NC8 * TT], BF16))
                ST1 = ph.enter_context(nc.sbuf_tensor("STa1", [128, NC8 * TT], F32))
                ST2 = ph.enter_context(nc.sbuf_tensor("STb1", [128, NC8 * TT], F32))
                WT1 = ph.enter_context(nc.sbuf_tensor("WT1a", [128, NC8 * D], BF16))
                WT2 = ph.enter_context(nc.sbuf_tensor("WT1b", [128, NC8 * D], BF16))
                load_weight_1024(b, WT1, lWiT[0], D)
                load_weight_1024(b, WT2, lWifT[0], D)
                for it in range(NT):
                    p1_tile(b, x_in, xln_h, xn_h, A_h[0], C_h[0],
                            XC, XLN, XN, XNB, ST1, ST2, WT1, WT2, it * TT, TT)
            barrier(b)

        if phases >= 2:
            if scan_stub:
                def stub2(sync, _st=b.s_st.n):
                    sync.wait_ge(b.s_st.h, _st)
                    ins = sync.dma_start(out=hs_h[0][:, :], in_=A_h[0][:, :])
                    b.s_st.inc(ins, 16)
                b.blk.sync(stub2)
            else:
                emit_scan(b, lWhfT[0], A_h[0], C_h[0], hs_h[0], 0, S, CH)
            barrier(b)

        if phases >= 3:
            with contextlib.ExitStack() as ph:
                XNB = ph.enter_context(nc.sbuf_tensor("XNB3", [128, NC8 * TT], BF16))
                ST1 = ph.enter_context(nc.sbuf_tensor("STa3", [128, NC8 * TT], F32))
                ST2 = ph.enter_context(nc.sbuf_tensor("STb3", [128, NC8 * TT], F32))
                WT1 = ph.enter_context(nc.sbuf_tensor("WT3a", [128, NC8 * D], BF16))
                WT2 = ph.enter_context(nc.sbuf_tensor("WT3b", [128, NC8 * D], BF16))
                load_weight_1024(b, WT1, lWiT[1], D)
                load_weight_1024(b, WT2, lWifT[1], D)
                for it in range(NT):
                    t0 = it * TT

                    def p3_ld(gpsimd, _t0=t0, _vn=b.s_v.n, _an=b.s_a.n):
                        gpsimd.wait_ge(b.s_v.h, _vn)
                        gpsimd.wait_ge(b.s_a.h, _an)
                        for kc in range(NC8):
                            ins = gpsimd.dma_start(
                                out=XNB[:, kc * TT:(kc + 1) * TT],
                                in_=hs_h[0][kc * 128:(kc + 1) * 128, _t0:_t0 + TT])
                            b.s_ldg.inc(ins, 16)
                    blk.gpsimd(p3_ld)
                    mixer_pre(b, XNB, ST1, ST2, WT1, WT2, A_h[1], C_h[1], 1,
                              t0, TT, in_sems=(b.s_ldg, b.s_w))
            barrier(b)

        if phases >= 4:
            emit_scan(b, lWhfT[1], A_h[1], C_h[1], hs_h[1], 1, S, CH)
            barrier(b)

        if phases >= 5:
            with contextlib.ExitStack() as ph:
                XC = ph.enter_context(nc.sbuf_tensor("XC5", [128, NC8 * TT], F32))
                XLN = ph.enter_context(nc.sbuf_tensor("XLN5", [128, NC8 * TT], F32))
                XN = ph.enter_context(nc.sbuf_tensor("XN5", [128, NC8 * TT], F32))
                XNB = ph.enter_context(nc.sbuf_tensor("XNB5", [128, NC8 * TT], BF16))
                H1B = ph.enter_context(nc.sbuf_tensor("H1B5", [128, NC8 * TT], BF16))
                ST1 = ph.enter_context(nc.sbuf_tensor("STa5", [128, NC8 * TT], F32))
                ST2 = ph.enter_context(nc.sbuf_tensor("STb5", [128, NC8 * TT], F32))
                GS = ph.enter_context(nc.sbuf_tensor("GS5", [128, NC8 * TT], F32))
                WT1 = ph.enter_context(nc.sbuf_tensor("WT5a", [128, NC8 * D], BF16))
                WT2 = ph.enter_context(nc.sbuf_tensor("WT5b", [128, NC8 * D], BF16))
                WT3 = ph.enter_context(nc.sbuf_tensor("WT5c", [128, NC8 * D], BF16))
                load_weight_1024(b, WT1, gWiT, D)
                load_weight_1024(b, WT2, gWifT, D)
                load_weight_1024(b, WT3, gWhfT, D)
                for it in range(NT):
                    p5_tile(b, hs_h[1], xn_h, xln_h, x_in, xres, XC, XLN, XN,
                            XNB, H1B, ST1, ST2, GS, WT1, WT2, WT3, it * TT, TT)
            barrier(b)

        if phases >= 6:
            emit_p6(b, xres, w1T, w2T, h1T, h2T, w3T, S, TTF)
            barrier(b)

        if phases >= 7:
            emit_p7(b, xres, WqkvT, WoT, bqkv, qT_h, kT_h, v_h, zT_h, amask,
                    S, TT, QTILE)
            barrier(b)

        if phases >= 8:
            with contextlib.ExitStack() as ph:
                XC = ph.enter_context(nc.sbuf_tensor("XC8", [128, NC8 * TT], F32))
                XLN = ph.enter_context(nc.sbuf_tensor("XLN8", [128, NC8 * TT], F32))
                XNB = ph.enter_context(nc.sbuf_tensor("XNB8", [128, NC8 * TT], BF16))
                STB = ph.enter_context(nc.sbuf_tensor("STB8", [128, NC8 * TT], BF16))
                ST1 = ph.enter_context(nc.sbuf_tensor("STa8", [128, NC8 * TT], F32))
                ST2 = ph.enter_context(nc.sbuf_tensor("STb8", [128, NC8 * TT], F32))
                WT1 = ph.enter_context(nc.sbuf_tensor("WT8a", [128, NC8 * D], BF16))
                WT2 = ph.enter_context(nc.sbuf_tensor("WT8b", [128, NC8 * D], BF16))
                WT3 = ph.enter_context(nc.sbuf_tensor("WT8c", [128, NC8 * D], BF16))
                load_weight_1024(b, WT1, l1w, D)
                load_weight_1024(b, WT2, l1a1, D)
                load_weight_1024(b, WT3, l1a2, D)
                for it in range(NT):
                    p8_tile(b, xres, y_out, XC, XLN, XNB, STB, ST1, ST2,
                            WT1, WT2, WT3, it * TT, TT)
        elif phases < 8:
            def cp_out(sync, _vn=b.s_v.n):
                sync.wait_ge(b.s_v.h, _vn)
                ins = sync.dma_start(out=y_out[:, :], in_=xres[:, :])
                b.s_st.inc(ins, 16)
            blk.sync(cp_out)

        def fin(gpsimd, _n=b.s_st.n):
            gpsimd.wait_ge(b.s_st.h, _n)
        blk.gpsimd(fin)

    return nc



def p1_tile(b, x_in, xln_h, xn_h, A_hb, C_hb, XC, XLN, XN, XNB, ST1, ST2,
            WT1, WT2, t0, TT):
    blk = b.blk

    def ld(sync, _vn=b.s_v.n, _an=b.s_a.n):
        sync.wait_ge(b.s_v.h, _vn)
        sync.wait_ge(b.s_a.h, _an)
        for kc in range(NC8):
            ins = sync.dma_start(out=XC[:, kc * TT:(kc + 1) * TT],
                                 in_=x_in[kc * 128:(kc + 1) * 128, t0:t0 + TT])
            b.s_ld.inc(ins, 16)
    blk.sync(ld)

    src = lambda kc: XC[:, kc * TT:(kc + 1) * TT]
    oln = lambda kc: XLN[:, kc * TT:(kc + 1) * TT]
    emit_ln(b, src, oln, TT, wait_sems=(b.s_ld,))
    emit_stats(b, oln, NC8, TT, [(0, "sumsq")])

    def v_mx(vector, _mm=b.s_mm.n):
        vector.wait_ge(b.s_mm.h, _mm)
        ins = vector.tensor_scalar_max(b.row_a[0:1, 0:TT], b.PSST[0:1, 0:TT],
                                       1e-24)
        b.s_v.inc(ins)
    blk.vector(v_mx)

    def a_rs(scalar, _v=b.s_v.n):
        scalar.wait_ge(b.s_v.h, _v)
        ins = scalar.activation(b.row_b[0:1, 0:TT], b.row_a[0:1, 0:TT],
                                AF.Sqrt, scale=1.0 / D)
        b.s_a.inc(ins)
    blk.scalar(a_rs)

    def v_rec2(vector, _a=b.s_a.n):
        vector.wait_ge(b.s_a.h, _a)
        vector.drain()
        ins = vector.reciprocal(b.row_b[0:1, 0:TT], b.row_b[0:1, 0:TT])
        b.s_v.inc(ins)
    blk.vector(v_rec2)

    def t_bc(tensor, _a=b.s_v.n):
        tensor.wait_ge(b.s_v.h, _a)
        ins = bcast(b, tensor, b.row_b[0:1, 0:TT], b.PSB, TT)
        b.s_mm.inc(ins)
    blk.tensor(t_bc)

    def v_xn(vector, _mm=b.s_mm.n):
        vector.wait_ge(b.s_mm.h, _mm)
        for kc in range(NC8):
            vector.tensor_mul(XN[:, kc * TT:(kc + 1) * TT],
                              XLN[:, kc * TT:(kc + 1) * TT], b.PSB[:, 0:TT])
        vector.drain()
        for kc in range(NC8):
            vector.tensor_scalar_mul(XN[:, kc * TT:(kc + 1) * TT],
                                     XN[:, kc * TT:(kc + 1) * TT],
                                     b.g1[:, kc:kc + 1])
        vector.drain()
        for kc in range(NC8):
            ins = vector.tensor_copy(XNB[:, kc * TT:(kc + 1) * TT],
                                     XN[:, kc * TT:(kc + 1) * TT])
        b.s_v.inc(ins)
    blk.vector(v_xn)

    def st(sync, _v=b.s_v.n):
        sync.wait_ge(b.s_v.h, _v)
        for kc in range(NC8):
            i1 = sync.dma_start(out=xln_h[kc * 128:(kc + 1) * 128, t0:t0 + TT],
                                in_=XLN[:, kc * TT:(kc + 1) * TT])
            b.s_st.inc(i1, 16)
            i2 = sync.dma_start(out=xn_h[kc * 128:(kc + 1) * 128, t0:t0 + TT],
                                in_=XN[:, kc * TT:(kc + 1) * TT])
            b.s_st.inc(i2, 16)
    blk.sync(st)

    mixer_pre(b, XNB, ST1, ST2, WT1, WT2, A_hb, C_hb, 0, t0, TT,
              in_sems=(b.s_v, b.s_w))


def mixer_pre(b, SRCB, ST1, ST2, WT1, WT2, A_hb, C_hb, layer, t0, TT,
              in_sems=()):
    blk = b.blk
    rhs = lambda kc: SRCB[:, kc * TT:(kc + 1) * TT]

    plan = mm_set(b, WT1, rhs, TT, NC8, NC8, NC8 * 128, in_sems, b.s_a,
                  incs_per_oc=2)

    def a_t(scalar):
        for oc, ps, n in plan:
            scalar.wait_ge(b.s_mm.h, n)
            i1 = scalar.activation(ST1[:, oc * TT:(oc + 1) * TT],
                                   ps[:, 0:TT], AF.Tanh)
            b.s_a.inc(i1)
            scalar.drain()
            i2 = scalar.activation(ST1[:, oc * TT:(oc + 1) * TT],
                                   ST1[:, oc * TT:(oc + 1) * TT], AF.Tanh)
            b.s_a.inc(i2)
    blk.scalar(a_t)

    def st_a(sync, _a=b.s_a.n):
        sync.wait_ge(b.s_a.h, _a)
        for oc in range(NC8):
            ins = sync.dma_start(out=A_hb[oc * 128:(oc + 1) * 128, t0:t0 + TT],
                                 in_=ST1[:, oc * TT:(oc + 1) * TT])
            b.s_st.inc(ins, 16)
    blk.sync(st_a)

    plan2 = mm_set(b, WT2, rhs, TT, NC8, NC8, NC8 * 128, (), b.s_v)

    def v_c(vector):
        for oc, ps, n in plan2:
            vector.wait_ge(b.s_mm.h, n)
            ins = vector.tensor_scalar_add(
                ST2[:, oc * TT:(oc + 1) * TT], ps[:, 0:TT],
                b.v_lbhf[:, 8 * layer + oc:8 * layer + oc + 1])
            b.s_v.inc(ins)
    blk.vector(v_c)

    def st_c(sync, _v=b.s_v.n):
        sync.wait_ge(b.s_v.h, _v)
        for oc in range(NC8):
            ins = sync.dma_start(out=C_hb[oc * 128:(oc + 1) * 128, t0:t0 + TT],
                                 in_=ST2[:, oc * TT:(oc + 1) * TT])
            b.s_st.inc(ins, 16)
    blk.sync(st_c)



def emit_scan(b, WhfT, A_hb, C_hb, hs_hb, layer, S, CH):
    blk, nc = b.blk, b.nc
    n_chunks = S // CH

    with contextlib.ExitStack() as ph:
        WSC = ph.enter_context(nc.sbuf_tensor(f"WSC{layer}", [128, 64 * 128], BF16))
        A_sb = [ph.enter_context(nc.sbuf_tensor(f"scA{layer}_{i}", [128, NC8 * CH], F32)) for i in range(2)]
        C_sb = [ph.enter_context(nc.sbuf_tensor(f"scC{layer}_{i}", [128, NC8 * CH], F32)) for i in range(2)]
        H_sb = [ph.enter_context(nc.sbuf_tensor(f"scH{layer}_{i}", [128, NC8 * CH], BF16)) for i in range(2)]
        hcur = ph.enter_context(nc.sbuf_tensor(f"hcur{layer}", [128, 8], F32))
        hbf = ph.enter_context(nc.sbuf_tensor(f"hbf{layer}", [128, 8], BF16))
        ut = ph.enter_context(nc.sbuf_tensor(f"ut{layer}", [128, 8], F32))
        gt = ph.enter_context(nc.sbuf_tensor(f"gt{layer}", [128, 8], F32))
        et = ph.enter_context(nc.sbuf_tensor(f"et{layer}", [128, 8], F32))

        load_weight_1024(b, WSC, WhfT, 128 * NC8)

        base_mm, base_h = b.s_smm.n, b.s_sh.n
        base_u, base_g = b.s_su.n, b.s_sg.n
        base_ac, base_hs = b.s_ac.n, b.s_hs.n

        def vinit(vector, _w=b.s_w.n):
            vector.wait_ge(b.s_w.h, _w)
            vector.tensor_copy(hcur[:, :], b.v_linit[:, 8 * layer:8 * layer + 8])
            vector.drain()
            ins = vector.tensor_copy(hbf[:, :], hcur[:, :])
            ins.then_inc(b.s_sh.h, 1)
        blk.vector(vinit)
        b.s_sh.n += 1

        base_ac2 = [b.s_ac2[i].n for i in range(2)]
        base_acb = [b.s_acb[i].n for i in range(2)]
        base_hs2 = [b.s_hs2[i].n for i in range(2)]
        st_base = b.s_st.n

        H2 = CH // 2

        def dma_ac(sync):
            sync.wait_ge(b.s_st.h, st_base)
            for c in range(n_chunks):
                for half in range(2):
                    if c >= 2:
                        if half == 0:
                            sync.wait_ge(b.s_sh.h,
                                         base_h + 1 + (c - 2) * CH + H2)
                        else:
                            sync.wait_ge(b.s_sh.h, base_h + 1 + (c - 1) * CH)
                    sem = b.s_ac2[c % 2] if half == 0 else b.s_acb[c % 2]
                    for kc in range(NC8):
                        t0c = c * CH + half * H2
                        sb0 = kc * CH + half * H2
                        i1 = sync.dma_start(
                            out=A_sb[c % 2][:, sb0:sb0 + H2],
                            in_=A_hb[kc * 128:(kc + 1) * 128, t0c:t0c + H2])
                        sem.inc(i1, 16)
                        i2 = sync.dma_start(
                            out=C_sb[c % 2][:, sb0:sb0 + H2],
                            in_=C_hb[kc * 128:(kc + 1) * 128, t0c:t0c + H2])
                        sem.inc(i2, 16)
        blk.sync(dma_ac)

        def dma_hist(eng):
            if getattr(b, 'scan_no_hist', False):
                return
            for c in range(n_chunks):
                eng.wait_ge(b.s_sh.h, base_h + 1 + (c + 1) * CH)
                for kc in range(NC8):
                    ins = eng.dma_start(
                        out=hs_hb[kc * 128:(kc + 1) * 128, c * CH:(c + 1) * CH],
                        in_=H_sb[c % 2][:, kc * CH:(kc + 1) * CH])
                    b.s_hs2[c % 2].inc(ins, 16)
        blk.scalar(dma_hist) if getattr(b, 'scan_hist_on_scalar', False) else blk.sync(dma_hist)

        for c in range(n_chunks):
            def t_chunk(tensor, _c=c, _w=b.s_w.n):
                if getattr(b, 'scan_no_mm', False):
                    return
                if _c == 0:
                    tensor.wait_ge(b.s_w.h, _w)
                from concourse.expressions import RuntimeValue
                from concourse.bass_primitives_rust import RegisterHandles
                zr = b.wregs.get(("zero", tensor.engine))
                if zr is None:
                    zr = nc.alloc_register(tensor.engine, "zreg")
                    b.wregs[("zero", tensor.engine)] = zr
                tensor.reg_mov(zr, 0)
                zrv = RuntimeValue(val=RegisterHandles(zr), min_val=0, max_val=0)
                sw = tensor.Switch(zrv, 1)
                for _case in sw:
                    _emit_scan_tensor_body(b, tensor, _c, CH, WSC, hbf)
                return

            def _emit_scan_tensor_body(b, tensor, _c, CH, WSC, hbf):
                nc = b.nc
                with fori(nc, tensor, 0, CH) as i:
                    wait_reg(b, tensor, b.s_sh, i, base_h + 1 + _c * CH,
                             base_h + 1 + _c * CH, base_h + _c * CH + CH)
                    for dc in range(NC8):
                        for kc in range(NC8):
                            col = (kc * NC8 + dc) * 128
                            ins = tensor.matmul(b.PSS[:, dc:dc + 1],
                                                WSC[:, col:col + 128],
                                                hbf[:, kc:kc + 1],
                                                start=(kc == 0),
                                                stop=(kc == NC8 - 1))
                    ins.then_inc(b.s_smm.h, 1)
            blk.tensor(t_chunk)
            b.s_smm.n += CH

            def a_chunk(scalar, _c=c):
                with fori(nc, scalar, 0, CH) as i:
                    scalar.activation(b.dum[0:1, 0:1], b.cst[0:1, 0:1],
                                      AF.Sigmoid)
                    scalar.drain()
                    wait_reg(b, scalar, b.s_su, i, base_u + _c * CH + 1,
                             base_u + _c * CH + 1, base_u + _c * CH + CH)
                    ins = scalar.activation(gt[:, :], ut[:, :], AF.Sigmoid)
                    ins.then_inc(b.s_sg.h, 1)
            blk.scalar(a_chunk)
            b.s_sg.n += CH

            def v_chunk(vector, _c=c):
                vector.wait_ge(b.s_ac2[_c % 2].h,
                               base_ac2[_c % 2] + 16 * 2 * NC8 * (_c // 2 + 1))
                if _c >= 2 and not getattr(b, 'scan_no_hist', False):
                    vector.wait_ge(b.s_hs2[_c % 2].h,
                                   base_hs2[_c % 2] + 16 * NC8 * ((_c - 2) // 2 + 1))
                with fori(nc, vector, 0, CH) as i:
                    et2 = vector.engine
                    hreg = b.wregs.get(("half", et2))
                    if hreg is None:
                        hreg = nc.alloc_register(et2, "halfreg")
                        b.wregs[("half", et2)] = hreg
                    vector.reg_alu(hreg, i, CH // 2, op=ALU.is_ge)
                    vector.reg_mul(hreg, hreg, 16 * 2 * NC8)
                    vector.reg_add(hreg, hreg,
                                   base_acb[_c % 2] + 16 * 2 * NC8 * (_c // 2))
                    vector.wait_ge(b.s_acb[_c % 2].h, hreg)
                    if not getattr(b, 'scan_no_mm', False):
                        wait_reg(b, vector, b.s_smm, i, base_mm + _c * CH + 1,
                                 base_mm + _c * CH + 1, base_mm + _c * CH + CH)
                    sl = DynSlice(i, 8, CH)
                    ins = vector.tensor_tensor(ut[:, :], b.PSS[:, 0:8],
                                               C_sb[_c % 2][:, sl], ALU.add)
                    ins.then_inc(b.s_su.h, 1)
                    vector.tensor_tensor(et[:, :], A_sb[_c % 2][:, sl],
                                         hcur[:, :], ALU.subtract)
                    wait_reg(b, vector, b.s_sg, i, base_g + _c * CH + 1,
                             base_g + _c * CH + 1, base_g + _c * CH + CH)
                    vector.drain()
                    vector.tensor_mul(et[:, :], gt[:, :], et[:, :])
                    vector.drain()
                    vector.tensor_add(hbf[:, :], hcur[:, :], et[:, :])
                    ins = vector.tensor_add(H_sb[_c % 2][:, sl], hcur[:, :],
                                            et[:, :])
                    ins.then_inc(b.s_sh.h, 1)
                    vector.drain()
                    vector.tensor_add(hcur[:, :], hcur[:, :], et[:, :])
                    vector.drain()
            blk.vector(v_chunk)
            b.s_su.n += CH
            b.s_sh.n += CH



def p5_tile(b, hs1_h, xn_h, xln_h, x_in, xres, XC, XLN, XN, XNB, H1B,
            ST1, ST2, GS, WT1, WT2, WT3, t0, TT):
    blk = b.blk

    def ld(sync, _vn=b.s_v.n, _an=b.s_a.n):
        sync.wait_ge(b.s_v.h, _vn)
        sync.wait_ge(b.s_a.h, _an)
        for kc in range(NC8):
            for dst, src in ((XC, x_in), (XLN, xln_h), (XN, xn_h)):
                ins = sync.dma_start(out=dst[:, kc * TT:(kc + 1) * TT],
                                     in_=src[kc * 128:(kc + 1) * 128, t0:t0 + TT])
                b.s_ld.inc(ins, 16)
    blk.sync(ld)

    def ldg(gpsimd, _vn=b.s_v.n, _an=b.s_a.n):
        gpsimd.wait_ge(b.s_v.h, _vn)
        gpsimd.wait_ge(b.s_a.h, _an)
        for kc in range(NC8):
            i1 = gpsimd.dma_start(out=H1B[:, kc * TT:(kc + 1) * TT],
                                  in_=hs1_h[kc * 128:(kc + 1) * 128, t0:t0 + TT])
            b.s_ldg.inc(i1, 16)
            i2 = gpsimd.dma_start(out=XNB[:, kc * TT:(kc + 1) * TT],
                                  in_=xn_h[kc * 128:(kc + 1) * 128, t0:t0 + TT])
            b.s_ldg.inc(i2, 16)
    blk.gpsimd(ldg)

    rh = lambda kc: H1B[:, kc * TT:(kc + 1) * TT]
    rx = lambda kc: XNB[:, kc * TT:(kc + 1) * TT]

    plan1 = mm_set(b, WT1, rh, TT, NC8, NC8, NC8 * 128, (b.s_ldg, b.s_w),
                   b.s_a, incs_per_oc=2)

    def a_t(scalar):
        for oc, ps, n in plan1:
            scalar.wait_ge(b.s_mm.h, n)
            i1 = scalar.activation(ST1[:, oc * TT:(oc + 1) * TT],
                                   ps[:, 0:TT], AF.Tanh)
            b.s_a.inc(i1)
            scalar.drain()
            i2 = scalar.activation(ST1[:, oc * TT:(oc + 1) * TT],
                                   ST1[:, oc * TT:(oc + 1) * TT], AF.Tanh)
            b.s_a.inc(i2)
    blk.scalar(a_t)

    plan2 = mm_set(b, WT2, rh, TT, NC8, NC8, NC8 * 128, (), b.s_v)

    def v_c(vector):
        for oc, ps, n in plan2:
            vector.wait_ge(b.s_mm.h, n)
            ins = vector.tensor_copy(ST2[:, oc * TT:(oc + 1) * TT],
                                     ps[:, 0:TT])
            b.s_v.inc(ins)
    blk.vector(v_c)

    plan3 = mm_set(b, WT3, rx, TT, NC8, NC8, NC8 * 128, (), b.s_v)

    def v_u(vector):
        for oc, ps, n in plan3:
            vector.wait_ge(b.s_mm.h, n)
            ins = vector.scalar_tensor_tensor(
                GS[:, oc * TT:(oc + 1) * TT], ps[:, 0:TT],
                b.v_gbhf[:, oc:oc + 1], ST2[:, oc * TT:(oc + 1) * TT],
                ALU.add, ALU.add)
            b.s_v.inc(ins)
    blk.vector(v_u)

    def a_sig(scalar, _v=b.s_v.n):
        scalar.wait_ge(b.s_v.h, _v)
        for oc in range(NC8):
            ins = scalar.activation(GS[:, oc * TT:(oc + 1) * TT],
                                    GS[:, oc * TT:(oc + 1) * TT], AF.Sigmoid)
        b.s_a.inc(ins)
    blk.scalar(a_sig)

    def v_cell(vector, _a=b.s_a.n, _ld=b.s_ld.n):
        vector.wait_ge(b.s_a.h, _a)
        vector.wait_ge(b.s_ld.h, _ld)
        vector.drain()
        for oc in range(NC8):
            sl = slice(oc * TT, (oc + 1) * TT)
            vector.tensor_sub(ST1[:, sl], ST1[:, sl], XN[:, sl])
        vector.drain()
        for oc in range(NC8):
            sl = slice(oc * TT, (oc + 1) * TT)
            vector.tensor_mul(ST1[:, sl], GS[:, sl], ST1[:, sl])
        vector.drain()
        for oc in range(NC8):
            sl = slice(oc * TT, (oc + 1) * TT)
            vector.tensor_add(ST1[:, sl], ST1[:, sl], XN[:, sl])
        vector.drain()
        for oc in range(NC8):
            sl = slice(oc * TT, (oc + 1) * TT)
            ins = vector.tensor_add(ST1[:, sl], ST1[:, sl], XLN[:, sl])
        b.s_v.inc(ins)
    blk.vector(v_cell)

    emit_ln(b, lambda kc: ST1[:, kc * TT:(kc + 1) * TT],
            lambda kc: ST2[:, kc * TT:(kc + 1) * TT], TT,
            add_src=lambda kc: XC[:, kc * TT:(kc + 1) * TT])

    def st(sync, _v=b.s_v.n):
        sync.wait_ge(b.s_v.h, _v)
        for kc in range(NC8):
            ins = sync.dma_start(out=xres[kc * 128:(kc + 1) * 128, t0:t0 + TT],
                                 in_=ST2[:, kc * TT:(kc + 1) * TT])
            b.s_st.inc(ins, 16)
    blk.sync(st)



def emit_p6(b, xres, w1T, w2T, h1T, h2T, w3T, S, TTF):
    blk, nc = b.blk, b.nc
    NB = S // TTF
    NSUB = TTF // 512

    with contextlib.ExitStack() as ph:
        XC = ph.enter_context(nc.sbuf_tensor("XC6", [128, NC8 * 512], F32))
        XLN = ph.enter_context(nc.sbuf_tensor("XLN6", [128, NC8 * 512], F32))
        XNB = ph.enter_context(nc.sbuf_tensor("XNB6", [128, NC8 * TTF], BF16))
        U = ph.enter_context(nc.sbuf_tensor("U6", [128, 32 * TTF], BF16))
        HM = ph.enter_context(nc.sbuf_tensor("HM6", [128, 32 * TTF], BF16))
        ZS = ph.enter_context(nc.sbuf_tensor("ZS6", [128, 512], F32))
        OUT4 = XLN
        OLN = XLN
        WB = [ph.enter_context(nc.sbuf_tensor(f"WB6{i}", [128, 32 * 128], BF16))
              for i in range(2)]
        WB2 = [ph.enter_context(nc.sbuf_tensor(f"WBb6{i}", [128, NC8 * 128], BF16))
               for i in range(2)]

        for ib in range(NB):
            t0 = ib * TTF
            for sub in range(NSUB):
                ts = t0 + sub * 512

                def ld(sync, _ts=ts, _vn=b.s_v.n):
                    sync.wait_ge(b.s_v.h, _vn)
                    for kc in range(NC8):
                        ins = sync.dma_start(
                            out=XC[:, kc * 512:(kc + 1) * 512],
                            in_=xres[kc * 128:(kc + 1) * 128, _ts:_ts + 512])
                        b.s_ld.inc(ins, 16)
                blk.sync(ld)
                emit_ln(b, lambda kc: XC[:, kc * 512:(kc + 1) * 512],
                        lambda kc: XLN[:, kc * 512:(kc + 1) * 512], 512,
                        wait_sems=(b.s_ld,),
                        out_bf=lambda kc, _s=sub: XNB[:, kc * TTF + _s * 512:
                                                      kc * TTF + (_s + 1) * 512])

            guards = {}
            base_wb = [b.s_wb[i].n for i in range(2)]
            for oc in range(32):
                pw = oc % 2

                def wld(gpsimd, _oc=oc, _pw=pw, _mmn=b.s_mm.n):
                    if _oc >= 2:
                        gv, ga = guards[_oc - 2]
                        gpsimd.wait_ge(b.s_v.h, gv)
                        gpsimd.wait_ge(b.s_a.h, ga)
                    else:
                        gpsimd.wait_ge(b.s_mm.h, _mmn)
                    i1 = gpsimd.dma_start(
                        out=WB[_pw][:, 0:NC8 * 128].rearrange(
                            "p (kc o) -> p kc o", o=128),
                        in_=w1T[:, _oc * 128:(_oc + 1) * 128].rearrange(
                            "(kc p) o -> p kc o", p=128))
                    b.s_wb[_pw].inc(i1, 16)
                    i2 = gpsimd.dma_start(
                        out=WB2[_pw][:, 0:NC8 * 128].rearrange(
                            "p (kc o) -> p kc o", o=128),
                        in_=w2T[:, _oc * 128:(_oc + 1) * 128].rearrange(
                            "(kc p) o -> p kc o", p=128))
                    b.s_wb[_pw].inc(i2, 16)
                blk.gpsimd(wld)

                for sub in range(NSUB):
                    rhs = lambda kc, _s=sub: XNB[:, kc * TTF + _s * 512:
                                                 kc * TTF + (_s + 1) * 512]
                    plan = []

                    def t_z(tensor, _oc=oc, _pw=pw, _s=sub, _rhs=rhs,
                            _wn=base_wb[pw] + 32 * (oc // 2 + 1),
                            _vn=b.s_v.n, _plan=plan,
                            _a0=b.s_a.n, _v0=b.s_v.n):
                        tensor.wait_ge(b.s_wb[_pw].h, _wn)
                        if _oc == 0 and _s == 0:
                            tensor.wait_ge(b.s_v.h, _vn)
                        ps1, sl1 = b.pool.acquire(tensor)
                        b.pool.expect(sl1, b.s_a, _a0 + 1)
                        ins = mm_acc(b, tensor, WB[_pw], _rhs, 512, 0, NC8,
                                     ps1, 128)
                        b.s_mm.inc(ins)
                        _plan.append((ps1, b.s_mm.n))
                        ps2, sl2 = b.pool.acquire(tensor)
                        b.pool.expect(sl2, b.s_v, _v0 + 1)
                        ins = mm_acc(b, tensor, WB2[_pw], _rhs, 512, 0, NC8,
                                     ps2, 128)
                        b.s_mm.inc(ins)
                        _plan.append((ps2, b.s_mm.n))
                    blk.tensor(t_z)

                    def a_sil(scalar, _oc=oc, _s=sub, _plan=plan,
                              _v0=b.s_v.n):
                        scalar.wait_ge(b.s_v.h, _v0)
                        ps1, n1 = _plan[0]
                        scalar.wait_ge(b.s_mm.h, n1)
                        ins = scalar.activation(
                            ZS[:, 0:512], ps1[:, 0:512],
                            AF.Silu, scale=b.v_w1s[:, _oc:_oc + 1])
                        b.s_a.inc(ins)
                    blk.scalar(a_sil)

                    def v_u(vector, _oc=oc, _s=sub, _plan=plan, _an=b.s_a.n):
                        ps2, n2 = _plan[1]
                        vector.wait_ge(b.s_mm.h, n2)
                        vector.wait_ge(b.s_a.h, _an)
                        ins = vector.scalar_tensor_tensor(
                            U[:, _oc * TTF + _s * 512:_oc * TTF + (_s + 1) * 512],
                            ps2[:, 0:512], b.v_w2s[:, _oc:_oc + 1],
                            ZS[:, 0:512],
                            ALU.mult, ALU.mult)
                        b.s_v.inc(ins)
                    blk.vector(v_u)
                guards[oc] = (b.s_v.n, b.s_a.n)

            _p6_mid(b, U, HM, h1T, b.v_h1s, WB, TTF, NSUB)
            _p6_mid(b, HM, U, h2T, b.v_h2s, WB, TTF, NSUB)

            for sub in range(NSUB):
                ts = t0 + sub * 512
                for grp in range(2):
                    g_hist = {}
                    base_wb = [b.s_wb[i].n for i in range(2)]
                    for oc3 in range(32):
                        pw = oc3 % 2

                        def w3ld(gpsimd, _oc3=oc3, _grp=grp, _pw=pw,
                                 _mmn=b.s_mm.n):
                            if _oc3 >= 2:
                                gpsimd.wait_ge(b.s_g.h, g_hist[_oc3 - 2])
                            else:
                                gpsimd.wait_ge(b.s_mm.h, _mmn)
                            ins = gpsimd.dma_start(
                                out=WB[_pw][:, 0:512],
                                in_=w3T[_oc3 * 128:(_oc3 + 1) * 128,
                                        _grp * 512:(_grp + 1) * 512])
                            b.s_wb[_pw].inc(ins, 16)
                        blk.gpsimd(w3ld)

                        def t_w3(tensor, _oc3=oc3, _pw=pw, _s=sub,
                                 _wn=base_wb[oc3 % 2] + 16 * (oc3 // 2 + 1),
                                 _vn=b.s_v.n, _an=b.s_a.n):
                            tensor.wait_ge(b.s_wb[_pw].h, _wn)
                            if _oc3 == 0:
                                tensor.wait_ge(b.s_v.h, _vn)
                                tensor.wait_ge(b.s_a.h, _an)
                            for oc4 in range(4):
                                ins = tensor.matmul(
                                    b.PS[oc4][:, 0:512],
                                    WB[_pw][:, oc4 * 128:(oc4 + 1) * 128],
                                    U[:, _oc3 * TTF + _s * 512:
                                      _oc3 * TTF + (_s + 1) * 512],
                                    start=(_oc3 == 0), stop=(_oc3 == 31))
                            if _oc3 == 31:
                                b.s_mm.inc(ins)
                            else:
                                b.s_g.inc(ins)
                        blk.tensor(t_w3)
                        g_hist[oc3] = b.s_g.n

                    def v_ev(vector, _grp=grp, _n=b.s_mm.n):
                        vector.wait_ge(b.s_mm.h, _n)
                        for oc4 in range(4):
                            oc = _grp * 4 + oc4
                            ins = vector.tensor_scalar_mul(
                                OUT4[:, oc * 512:(oc + 1) * 512],
                                b.PS[oc4][:, 0:512], b.v_w3s[:, oc:oc + 1])
                        b.s_v.inc(ins)
                    blk.vector(v_ev)

                def ld2(sync, _ts=ts, _vn=b.s_v.n):
                    sync.wait_ge(b.s_v.h, _vn)
                    for kc in range(NC8):
                        ins = sync.dma_start(
                            out=XC[:, kc * 512:(kc + 1) * 512],
                            in_=xres[kc * 128:(kc + 1) * 128, _ts:_ts + 512])
                        b.s_ld.inc(ins, 16)
                blk.sync(ld2)
                emit_ln(b, lambda kc: OUT4[:, kc * 512:(kc + 1) * 512],
                        lambda kc: OLN[:, kc * 512:(kc + 1) * 512], 512,
                        wait_sems=(b.s_ld,),
                        add_src=lambda kc: XC[:, kc * 512:(kc + 1) * 512])

                def st2(sync, _ts=ts, _v=b.s_v.n):
                    sync.wait_ge(b.s_v.h, _v)
                    for kc in range(NC8):
                        ins = sync.dma_start(
                            out=xres[kc * 128:(kc + 1) * 128, _ts:_ts + 512],
                            in_=OLN[:, kc * 512:(kc + 1) * 512])
                        b.s_st.inc(ins, 16)
                blk.sync(st2)


def _p6_mid(b, SRC, DST, WT_h, scale_vec, WB, TTF, NSUB):
    blk = b.blk
    guards = {}
    base_wb = [b.s_wb[i].n for i in range(2)]
    for oc in range(32):
        pw = oc % 2

        def wld(gpsimd, _oc=oc, _pw=pw, _mmn=b.s_mm.n):
            if _oc >= 2:
                gpsimd.wait_ge(b.s_a.h, guards[_oc - 2])
            else:
                gpsimd.wait_ge(b.s_mm.h, _mmn)
            ins = gpsimd.dma_start(
                out=WB[_pw][:, :].rearrange("p (kc o) -> p kc o", o=128),
                in_=WT_h[:, _oc * 128:(_oc + 1) * 128].rearrange(
                    "(kc p) o -> p kc o", p=128))
            b.s_wb[_pw].inc(ins, 16)
        blk.gpsimd(wld)

        for sub in range(NSUB):
            plan = []

            def t_mm(tensor, _oc=oc, _pw=pw, _s=sub,
                     _wn=base_wb[pw] + 16 * (oc // 2 + 1),
                     _vn=b.s_v.n, _an=b.s_a.n, _plan=plan):
                tensor.wait_ge(b.s_wb[_pw].h, _wn)
                if _oc == 0 and _s == 0:
                    tensor.wait_ge(b.s_v.h, _vn)
                    tensor.wait_ge(b.s_a.h, _an)
                ps, sl = b.pool.acquire(tensor)
                b.pool.expect(sl, b.s_a, _an + 1)
                for kc in range(32):
                    ins = tensor.matmul(
                        ps[:, 0:512], WB[_pw][:, kc * 128:(kc + 1) * 128],
                        SRC[:, kc * TTF + _s * 512:kc * TTF + (_s + 1) * 512],
                        start=(kc == 0), stop=(kc == 31))
                b.s_mm.inc(ins)
                _plan.append((ps, b.s_mm.n))
            blk.tensor(t_mm)

            def a_g(scalar, _oc=oc, _s=sub, _plan=plan):
                ps, n = _plan[0]
                scalar.wait_ge(b.s_mm.h, n)
                ins = scalar.activation(
                    DST[:, _oc * TTF + _s * 512:_oc * TTF + (_s + 1) * 512],
                    ps[:, 0:512], AF.Gelu, scale=scale_vec[:, _oc:_oc + 1])
                b.s_a.inc(ins)
            blk.scalar(a_g)
        guards[oc] = b.s_a.n



def emit_p7(b, xres, WqkvT, WoT, bqkv, qT_h, kT_h, v_h, zT_h, amask,
            S, TT, QTILE):
    blk, nc = b.blk, b.nc
    NT = S // TT
    NKT = S // 128
    NQT = S // QTILE
    KPQ = QTILE // 128

    with contextlib.ExitStack() as ph:
        XC = ph.enter_context(nc.sbuf_tensor("XC7", [128, NC8 * 512], F32))
        b.row_q = ph.enter_context(nc.sbuf_tensor("row_q", [1, D], F32))
        XLN = ph.enter_context(nc.sbuf_tensor("XLN7", [128, NC8 * 512], F32))
        pha = ph.enter_context(contextlib.ExitStack())
        XNB = pha.enter_context(nc.sbuf_tensor("XNB7", [128, NC8 * TT], BF16))
        STB = pha.enter_context(nc.sbuf_tensor("STB7", [128, NC8 * TT], BF16))
        VS = pha.enter_context(nc.sbuf_tensor("VS7", [128, D], F32))
        WQ = pha.enter_context(nc.sbuf_tensor("WQ7", [128, NC8 * 3 * D], BF16))
        BIASV = pha.enter_context(nc.sbuf_tensor("BIASV7", [128, D], F32))
        MASK = pha.enter_context(nc.sbuf_tensor("MASK7", [128, KPQ * QTILE], F32))

        load_weight_1024(b, WQ, WqkvT, 3 * D)

        def mload(sync):
            ins0 = sync.dma_start(out=b.row_q[0:1, :], in_=bqkv[2 * D:3 * D])
            b.s_ld.inc(ins0, 16)
            ins = sync.dma_start(
                out=MASK[:, :].rearrange("p (a q) -> p a q", q=QTILE),
                in_=amask.rearrange("(a p) q -> p a q", p=128))
            b.s_ld.inc(ins, 16)
        blk.sync(mload)

        def t_bv(tensor, _ld=b.s_ld.n):
            tensor.wait_ge(b.s_ld.h, _ld)
            bcast(b, tensor, b.row_q[0:1, 0:512], b.PSB, 512)
            ins = bcast(b, tensor, b.row_q[0:1, 512:1024], b.PSB2, 512)
            b.s_mm.inc(ins)
        blk.tensor(t_bv)

        def v_bv(vector, _mm=b.s_mm.n):
            vector.wait_ge(b.s_mm.h, _mm)
            vector.tensor_copy(BIASV[:, 0:512], b.PSB[:, 0:512])
            ins = vector.tensor_copy(BIASV[:, 512:1024], b.PSB2[:, 0:512])
            b.s_v.inc(ins)
        blk.vector(v_bv)

        for it in range(NT):
            t0 = it * TT
            for sub in range(TT // 512):
                ts = t0 + sub * 512

                def ld(sync, _ts=ts, _vn=b.s_v.n):
                    sync.wait_ge(b.s_v.h, _vn)
                    for kc in range(NC8):
                        ins = sync.dma_start(
                            out=XC[:, kc * 512:(kc + 1) * 512],
                            in_=xres[kc * 128:(kc + 1) * 128, _ts:_ts + 512])
                        b.s_ld.inc(ins, 16)
                blk.sync(ld)

                emit_ln(b, lambda kc: XC[:, kc * 512:(kc + 1) * 512],
                        lambda kc: XLN[:, kc * 512:(kc + 1) * 512], 512,
                        wait_sems=(b.s_ld,))
                emit_stats(b, lambda kc: XLN[:, kc * 512:(kc + 1) * 512],
                           NC8, 512, [(0, "sumsq")])

                def a_rs(scalar, _mm=b.s_mm.n):
                    scalar.wait_ge(b.s_mm.h, _mm)
                    ins = scalar.activation(b.row_b[0:1, 0:512],
                                            b.PSST[0:1, 0:512], AF.Sqrt,
                                            bias=RMS_EPS, scale=1.0 / D)
                    b.s_a.inc(ins)
                blk.scalar(a_rs)

                def v_rec3(vector, _a=b.s_a.n):
                    vector.wait_ge(b.s_a.h, _a)
                    vector.drain()
                    ins = vector.reciprocal(b.row_b[0:1, 0:512],
                                            b.row_b[0:1, 0:512])
                    b.s_v.inc(ins)
                blk.vector(v_rec3)

                def t_bc(tensor, _a=b.s_v.n):
                    tensor.wait_ge(b.s_v.h, _a)
                    ins = bcast(b, tensor, b.row_b[0:1, 0:512], b.PSB, 512)
                    b.s_mm.inc(ins)
                blk.tensor(t_bc)

                def v_n(vector, _s=sub, _mm=b.s_mm.n):
                    vector.wait_ge(b.s_mm.h, _mm)
                    for kc in range(NC8):
                        ins = vector.tensor_mul(
                            XNB[:, kc * TT + _s * 512:kc * TT + (_s + 1) * 512],
                            XLN[:, kc * 512:(kc + 1) * 512], b.PSB[:, 0:512])
                    b.s_v.inc(ins)
                blk.vector(v_n)

            for part, dst_h in ((0, qT_h), (1, kT_h)):
                for sub in range(TT // 512):
                    rhs = lambda kc, _s=sub: XNB[:, kc * TT + _s * 512:
                                                 kc * TT + (_s + 1) * 512]
                    plan = []

                    def t_qk(tensor, _part=part, _s=sub, _rhs=rhs,
                             _wn=b.s_w.n, _vn=b.s_v.n, _plan=plan,
                             _v0=b.s_v.n):
                        tensor.wait_ge(b.s_w.h, _wn)
                        tensor.wait_ge(b.s_v.h, _vn)
                        for oc in range(NC8):
                            ps, sl = b.pool.acquire(tensor)
                            b.pool.expect(sl, b.s_v, _v0 + oc + 1)
                            ins = mm_acc(b, tensor, WQ, _rhs, 512,
                                         _part * 8 + oc, NC8, ps, 3 * D)
                            b.s_mm.inc(ins)
                            _plan.append((oc, ps, b.s_mm.n))
                    blk.tensor(t_qk)

                    def v_qk(vector, _part=part, _s=sub, _plan=plan,
                             _stn=b.s_st.n):
                        vector.wait_ge(b.s_st.h, _stn)
                        for oc, ps, n in _plan:
                            vector.wait_ge(b.s_mm.h, n)
                            ins = vector.tensor_scalar_add(
                                STB[:, oc * TT + _s * 512:oc * TT + (_s + 1) * 512],
                                ps[:, 0:512],
                                b.v_bqkv[:, _part * 8 + oc:_part * 8 + oc + 1])
                            b.s_v.inc(ins)
                    blk.vector(v_qk)

                def stqk(sync, _dst=dst_h, _t0=t0, _v=b.s_v.n):
                    sync.wait_ge(b.s_v.h, _v)
                    for oc in range(NC8):
                        ins = sync.dma_start(
                            out=_dst[oc * 128:(oc + 1) * 128, _t0:_t0 + TT],
                            in_=STB[:, oc * TT:(oc + 1) * TT])
                        b.s_st.inc(ins, 16)
                blk.sync(stqk)

            for tsub in range(TT // 128):
                tks = t0 + tsub * 128
                lsub = tsub * 128 // 512
                loff = (tsub * 128) % 512
                plan = []

                def t_v(tensor, _lsub=lsub, _loff=loff, _plan=plan,
                        _wn=b.s_w.n, _vn=b.s_v.n, _v0=b.s_v.n):
                    tensor.wait_ge(b.s_w.h, _wn)
                    tensor.wait_ge(b.s_v.h, _vn)
                    for half in range(2):
                        ps, sl = b.pool.acquire(tensor)
                        b.pool.expect(sl, b.s_v, _v0 + half + 1)
                        for kc in range(NC8):
                            lhs = XNB[:, kc * TT + _lsub * 512 + _loff:
                                      kc * TT + _lsub * 512 + _loff + 128]
                            rhscol = kc * 3 * D + 2 * D + half * 512
                            ins = tensor.matmul(
                                ps[:, 0:512], lhs,
                                WQ[:, rhscol:rhscol + 512],
                                start=(kc == 0), stop=(kc == NC8 - 1))
                        b.s_mm.inc(ins)
                        _plan.append((half, ps, b.s_mm.n))
                blk.tensor(t_v)

                def v_v(vector, _plan=plan, _stn=b.s_st.n):
                    vector.wait_ge(b.s_st.h, _stn)
                    for half, ps, n in _plan:
                        vector.wait_ge(b.s_mm.h, n)
                        ins = vector.tensor_add(
                            VS[:, half * 512:(half + 1) * 512], ps[:, 0:512],
                            BIASV[:, half * 512:(half + 1) * 512])
                        b.s_v.inc(ins)
                blk.vector(v_v)

                def st_v(sync, _tks=tks, _v=b.s_v.n):
                    sync.wait_ge(b.s_v.h, _v)
                    ins = sync.dma_start(
                        out=v_h[:, _tks:_tks + 128, :].rearrange("h t dh -> t h dh"),
                        in_=VS[:, :].rearrange("p (h dh) -> p h dh", dh=DHD))
                    b.s_st.inc(ins, 16)
                blk.sync(st_v)

        QT = pha.enter_context(nc.sbuf_tensor("QT7", [64, S], BF16))
        KT = pha.enter_context(nc.sbuf_tensor("KT7", [64, S], BF16))
        VT = pha.enter_context(nc.sbuf_tensor("VT7", [128, NKT * DHD], F32))
        VNB = pha.enter_context(nc.sbuf_tensor("VNB7", [128, NKT * DHD], BF16))
        MC = pha.enter_context(nc.sbuf_tensor("MC7", [128, NKT], F32))
        ES = pha.enter_context(nc.sbuf_tensor("ES7", [128, QTILE], BF16))
        SD = pha.enter_context(nc.sbuf_tensor("SD7", [128, QTILE], F32))
        CL = pha.enter_context(nc.sbuf_tensor("CL7", [128, 8], F32))
        ZB = pha.enter_context(nc.sbuf_tensor("ZB7", [64, S], BF16))

        for h in range(NH):
            def ldh(sync, _h=h, _vn=b.s_v.n, _an=b.s_a.n, _mmn=b.s_mm.n,
                    _stn=b.s_st.n):
                sync.wait_ge(b.s_v.h, _vn)
                sync.wait_ge(b.s_a.h, _an)
                sync.wait_ge(b.s_mm.h, _mmn)
                sync.wait_ge(b.s_st.h, _stn)
                i1 = sync.dma_start(out=QT[:, :],
                                    in_=qT_h[_h * 64:(_h + 1) * 64, :])
                b.s_ld.inc(i1, 16)
                i2 = sync.dma_start(out=KT[:, :],
                                    in_=kT_h[_h * 64:(_h + 1) * 64, :])
                b.s_ld.inc(i2, 16)
                i3 = sync.dma_start(
                    out=VT[:, :].rearrange("p (kt dh) -> p kt dh", dh=DHD),
                    in_=v_h[_h, :, :].rearrange("(kt p) dh -> p kt dh", p=128))
                b.s_ld.inc(i3, 16)
            blk.sync(ldh)

            for kt in range(NKT):
                qt0 = kt // KPQ
                al = kt % KPQ

                def v_init(vector, _ld=b.s_ld.n):
                    vector.wait_ge(b.s_ld.h, _ld)
                    vector.drain()
                    vector.memset(CL[:, 0:1], -1e30)
                    vector.memset(CL[:, 1:2], 0.0)
                    vector.drain()
                blk.vector(v_init)

                for qt in range(qt0, NQT):
                    diag = (qt == qt0)
                    plan = []

                    def t_s(tensor, _kt=kt, _qt=qt, _plan=plan, _diag=diag,
                            _first=(kt == 0 and qt == qt0), _ldn=b.s_ld.n,
                            _v0=b.s_v.n, _a0=b.s_a.n):
                        if _first:
                            tensor.wait_ge(b.s_ld.h, _ldn)
                        ps, sl = b.pool.acquire(tensor)
                        if _diag:
                            b.pool.expect(sl, b.s_v, _v0 + 1)
                        else:
                            b.pool.expect(sl, b.s_a, _a0 + 2)
                        ins = tensor.matmul(
                            ps[:, 0:QTILE],
                            KT[:, _kt * 128:(_kt + 1) * 128],
                            QT[:, _qt * QTILE:(_qt + 1) * QTILE],
                            start=True, stop=True)
                        b.s_mm.inc(ins)
                        _plan.append((ps, b.s_mm.n))
                    blk.tensor(t_s)

                    def v_flash(vector, _diag=diag, _al=al, _plan=plan):
                        ps, n = _plan[0]
                        vector.wait_ge(b.s_mm.h, n)
                        vector.drain()
                        if _diag:
                            ins = vector.tensor_add(
                                SD[:, :], ps[:, 0:QTILE],
                                MASK[:, _al * QTILE:(_al + 1) * QTILE])
                            b.s_v.inc(ins)
                            src = SD[:, :]
                            vector.drain()
                        else:
                            src = ps[:, 0:QTILE]
                        vector.reduce_max(CL[:, 2:3], src, axis=mybir.AxisListType.X)
                        vector.drain()
                        vector.tensor_max(CL[:, 3:4], CL[:, 0:1], CL[:, 2:3])
                        vector.drain()
                        vector.tensor_sub(CL[:, 4:5], CL[:, 0:1], CL[:, 3:4])
                        ins = vector.tensor_scalar_mul(CL[:, 7:8], CL[:, 3:4], -1.0)
                        b.s_v.inc(ins)
                    blk.vector(v_flash)

                    def a_flash(scalar, _diag=diag, _plan=plan, _v=b.s_v.n):
                        scalar.activation(b.dum[0:1, 0:1], b.cst[0:1, 0:1],
                                          AF.Exp)
                        scalar.drain()
                        scalar.wait_ge(b.s_v.h, _v)
                        ins = scalar.activation(CL[:, 5:6], CL[:, 4:5], AF.Exp)
                        b.s_a.inc(ins)
                        ps, n = _plan[0]
                        src = SD[:, :] if _diag else ps[:, 0:QTILE]
                        ins = scalar.activation(ES[:, :], src, AF.Exp,
                                                bias=CL[:, 7:8])
                        b.s_a.inc(ins)
                    blk.scalar(a_flash)

                    def v_merge(vector, _a=b.s_a.n):
                        vector.wait_ge(b.s_a.h, _a)
                        vector.drain()
                        vector.reduce_sum(CL[:, 6:7], ES[:, :], axis=mybir.AxisListType.X)
                        vector.tensor_mul(CL[:, 1:2], CL[:, 1:2], CL[:, 5:6])
                        vector.drain()
                        vector.tensor_add(CL[:, 1:2], CL[:, 1:2], CL[:, 6:7])
                        ins = vector.tensor_copy(CL[:, 0:1], CL[:, 3:4])
                        b.s_v.inc(ins)
                    blk.vector(v_merge)

                def v_fin(vector, _kt=kt, _v=b.s_v.n):
                    vector.wait_ge(b.s_v.h, _v)
                    vector.drain()
                    vector.reciprocal(CL[:, 6:7], CL[:, 1:2])
                    vector.tensor_scalar_mul(MC[:, _kt:_kt + 1], CL[:, 0:1], -1.0)
                    vector.drain()
                    ins = vector.tensor_scalar_mul(
                        VNB[:, _kt * DHD:(_kt + 1) * DHD],
                        VT[:, _kt * DHD:(_kt + 1) * DHD], CL[:, 6:7])
                    b.s_v.inc(ins)
                blk.vector(v_fin)

            for qt in range(NQT):
                nk = (qt + 1) * KPQ
                for kt in range(nk):
                    diag = (kt // KPQ == qt)
                    al = kt % KPQ
                    plan = []

                    def t_s2(tensor, _kt=kt, _qt=qt, _plan=plan, _diag=diag,
                             _v0=b.s_v.n, _a0=b.s_a.n):
                        ps, sl = b.pool.acquire(tensor)
                        if _diag:
                            b.pool.expect(sl, b.s_v, _v0 + 1)
                        else:
                            b.pool.expect(sl, b.s_a, _a0 + 1)
                        ins = tensor.matmul(
                            ps[:, 0:QTILE],
                            KT[:, _kt * 128:(_kt + 1) * 128],
                            QT[:, _qt * QTILE:(_qt + 1) * QTILE],
                            start=True, stop=True)
                        b.s_mm.inc(ins)
                        _plan.append((ps, b.s_mm.n))
                    blk.tensor(t_s2)

                    if diag:
                        def v_m(vector, _al=al, _plan=plan):
                            ps, n = _plan[0]
                            vector.wait_ge(b.s_mm.h, n)
                            ins = vector.tensor_add(
                                SD[:, :], ps[:, 0:QTILE],
                                MASK[:, _al * QTILE:(_al + 1) * QTILE])
                            b.s_v.inc(ins)
                        blk.vector(v_m)

                    def a_e2(scalar, _kt=kt, _diag=diag, _plan=plan,
                             _v=b.s_v.n, _g=b.s_g.n):
                        scalar.activation(b.dum[0:1, 0:1], b.cst[0:1, 0:1],
                                          AF.Exp)
                        scalar.drain()
                        scalar.wait_ge(b.s_g.h, _g)
                        ps, n = _plan[0]
                        if _diag:
                            scalar.wait_ge(b.s_v.h, _v)
                            src = SD[:, :]
                        else:
                            scalar.wait_ge(b.s_mm.h, n)
                            src = ps[:, 0:QTILE]
                        ins = scalar.activation(ES[:, :], src, AF.Exp,
                                                bias=MC[:, _kt:_kt + 1])
                        b.s_a.inc(ins)
                    blk.scalar(a_e2)

                    def t_av(tensor, _kt=kt, _first=(kt == 0),
                             _last=(kt == nk - 1), _a=b.s_a.n, _vn=b.s_v.n):
                        tensor.wait_ge(b.s_a.h, _a)
                        if _first:
                            tensor.wait_ge(b.s_v.h, _vn)
                        ins = tensor.matmul(
                            b.PSS[0:64, 0:QTILE],
                            VNB[:, _kt * DHD:(_kt + 1) * DHD],
                            ES[:, :], start=_first, stop=_last)
                        b.s_g.inc(ins)
                    blk.tensor(t_av)

                def v_z(vector, _qt=qt, _mm=b.s_g.n, _stn=b.s_st.n):
                    vector.wait_ge(b.s_g.h, _mm)
                    vector.wait_ge(b.s_st.h, _stn)
                    ins = vector.tensor_copy(
                        ZB[:, _qt * QTILE:(_qt + 1) * QTILE],
                        b.PSS[0:64, 0:QTILE])
                    b.s_v.inc(ins)
                blk.vector(v_z)

            def st_z(sync, _h=h, _v=b.s_v.n):
                sync.wait_ge(b.s_v.h, _v)
                ins = sync.dma_start(out=zT_h[_h * 64:(_h + 1) * 64, :],
                                     in_=ZB[:, :])
                b.s_st.inc(ins, 16)
            blk.sync(st_z)

        pha.close()

        def gate_wob(gpsimd, _vn=b.s_v.n, _an=b.s_a.n, _gn=b.s_g.n):
            gpsimd.wait_ge(b.s_v.h, _vn)
            gpsimd.wait_ge(b.s_a.h, _an)
            gpsimd.wait_ge(b.s_g.h, _gn)
        blk.gpsimd(gate_wob)
        WOB = ph.enter_context(nc.sbuf_tensor("WOB7", [128, NC8 * D], BF16))
        ZNB = ph.enter_context(nc.sbuf_tensor("ZNB7", [128, NC8 * 512], BF16))
        OP = ph.enter_context(nc.sbuf_tensor("OP7", [128, NC8 * 512], F32))
        OL = ph.enter_context(nc.sbuf_tensor("OL7", [128, NC8 * 512], F32))
        load_weight_1024(b, WOB, WoT, D, sem=b.s_w2)

        for it in range(S // 512):
            t0 = it * 512

            def ldz(gpsimd, _t0=t0, _vn=b.s_v.n, _stn=b.s_st.n):
                gpsimd.wait_ge(b.s_v.h, _vn)
                gpsimd.wait_ge(b.s_st.h, _stn)
                for kc in range(NC8):
                    ins = gpsimd.dma_start(
                        out=ZNB[:, kc * 512:(kc + 1) * 512],
                        in_=zT_h[kc * 128:(kc + 1) * 128, _t0:_t0 + 512])
                    b.s_ldg.inc(ins, 16)
            blk.gpsimd(ldz)

            def ldx(sync, _t0=t0, _vn=b.s_v.n):
                sync.wait_ge(b.s_v.h, _vn)
                for kc in range(NC8):
                    ins = sync.dma_start(
                        out=XC[:, kc * 512:(kc + 1) * 512],
                        in_=xres[kc * 128:(kc + 1) * 128, _t0:_t0 + 512])
                    b.s_ld.inc(ins, 16)
            blk.sync(ldx)

            rhs = lambda kc: ZNB[:, kc * 512:(kc + 1) * 512]
            plan = mm_set(b, WOB, rhs, 512, NC8, NC8, NC8 * 128,
                          (b.s_ldg, b.s_w2), b.s_v)

            def v_o(vector, _plan=plan):
                for oc, ps, n in _plan:
                    vector.wait_ge(b.s_mm.h, n)
                    ins = vector.tensor_scalar_add(
                        OP[:, oc * 512:(oc + 1) * 512], ps[:, 0:512],
                        b.v_bo[:, oc:oc + 1])
                    b.s_v.inc(ins)
            blk.vector(v_o)

            emit_ln(b, lambda kc: OP[:, kc * 512:(kc + 1) * 512],
                    lambda kc: OL[:, kc * 512:(kc + 1) * 512], 512,
                    wait_sems=(b.s_ld,),
                    add_src=lambda kc: XC[:, kc * 512:(kc + 1) * 512])

            def sto(sync, _t0=t0, _v=b.s_v.n):
                sync.wait_ge(b.s_v.h, _v)
                for kc in range(NC8):
                    ins = sync.dma_start(
                        out=xres[kc * 128:(kc + 1) * 128, _t0:_t0 + 512],
                        in_=OL[:, kc * 512:(kc + 1) * 512])
                    b.s_st.inc(ins, 16)
            blk.sync(sto)



def p8_tile(b, xres, y_out, XC, XLN, XNB, STB, ST1, ST2, WT1, WT2, WT3,
            t0, TT):
    blk = b.blk

    def ld(sync, _vn=b.s_v.n, _an=b.s_a.n):
        sync.wait_ge(b.s_v.h, _vn)
        sync.wait_ge(b.s_a.h, _an)
        for kc in range(NC8):
            ins = sync.dma_start(out=XC[:, kc * TT:(kc + 1) * TT],
                                 in_=xres[kc * 128:(kc + 1) * 128, t0:t0 + TT])
            b.s_ld.inc(ins, 16)
    blk.sync(ld)

    emit_ln(b, lambda kc: XC[:, kc * TT:(kc + 1) * TT],
            lambda kc: XLN[:, kc * TT:(kc + 1) * TT], TT,
            wait_sems=(b.s_ld,),
            out_bf=lambda kc: XNB[:, kc * TT:(kc + 1) * TT])

    for wi, (WT, srcbuf, dstbuf) in enumerate(((WT1, XNB, STB),
                                               (WT2, STB, XNB),
                                               (WT3, XNB, STB))):
        rhs = lambda kc, _s=srcbuf: _s[:, kc * TT:(kc + 1) * TT]
        last = (wi == 2)
        dsem = b.s_a if last else b.s_v
        plan = mm_set(b, WT, rhs, TT, NC8, NC8, NC8 * 128, (b.s_v, b.s_w),
                      dsem)
        if not last:
            def v_c(vector, _dst=dstbuf, _plan=plan):
                for oc, ps, n in _plan:
                    vector.wait_ge(b.s_mm.h, n)
                    ins = vector.tensor_copy(
                        _dst[:, oc * TT:(oc + 1) * TT], ps[:, 0:TT])
                    b.s_v.inc(ins)
            blk.vector(v_c)
        else:
            def a_c(scalar, _plan=plan):
                for oc, ps, n in _plan:
                    scalar.wait_ge(b.s_mm.h, n)
                    ins = scalar.activation(
                        ST1[:, oc * TT:(oc + 1) * TT], ps[:, 0:TT],
                        AF.Tanh, bias=b.v_l1b[:, oc:oc + 1])
                    b.s_a.inc(ins)
            blk.scalar(a_c)

    def w_t(vector, _a=b.s_a.n):
        vector.wait_ge(b.s_a.h, _a)
        ins = vector.engine_nop()
        b.s_v.inc(ins)
    blk.vector(w_t)

    emit_ln(b, lambda kc: ST1[:, kc * TT:(kc + 1) * TT],
            lambda kc: ST2[:, kc * TT:(kc + 1) * TT], TT,
            add_src=lambda kc: XC[:, kc * TT:(kc + 1) * TT])

    def st(sync, _v=b.s_v.n):
        sync.wait_ge(b.s_v.h, _v)
        for kc in range(NC8):
            ins = sync.dma_start(out=y_out[kc * 128:(kc + 1) * 128, t0:t0 + TT],
                                 in_=ST2[:, kc * TT:(kc + 1) * TT])
            b.s_st.inc(ins, 16)
    blk.sync(st)



def make_masks(QTILE):
    NAL = QTILE // 128
    m = np.full((NAL * 128, QTILE), -1e30, np.float32)
    for al in range(NAL):
        for p in range(128):
            k = al * 128 + p
            m[al * 128 + p, k:] = 0.0
    return m


def make_in_map(inputs, bidx, S=4096, QTILE=512):
    T = np.ascontiguousarray
    x = np.asarray(inputs["L0_attn_x"])[bidx]
    Wq = np.asarray(inputs["L1_attn_W_qkv"])
    Wq3 = Wq.reshape(NH, DHD, 3, D)
    Wq_de = np.concatenate([Wq3[:, :, c, :].reshape(NH * DHD, D)
                            for c in range(3)], axis=0)
    bq = np.asarray(inputs["L1_attn_b_qkv"]).reshape(NH, DHD, 3)
    bq_de = np.concatenate([bq[:, :, c].reshape(-1) for c in range(3)])

    m = {
        "x": T(x.T),
        "gamma": np.asarray(inputs["L0_attn_gamma"]),
        "gWiT": T(np.asarray(inputs["L0_attn_gWi"]).T),
        "gWifT": T(np.asarray(inputs["L0_attn_gWif"]).T),
        "gWhfT": T(np.asarray(inputs["L0_attn_gWhf"]).T),
        "gbhf": np.asarray(inputs["L0_attn_gbhf"]),
        "w1T": T(np.asarray(inputs["L0_ffn_w1_w"]).T),
        "w2T": T(np.asarray(inputs["L0_ffn_w2_w"]).T),
        "h1T": T(np.asarray(inputs["L0_ffn_h1_w"]).T),
        "h2T": T(np.asarray(inputs["L0_ffn_h2_w"]).T),
        "w3T": T(np.asarray(inputs["L0_ffn_w3_w"]).T),
        "w1s": np.asarray(inputs["L0_ffn_w1_s"]),
        "w2s": np.asarray(inputs["L0_ffn_w2_s"]),
        "h1s": np.asarray(inputs["L0_ffn_h1_s"]),
        "h2s": np.asarray(inputs["L0_ffn_h2_s"]),
        "w3s": np.asarray(inputs["L0_ffn_w3_s"]),
        "WqkvT": T(Wq_de.T),
        "bqkv": T(bq_de),
        "WoT": T(np.asarray(inputs["L1_attn_W_o"]).T),
        "bo": np.asarray(inputs["L1_attn_b_o"]),
        "l1w": np.asarray(inputs["L1_ffn_weight"]),
        "l1a1": np.asarray(inputs["L1_ffn_aux_weight1"]),
        "l1a2": np.asarray(inputs["L1_ffn_aux_weight2"]),
        "l1bias": np.asarray(inputs["L1_ffn_bias"]),
        "amask": make_masks(QTILE),
    }
    for i in range(2):
        m[f"lWiT{i}"] = T(np.asarray(inputs["L0_attn_lWi"])[i].T)
        m[f"lWifT{i}"] = T(np.asarray(inputs["L0_attn_lWif"])[i].T)
        m[f"lWhfT{i}"] = T(np.asarray(inputs["L0_attn_lWhf"])[i].T)
        m[f"lbhf{i}"] = np.asarray(inputs["L0_attn_lbhf"])[i]
        m[f"linit{i}"] = np.asarray(inputs["L0_attn_l_init_hidden"])[i]
    return {k: np.ascontiguousarray(np.asarray(v, dtype=np.float32))
            for k, v in m.items()}


_CACHED = {}


def kernel(**inputs):
    S = 4096
    if "nc" not in _CACHED:
        _CACHED["nc"] = build_program(S=S)
    nc = _CACHED["nc"]
    in_maps = [make_in_map(inputs, c % 4, S=S) for c in range(8)]
    res = run_bass_kernel_spmd(nc, in_maps, core_ids=list(range(8)))
    out = np.stack([res.results[bidx]["y"].T for bidx in range(4)])
    return out.astype(np.float32)
